# version 01
# baseline (speedup 1.0000x reference)
import math
import contextlib
import numpy as np
import concourse.bass as bass
import concourse.mybir as mybir
from concourse.bass_utils import run_bass_kernel_spmd

F32 = mybir.dt.float32
F32R = mybir.dt.float32r
BF16 = mybir.dt.bfloat16
I32 = mybir.dt.int32
OP = mybir.AluOpType
AF = mybir.ActivationFunctionType
AX = mybir.AxisListType.X

T = 2048
D = 1024
HID = 4096
NHEAD = 16
DH = 64
NCH = D // 128
LN_EPS = 1e-5
ALL_PHASES = ("l0a", "l0f", "l1a", "l1f")
DEBUG_O = False


class Sem:
    __slots__ = ("name", "h", "v")

    def __init__(self, name):
        self.name = name
        self.h = None
        self.v = 0


def w(sem):
    return (sem, sem.v)


class Prog:
    ENGINES = ("pe", "act", "dve", "gps", "syn")

    def __init__(self, nc):
        self.nc = nc
        self.items = {e: [] for e in self.ENGINES}
        self.sems = []
        self.sbuf_bytes = 0
        self.hz = {e: {"cycle": 0, "writes": []} for e in self.ENGINES}
        self.default_stack = None
        self.sb_names = set()

    def sem(self, name):
        s = Sem(name)
        self.sems.append(s)
        return s

    def sb(self, name, shape, dt):
        if name in self.sb_names:
            i = 2
            while f"{name}_{i}" in self.sb_names:
                i += 1
            name = f"{name}_{i}"
        self.sb_names.add(name)
        self.sbuf_bytes += shape[-1] * mybir.dt.size(dt)
        side = "right" if dt == F32R else "left"
        if self.default_stack is not None:
            return self.default_stack.enter_context(
                self.nc.sbuf_tensor(name, shape, dt, side=side))
        return self.nc.alloc_sbuf_tensor(name, shape, dt, side=side)

    HZ_SLACK = {"dve": 96, "act": 640, "gps": 256}
    HZ_DEPTH = {"dve": 24, "act": 420, "gps": 64}

    @staticmethod
    def _ap_info(x):
        try:
            name = x.tensor.name
            span = 1
            for step, count in list(x.ap)[1:]:
                span += abs(int(step)) * (int(count) - 1)
            off = int(x.offset)
            return (name, off, off + span, int(x.free_size()))
        except Exception:
            return None

    def _hazard(self, eng, meth, a, kw):
        if eng not in self.HZ_SLACK:
            return False
        st = self.hz[eng]
        if meth == "dma_start":
            return False
        aps = [x for x in a if self._ap_info(x) is not None]
        for k in ("bias", "scale", "accum_out"):
            v = kw.get(k)
            if v is not None and self._ap_info(v) is not None:
                aps.append(v)
        infos = [self._ap_info(x) for x in aps]
        outs, ins = infos[:1], infos[1:]
        if kw.get("accum_out") is not None and infos:
            outs = [infos[0], infos[-1]]
            ins = infos[1:-1]
        slack = self.HZ_SLACK[eng]
        depth = self.HZ_DEPTH[eng]
        now = st["cycle"]
        st["writes"] = [wr for wr in st["writes"] if now - wr[3] < slack]
        need = False
        for (n, lo, hi, sz) in ins:
            for (wn, wlo, whi, wend, wsz) in st["writes"]:
                if wn != n or lo >= whi or wlo >= hi:
                    continue
                if now - wend >= depth:
                    continue
                if wsz >= depth and sz >= wsz and lo == wlo and hi == whi:
                    continue
                need = True
                break
            if need:
                break
        if need:
            st["writes"] = []
        dur = max((i[3] for i in infos), default=1)
        st["cycle"] = now + dur
        for o in outs:
            if o is not None:
                st["writes"].append((o[0], o[1], o[2], st["cycle"], o[3]))
        return need

    def op(self, eng, meth, *a, waits=(), inc=None, inc16=False, **kw):
        waits = tuple((s, v) for (s, v) in waits if v > 0)
        inc_n = 16 if inc16 else 1
        need_drain = self._hazard(eng, meth, a, kw)

        def f(e, _m=meth, _a=a, _kw=kw, _w=waits, _i=inc, _n=inc_n,
              _d=need_drain):
            if _d:
                e.drain()
            for (s, v) in _w:
                e.wait_ge(s.h, v)
            ins = getattr(e, _m)(*_a, **_kw)
            if _i is not None:
                ins.then_inc(_i.h, _n)

        self.items[eng].append(f)
        if inc is not None:
            inc.v += inc_n

    def wait(self, eng, *waits):
        waits = tuple((s, v) for (s, v) in waits if v > 0)

        def f(e, _w=waits):
            for (s, v) in _w:
                e.wait_ge(s.h, v)
        self.items[eng].append(f)

    def emit(self, stack):
        nc = self.nc
        for s in self.sems:
            s.h = stack.enter_context(nc.semaphore(s.name))
        block = stack.enter_context(nc.Block())
        eng_map = {"pe": block.tensor, "act": block.scalar, "dve": block.vector,
                   "gps": block.gpsimd, "syn": block.sync}
        for name in self.ENGINES:
            fns = self.items[name]
            if not fns:
                continue

            def body(eng, _fns=fns):
                for f in _fns:
                    f(eng)
            eng_map[name](body)


class Banks:
    def __init__(self, pbs):
        self.pb = pbs
        self.gate = [None] * len(pbs)

    def start(self, i):
        g = self.gate[i]
        return [g] if g else []

    def read_done(self, i, sem):
        self.gate[i] = (sem, sem.v)


class Pool:
    def __init__(self, aps):
        self.aps = aps
        self.gate = [None] * len(aps)
        self.i = 0

    def next(self):
        i = self.i
        self.i = (i + 1) % len(self.aps)
        g = self.gate[i]
        return i, self.aps[i], ([g] if g else [])

    def used(self, i, sem):
        self.gate[i] = (sem, sem.v)


def hi_lo_const(v):
    hi = np.float32(v)
    lo = np.float32(v - float(hi))
    return float(hi), float(lo)


def build_nc(phases=ALL_PHASES):
    nc = bass.Bass("TRN2", target_bir_lowering=False,
                   detect_race_conditions=False)

    def din(name, shape, dt=F32):
        return nc.dram_tensor(name, shape, dt, kind="ExternalInput").ap()

    x0 = din("x0", [D, T])
    w_attn = din("w_attn", [D, 3 * D])
    b_attn = din("b_attn", [1, 3 * D])
    w_proj = din("w_proj", [D, D])
    b_proj = din("b_proj", [128, NCH])
    w1t = din("w1t", [D, HID], F32R)
    w2t = din("w2t", [D, HID], F32R)
    h1t = din("h1t", [HID, HID], F32R)
    h2t = din("h2t", [HID, HID], F32R)
    w3t = din("w3t", [HID, D], F32R)
    s1c = din("s1c", [128, 32])
    s2c = din("s2c", [128, 32])
    h1sc = din("h1sc", [128, 32])
    h2sc = din("h2sc", [128, 32])
    w3sc = din("w3sc", [128, 8])
    wqkvo = din("wqkvo", [D, 4 * D])
    bqkv = din("bqkv", [1, 3 * D])
    bo_c = din("bo_c", [128, NCH])
    f1 = din("f1", [D, HID])
    f2 = din("f2", [HID, D])
    convw = din("convw", [128, 96])
    convb = din("convb", [128, 32])
    perm = din("perm", [128, 128])
    ident = din("ident", [128, 128])
    fbits = din("fbits", [128, 5])
    ssign = din("ssign", [128, 1])

    y_out = nc.dram_tensor("y", [D, T], F32, kind="ExternalOutput").ap()
    t1_hbm = nc.dram_tensor("t1_hbm", [D, HID], BF16).ap()
    xs_hbm = nc.dram_tensor("xs_hbm", [D, T], F32).ap()
    dbg_o = nc.dram_tensor("dbg_o", [D, T], F32, kind="ExternalOutput").ap() if DEBUG_O else None
    dbg_b = nc.dram_tensor("dbg_b", [128, 8192], F32, kind="ExternalOutput").ap() if DEBUG_O else None
    dbg_c = nc.dram_tensor("dbg_c", [128, 16384], BF16, kind="ExternalOutput").ap() if DEBUG_O else None
    t2_hbm = nc.dram_tensor("t2_hbm", [HID, D], BF16).ap()

    p = Prog(nc)
    sb = p.sb

    s_pe = p.sem("s_pe")
    s_act = p.sem("s_act")
    s_dve = p.sem("s_dve")
    s_gps = p.sem("s_gps")
    s_dsyn = p.sem("s_dsyn")
    s_dgps = p.sem("s_dgps")

    xb = sb("xb", [128, NCH * T], F32)
    meanb = sb("meanb", [128, T], F32)
    rstdb = sb("rstdb", [128, T], F32)
    ones14 = sb("ones14", [128, 128], F32)
    onescol = sb("onescol", [1, 128], F32)
    onesAc = sb("onesAc", [128, 1], F32)
    onesBc = sb("onesBc", [128, 1], F32)
    cpool = sb("cpool", [128, 8], F32)
    C_EIGHTH, C_EPS, C_ONE, C_NEG2, C_EPS2 = 0, 1, 2, 3, 4
    cells = sb("cells", [1, 32], F32)
    bcol = sb("bcol", [128, 8], F32)
    sqs = sb("sqs", [128, 2 * 256], F32)

    pb = [nc.alloc_psum_tensor(f"pb{i}", [128, 512], F32) for i in range(8)]
    banks = Banks(pb)


    def row(a, n):
        return rows[0:1, a:a + n]

    def cell(i, n=1):
        return cells[0:1, i:i + n]

    p.op("dve", "memset", ones14[:, :], 1.0 / 1024.0)
    p.op("dve", "memset", onescol[:, :], 1.0)
    p.op("dve", "memset", onesAc[0:64, :], 1.0)
    p.op("dve", "memset", onesAc[64:128, :], 0.0)
    p.op("dve", "memset", onesBc[0:64, :], 0.0)
    p.op("dve", "memset", onesBc[64:128, :], 1.0)
    p.op("dve", "memset", cpool[:, C_EIGHTH:C_EIGHTH + 1], 0.125)
    p.op("dve", "memset", cpool[:, C_EPS:C_EPS + 1], LN_EPS)
    p.op("dve", "memset", bcol[:, :], 0.0)
    p.op("dve", "memset", cpool[:, C_ONE:C_ONE + 1], 1.0)
    p.op("dve", "memset", cpool[:, C_NEG2:C_NEG2 + 1], -2.0)
    p.op("dve", "memset", cpool[:, C_EPS2:C_EPS2 + 1], 1e-8, inc=s_dve)

    x0v = x0.rearrange("(c q) t -> q c t", q=128)
    xbv = xb[:, :].rearrange("q (c t) -> q c t", c=NCH)
    p.op("syn", "dma_start", out=xbv, in_=x0v, inc=s_dsyn, inc16=True)
    init_dma = w(s_dsyn)


    def xch(c, t0=0, tw=T):
        return xb[:, c * T + t0:c * T + t0 + tw]

    def phase_barrier():
        snaps = [w(s_pe), w(s_act), w(s_dve), w(s_gps), w(s_dsyn), w(s_dgps)]
        for e in ("pe", "act", "dve", "gps", "syn"):
            p.wait(e, *snaps)

    sqpool = Pool([sqs[:, 0:256], sqs[:, 256:512]])

    def ln_stats(input_rdy=()):
        for j in range(8):
            t0 = j * 256
            wm = banks.start(4)
            ws = banks.start(5)
            for c in range(NCH):
                si, sl, g = sqpool.next()
                p.op("dve", "tensor_mul", sl, xch(c, t0, 256), xch(c, t0, 256),
                     waits=g + list(input_rdy), inc=s_dve)
                p.op("pe", "matmul", pb[4][:, 0:256], ones14[:, :],
                     xch(c, t0, 256),
                     start=(c == 0), stop=(c == NCH - 1),
                     waits=(wm if c == 0 else []) + list(input_rdy))
                p.op("pe", "matmul", pb[5][:, 0:256], ones14[:, :], sl,
                     start=(c == 0), stop=(c == NCH - 1),
                     waits=[w(s_dve)] + (ws if c == 0 else []),
                     inc=s_pe)
                sqpool.used(si, s_pe)
            mmd = w(s_pe)
            p.op("dve", "tensor_copy", meanb[:, t0:t0 + 256], pb[4][:, 0:256],
                 waits=[mmd], inc=s_dve)
            banks.read_done(4, s_dve)
            si1, sl1, g1 = sqpool.next()
            p.op("dve", "tensor_mul", sl1, meanb[:, t0:t0 + 256],
                 meanb[:, t0:t0 + 256], waits=g1)
            p.op("dve", "tensor_sub", sl1, pb[5][:, 0:256], sl1, inc=s_dve)
            banks.read_done(5, s_dve)
            si2, sl2, g2 = sqpool.next()
            p.op("act", "activation", sl2, sl1, AF.Sqrt,
                 bias=cpool[:, C_EPS:C_EPS + 1], waits=[w(s_dve)] + g2,
                 inc=s_act)
            sqpool.used(si1, s_act)
            p.op("dve", "reciprocal", rstdb[:, t0:t0 + 256], sl2,
                 waits=[w(s_act)], inc=s_dve)
            sqpool.used(si2, s_dve)

    def ln_tile_into(dsth, dstl, scratch, c, t0, tw, extra_waits=()):
        p.op("dve", "tensor_sub", scratch, xch(c, t0, tw), meanb[:, t0:t0 + tw],
             waits=list(extra_waits))
        p.op("dve", "tensor_mul", scratch, scratch, rstdb[:, t0:t0 + tw])
        p.op("dve", "tensor_copy", dsth, scratch)
        p.op("dve", "tensor_sub", dstl, scratch, dsth, inc=s_dve)
        return w(s_dve)

    def split_into(src, dsth, dstl, waits=()):
        p.op("dve", "tensor_copy", dsth, src, waits=list(waits))
        p.op("dve", "tensor_sub", dstl, src, dsth, inc=s_dve)
        return w(s_dve)

    def colize(dst_col_idx, src_row_ap, bank=7):
        wb = banks.start(bank)
        if src_row_ap.shape[-1] == 1:
            lhsT, rhs = onescol[:, :], src_row_ap
        else:
            lhsT, rhs = src_row_ap, cpool[0:1, C_ONE:C_ONE + 1]
        p.op("pe", "matmul", pb[bank][:, dst_col_idx:dst_col_idx + 1],
             lhsT, rhs, start=True, stop=True,
             waits=wb, inc=s_pe)
        p.op("dve", "tensor_copy", bcol[:, dst_col_idx:dst_col_idx + 1],
             pb[bank][:, dst_col_idx:dst_col_idx + 1], waits=[w(s_pe)],
             inc=s_dve)
        banks.read_done(bank, s_dve)
        return w(s_dve)

    def l0_attention():
        pstk = contextlib.ExitStack()
        p.default_stack = pstk
        ln_src_dram = x0
        rows = sb("rows", [1, 1280], F32)
        R_BIAS, R_QKB, R_RINV = 0, 256, 512

        def row(a, n):
            return rows[0:1, a:a + n]
        lnsrc = sb("lnsrc", [128, 2 * 512], F32)
        lspool = Pool([lnsrc[:, 0:512], lnsrc[:, 512:1024]])
        sql = sb("sql", [128, 2 * 512], F32)
        sqlpool = Pool([sql[:, 0:512], sql[:, 512:1024]])
        NROPE = 16
        lcos = sb("lcos", [128, 16], F32)
        lsin = sb("lsin", [128, 16], F32)
        sfb = sb("sfb", [128, 5], F32)
        ssg = sb("ssg", [128, 1], F32)
        sperm = sb("sperm", [128, 128], F32)
        p.op("syn", "dma_start", out=sfb[:, :], in_=fbits, inc=s_dsyn,
             inc16=True)
        p.op("syn", "dma_start", out=ssg[:, :], in_=ssign, inc=s_dsyn,
             inc16=True)
        p.op("syn", "dma_start", out=sperm[:, :], in_=perm, inc=s_dsyn,
             inc16=True)
        rope_dma = w(s_dsyn)
        tht = sb("tht", [128, 4], F32)
        th1 = 10000.0 ** (-1.0 / 32.0)
        p.op("dve", "memset", tht[:, 0:1], 1.0)
        p.wait("dve", rope_dma)
        for k in range(5):
                ch, cl = hi_lo_const(th1 ** (2 ** k))
                p.op("dve", "tensor_scalar", tht[:, 1:2], sfb[:, k:k + 1], ch - 1.0, 1.0,
                         OP.mult, OP.add)
                p.op("dve", "scalar_tensor_tensor", tht[:, 1:2], sfb[:, k:k + 1], cl,
                         tht[:, 1:2], OP.mult, OP.add)
                p.op("dve", "tensor_mul", tht[:, 0:1], tht[:, 0:1], tht[:, 1:2])
        p2h, p2l = hi_lo_const(math.pi / 2)
        p.op("dve", "tensor_scalar", tht[:, 2:3], tht[:, 0:1], -1.0, p2h, OP.mult,
                 OP.add)
        p.op("dve", "tensor_scalar", tht[:, 2:3], tht[:, 2:3], p2l, None, OP.add,
                 inc=s_dve)
        p.op("act", "activation", tht[:, 1:2], tht[:, 0:1], AF.Sin, waits=[w(s_dve)])
        p.op("act", "activation", tht[:, 0:1], tht[:, 2:3], AF.Sin, inc=s_act)
        p.op("dve", "memset", lsin[:, 0:1], 0.0)
        p.wait("dve", w(s_act))
        p.op("dve", "memset", lcos[:, 0:1], 1.0)
        thc = tht[:, 0:1]
        ths = tht[:, 1:2]
        t1c = tht[:, 2:3]
        t2c = tht[:, 3:4]
        L = 1
        while L < NROPE:
                p.op("dve", "tensor_scalar", t2c, ths, -1.0, None, OP.mult)
                p.op("dve", "tensor_scalar", lcos[:, L:2 * L], lcos[:, 0:L], thc, None,
                         OP.mult)
                p.op("dve", "scalar_tensor_tensor", lcos[:, L:2 * L], lsin[:, 0:L], t2c,
                         lcos[:, L:2 * L], OP.mult, OP.add)
                p.op("dve", "tensor_scalar", lsin[:, L:2 * L], lsin[:, 0:L], thc, None,
                         OP.mult)
                p.op("dve", "scalar_tensor_tensor", lsin[:, L:2 * L], lcos[:, 0:L], ths,
                         lsin[:, L:2 * L], OP.mult, OP.add)
                L *= 2
                if L < NROPE:
                        p.op("dve", "tensor_mul", t1c, thc, thc)
                        p.op("dve", "tensor_mul", t2c, ths, ths)
                        p.op("dve", "tensor_mul", ths, ths, thc)
                        p.op("dve", "tensor_scalar", ths, ths, 2.0, None, OP.mult)
                        p.op("dve", "tensor_sub", thc, t1c, t2c)
        p.op("dve", "tensor_scalar", lsin[:, :], lsin[:, :], ssg[:, :], None, OP.mult,
                 inc=s_dve)
        ropecs = sb("ropecs", [128, 2], F32)

        ln_stats(input_rdy=[init_dma])
        stats_rdy = w(s_dve)

        wldA = sb("wldA", [128, 1024], F32)
        wqkh = sb("wqkh", [128, 2 * 1024], F32R)
        wqkl = sb("wqkl", [128, 2 * 1024], F32R)
        wvh = sb("wvh", [128, 8 * 256], F32R)
        wvl = sb("wvl", [128, 8 * 256], F32R)
        lnhp = [sb(f"lnh4_{i}", [128, 512], F32R) for i in range(2)]
        lnlp = [sb(f"lnl4_{i}", [128, 512], F32R) for i in range(2)]
        lnpool = Pool(list(zip(lnhp, lnlp)))
        vq = sb("vq", [128, 16 * 130], F32)
        qth = sb("qth", [128, 512], F32R)
        qtl = sb("qtl", [128, 512], F32R)
        khi = sb("khi", [128, T], F32R)
        klo = sb("klo", [128, T], F32R)
        ptile = sb("ptile", [128, 2 * 512], F32)
        otAh = sb("otAh", [64, 512], F32R)
        otAl = sb("otAl", [64, 512], F32R)
        otBh = sb("otBh", [64, 512], F32R)
        otBl = sb("otBl", [64, 512], F32R)
        wps = sb("wps", [128, 1024], F32R)
        wpld = sb("wpld", [64, 256], F32)
        bprj = sb("bprj", [128, 8], F32)

        p.op("syn", "dma_start", out=bprj[:, 0:8], in_=b_proj,
             inc=s_dsyn, inc16=True)
        bprj_rdy = w(s_dsyn)
        rope_tab_rdy = w(s_dve)

        def ln512(c, j, extra=()):
            si, (lh, ll), g = lnpool.next()
            t0 = j * 512
            ls_i, lsrc, lg = lspool.next()
            p.op("syn", "dma_start", out=lsrc,
                 in_=ln_src_dram[c * 128:(c + 1) * 128, t0:t0 + 512],
                 waits=lg, inc=s_dsyn, inc16=True)
            sq_i, scr, sg = sqlpool.next()
            p.op("dve", "tensor_sub", scr, lsrc, meanb[:, t0:t0 + 512],
                 waits=list(extra) + g + sg + [w(s_dsyn)])
            p.op("dve", "tensor_mul", scr, scr, rstdb[:, t0:t0 + 512],
                 inc=s_dve)
            lspool.used(ls_i, s_dve)
            p.op("act", "copy", lh[:, :], scr, waits=[w(s_dve)], inc=s_act)
            p.op("dve", "tensor_sub", ll[:, :], scr, lh[:, :], waits=[w(s_act)],
                 inc=s_dve)
            sqlpool.used(sq_i, s_dve)
            return si, lh, ll, w(s_dve)

        first_ln = [True]

        for quarter in range(4):
            vcol0 = 2 * D + quarter * 256
            for c in range(NCH):
                slot = c % 2
                p.op("syn", "dma_start", out=wldA[:, slot * 256:slot * 256 + 256],
                     in_=w_attn[c * 128:(c + 1) * 128, vcol0:vcol0 + 256],
                     waits=[w(s_dve)], inc=s_dsyn, inc16=True)
                p.op("dve", "tensor_copy", wvh[:, c * 256:(c + 1) * 256],
                     wldA[:, slot * 256:slot * 256 + 256],
                     waits=[w(s_dsyn), w(s_pe)])
                p.op("dve", "tensor_sub", wvl[:, c * 256:(c + 1) * 256],
                     wldA[:, slot * 256:slot * 256 + 256],
                     wvh[:, c * 256:(c + 1) * 256], inc=s_dve)
            wv_rdy = w(s_dve)
            p.op("syn", "dma_start", out=row(R_BIAS, 256),
                 in_=b_attn[:, vcol0:vcol0 + 256], waits=[w(s_pe)],
                 inc=s_dsyn, inc16=True)
            brow_rdy = w(s_dsyn)

            for hp in (2 * quarter, 2 * quarter + 1):
                hloc = hp % 2
                qc0 = hp * 128
                wldAv = wldA[:, :].rearrange("q (c o) -> q c o", c=NCH)
                p.op("syn", "dma_start", out=wldAv,
                     in_=w_attn[:, qc0:qc0 + 128].rearrange(
                         "(c q) o -> q c o", q=128),
                     waits=[w(s_dve)], inc=s_dsyn, inc16=True)
                split_into(wldA[:, :], wqkh[:, 0:1024], wqkl[:, 0:1024],
                           waits=[w(s_dsyn), w(s_pe)])
                p.op("syn", "dma_start", out=wldAv,
                     in_=w_attn[:, D + qc0:D + qc0 + 128].rearrange(
                         "(c q) o -> q c o", q=128),
                     waits=[w(s_dve)], inc=s_dsyn, inc16=True)
                wqk_rdy = split_into(wldA[:, :], wqkh[:, 1024:2048],
                                     wqkl[:, 1024:2048],
                                     waits=[w(s_dsyn), w(s_pe)])
                p.op("syn", "dma_start", out=row(R_QKB, 128),
                     in_=b_attn[:, qc0:qc0 + 128],
                     waits=[w(s_pe)], inc=s_dsyn, inc16=True)
                p.op("syn", "dma_start", out=row(R_QKB + 128, 128),
                     in_=b_attn[:, D + qc0:D + qc0 + 128], inc=s_dsyn,
                     inc16=True)
                p.wait("pe", w(s_dsyn), w(s_act))
                colize(0, row(R_QKB, 128))
                bias_rdy = colize(1, row(R_QKB + 128, 128))
                zcols = vq[:, :].rearrange(
                    "q (i h c) -> q i h c", i=16, h=2)[:, :, :, 64:65]
                p.op("dve", "memset", zcols, 1.0, waits=[w(s_pe)], inc=s_dve)
                z_rdy = w(s_dve)
                p.op("dve", "memset", cells[0:1, 0:4], 0.0, inc=s_dve)
                hA, hB = 2 * hp, 2 * hp + 1
                p.op("dve", "tensor_copy", ropecs[0:64, 0:1],
                     lcos[0:64, hA:hA + 1], waits=[rope_tab_rdy, w(s_act)])
                p.op("dve", "tensor_copy", ropecs[64:128, 0:1],
                     lcos[64:128, hB:hB + 1])
                p.op("dve", "tensor_copy", ropecs[0:64, 1:2],
                     lsin[0:64, hA:hA + 1])
                p.op("dve", "tensor_copy", ropecs[64:128, 1:2],
                     lsin[64:128, hB:hB + 1], inc=s_dve)
                ropec_rdy = w(s_dve)

                for j in range(4):
                    t0 = j * 512
                    wsv = [banks.start(b) for b in range(4)]
                    wsq = banks.start(4)
                    wsk = banks.start(5)
                    for c in range(NCH):
                        extra = [stats_rdy] if first_ln[0] else []
                        first_ln[0] = False
                        si, lh, ll, lt = ln512(c, j, extra=extra)
                        for sub in range(4):
                            st_ = lh[:, sub * 128:(sub + 1) * 128]
                            sl_ = ll[:, sub * 128:(sub + 1) * 128]
                            p.op("pe", "matmul", pb[sub][0:128, 0:256], st_,
                                 wvh[:, c * 256:(c + 1) * 256], start=(c == 0),
                                 stop=False,
                                 waits=[lt] + ([wv_rdy] if c == 0 else []) +
                                       (wsv[sub] if c == 0 else []))
                            p.op("pe", "matmul", pb[sub][0:128, 0:256], sl_,
                                 wvh[:, c * 256:(c + 1) * 256], start=False,
                                 stop=False)
                            p.op("pe", "matmul", pb[sub][0:128, 0:256], st_,
                                 wvl[:, c * 256:(c + 1) * 256], start=False,
                                 stop=False)
                        qh_ = wqkh[:, c * 128:(c + 1) * 128]
                        ql_ = wqkl[:, c * 128:(c + 1) * 128]
                        kh_ = wqkh[:, 1024 + c * 128:1024 + (c + 1) * 128]
                        kl_ = wqkl[:, 1024 + c * 128:1024 + (c + 1) * 128]
                        p.op("pe", "matmul", pb[4][:, :], qh_, lh[:, :],
                             start=(c == 0), stop=False,
                             waits=([wqk_rdy] if (j == 0 and c == 0) else []) +
                                   (wsq if c == 0 else []))
                        p.op("pe", "matmul", pb[4][:, :], qh_, ll[:, :],
                             start=False, stop=False)
                        p.op("pe", "matmul", pb[4][:, :], ql_, lh[:, :],
                             start=False, stop=(c == NCH - 1))
                        p.op("pe", "matmul", pb[5][:, :], kh_, lh[:, :],
                             start=(c == 0), stop=False,
                             waits=(wsk if c == 0 else []))
                        p.op("pe", "matmul", pb[5][:, :], kh_, ll[:, :],
                             start=False, stop=False)
                        p.op("pe", "matmul", pb[5][:, :], kl_, lh[:, :],
                             start=False, stop=(c == NCH - 1),
                             inc=s_pe)
                        lnpool.used(si, s_pe)
                    for sub in range(4):
                        i = 4 * j + sub
                        p.op("pe", "matmul", pb[sub][0:128, 0:256], onescol[:, :],
                             row(R_BIAS, 256), start=False, stop=True,
                             waits=([brow_rdy] if i == 0 else []), inc=s_pe)
                        vdone = w(s_pe)
                        for hh in range(2):
                            p.op("act", "copy",
                                 vq[:, i * 130 + hh * 65:i * 130 + hh * 65 + 64],
                                 pb[sub][0:128, hloc * 128 + hh * 64:
                                         hloc * 128 + (hh + 1) * 64],
                                 waits=([vdone] if hh == 0 else []),
                                 inc=(s_act if hh == 1 else None))
                        banks.read_done(sub, s_act)
                    for (bank, bci, dsth, dstl) in (
                            (4, 0, qth[:, :], qtl[:, :]),
                            (5, 1, khi[:, t0:t0 + 512], klo[:, t0:t0 + 512])):
                        qk_done = w(s_pe)
                        sq_i, scr, sg = sqlpool.next()
                        p.op("act", "activation", scr, pb[bank][:, :],
                             AF.Identity, bias=bcol[:, bci:bci + 1],
                             waits=[qk_done, bias_rdy, w(s_dve)] + sg,
                             inc=s_act)
                        banks.read_done(bank, s_act)
                        wb7 = banks.start(7)
                        p.op("pe", "matmul", pb[7][:, :], sperm[:, :], scr,
                             start=True, stop=True, waits=[w(s_act)] + wb7,
                             inc=s_pe)
                        swp = w(s_pe)
                        sq_i2, scr2, sg2 = sqlpool.next()
                        p.op("dve", "tensor_scalar", scr2, pb[7][:, :],
                             ropecs[:, 1:2], None, OP.mult,
                             waits=[swp, ropec_rdy] + sg2, inc=s_dve)
                        banks.read_done(7, s_dve)
                        p.op("dve", "tensor_scalar", scr, scr,
                             ropecs[:, 0:1], None, OP.mult)
                        p.op("dve", "tensor_add", scr, scr, scr2, inc=s_dve)
                        sqlpool.used(sq_i2, s_dve)
                        p.op("act", "copy", dsth, scr,
                             waits=[w(s_dve), w(s_pe)], inc=s_act)
                        p.op("dve", "tensor_sub", dstl, scr, dsth,
                             waits=[w(s_act)], inc=s_dve)
                        sqlpool.used(sq_i, s_dve)
                    qk_split_rdy = w(s_dve)
                    if DEBUG_O and hp == 0:
                        p.op("gps", "dma_start", out=dbg_b[:, 0:2048],
                             in_=meanb[:, :], waits=[qk_split_rdy],
                             inc=s_dgps, inc16=True)
                        p.op("gps", "dma_start", out=dbg_b[:, 2048:4096],
                             in_=rstdb[:, :], inc=s_dgps, inc16=True)
                        p.op("gps", "dma_start",
                             out=dbg_b[:, 4096 + t0:4096 + t0 + 512],
                             in_=khi[:, t0:t0 + 512], inc=s_dgps, inc16=True)
                        p.op("gps", "dma_start",
                             out=dbg_b[:, 6144 + t0 // 4:6144 + t0 // 4 + 128],
                             in_=qth[:, 0:128], inc=s_dgps, inc16=True)
                        p.op("gps", "dma_start", out=dbg_b[:, 6656:6658],
                             in_=ropecs[:, :], inc=s_dgps, inc16=True)
                        p.op("gps", "dma_start", out=dbg_b[:, 6658:6674],
                             in_=lcos[:, :], inc=s_dgps, inc16=True)
                        p.op("gps", "dma_start", out=dbg_b[:, 6674:6690],
                             in_=lsin[:, :], inc=s_dgps, inc16=True)
                        p.op("gps", "dma_start", out=dbg_b[:, 6690:6698],
                             in_=bcol[:, :], inc=s_dgps, inc16=True)

                    for r, (bhi, blo, msk, acc) in enumerate(
                            ((qth[:, :], qtl[:, :], onesAc, 0),
                             (qth[:, :], qtl[:, :], onesBc, 1),
                             (khi[:, t0:t0 + 512], klo[:, t0:t0 + 512],
                              onesAc, 2),
                             (khi[:, t0:t0 + 512], klo[:, t0:t0 + 512],
                              onesBc, 3))):
                        msk = msk[:, :]
                        si, sl, g = sqlpool.next()
                        p.op("dve", "tensor_add", sl, bhi, blo,
                             waits=[qk_split_rdy] + g)
                        p.op("dve", "tensor_mul", sl, sl, sl, inc=s_dve)
                        wb6 = banks.start(6)
                        p.op("pe", "matmul", pb[6][0:1, :], msk, sl,
                             start=True, stop=True, waits=[w(s_dve)] + wb6,
                             inc=s_pe)
                        sqlpool.used(si, s_pe)
                        p.op("dve", "reduce_max", cell(16), pb[6][0:1, :], AX,
                             waits=[w(s_pe)], inc=s_dve)
                        banks.read_done(6, s_dve)
                        if r < 2:
                            p.op("dve", "tensor_copy", cell(acc), cell(16))
                        else:
                            p.op("dve", "tensor_max", cell(acc), cell(acc),
                                 cell(16))
                    for hh in range(2):
                        p.op("dve", "tensor_mul", cell(4 + hh), cell(hh),
                             cell(2 + hh))
                    p.op("dve", "tensor_copy", cell(8, 2), cell(4, 2),
                         inc=s_dve)
                    p.op("act", "activation", cell(6, 2), cell(8, 2), AF.Sqrt,
                         waits=[w(s_dve)], inc=s_act)
                    p.op("dve", "tensor_scalar", cell(10, 2), cell(6, 2),
                         -0.125, None, OP.mult, waits=[w(s_act)], inc=s_dve)
                    p.wait("pe", w(s_dve), w(s_act))
                    colize(2, cell(10))
                    smax_rdy = colize(3, cell(11))

                    ntk = 4 * j + 4
                    for half in range(2):
                        hrow0 = 64 * half
                        obank = 2 + half
                        wso = banks.start(obank)
                        for i in range(ntk):
                            k0 = i * 128
                            sbk = i % 2
                            wss = banks.start(sbk)
                            p.op("pe", "matmul", pb[sbk][:, :],
                                 khi[hrow0:hrow0 + 64, k0:k0 + 128],
                                 qth[hrow0:hrow0 + 64, :],
                                 start=True, stop=False,
                                 waits=[qk_split_rdy] + wss)
                            p.op("pe", "matmul", pb[sbk][:, :],
                                 khi[hrow0:hrow0 + 64, k0:k0 + 128],
                                 qtl[hrow0:hrow0 + 64, :],
                                 start=False, stop=False)
                            p.op("pe", "matmul", pb[sbk][:, :],
                                 klo[hrow0:hrow0 + 64, k0:k0 + 128],
                                 qth[hrow0:hrow0 + 64, :],
                                 start=False, stop=True, inc=s_pe)
                            s_done = w(s_pe)
                            pt = ptile[:, (i % 2) * 512:(i % 2) * 512 + 512]
                            p.op("act", "activation", pt, pb[sbk][:, :], AF.Exp,
                                 bias=bcol[:, 2 + half:3 + half],
                                 scale=cpool[:, C_EIGHTH:C_EIGHTH + 1],
                                 waits=[s_done, smax_rdy], inc=s_act)
                            banks.read_done(sbk, s_act)
                            pw = [w(s_act)]
                            if k0 >= t0:
                                p.op("gps", "affine_select", pt, pt,
                                     [[1, 512]], OP.is_ge, 0.0,
                                     base=t0 - k0, channel_multiplier=-1,
                                     waits=[w(s_act)], inc=s_gps)
                                pw = [w(s_gps)]
                            vstat = vq[:, i * 130 + half * 65:
                                       i * 130 + half * 65 + 65]
                            p.op("pe", "matmul", pb[obank][0:65, :], vstat, pt,
                                 start=(i == 0), stop=(i == ntk - 1),
                                 waits=pw + ([z_rdy] if i == 0 else []) +
                                       (wso if i == 0 else []),
                                 inc=(s_pe if i == ntk - 1 else None))
                    pv_done = w(s_pe)
                    otA = ptile[0:64, 0:512]
                    otB = ptile[0:64, 512:1024]
                    for half in range(2):
                        obank = 2 + half
                        ot = otA if half == 0 else otB
                        p.op("dve", "reciprocal", row(R_RINV, 512),
                             pb[obank][64:65, :],
                             waits=[pv_done] + ([w(s_gps)] if half == 0 else []),
                             inc=s_dve)
                        wb6 = banks.start(6)
                        p.op("pe", "matmul", pb[6][0:64, :], onescol[0:1, 0:64],
                             row(R_RINV, 512), start=True, stop=True,
                             waits=[w(s_dve)] + wb6, inc=s_pe)
                        si_r, scr_r, sg_r = sqlpool.next()
                        p.op("dve", "tensor_copy", scr_r[0:64, :],
                             pb[6][0:64, :], waits=[w(s_pe)] + sg_r, inc=s_dve)
                        banks.read_done(6, s_dve)
                        p.op("dve", "tensor_mul", ot, pb[obank][0:64, :],
                             scr_r[0:64, :], inc=s_dve)
                        sqlpool.used(si_r, s_dve)
                        banks.read_done(obank, s_dve)
                    o_rdy = w(s_dve)
                    if DEBUG_O:
                        p.op("gps", "dma_start",
                             out=dbg_o[hp * 128:hp * 128 + 64, t0:t0 + 512],
                             in_=otA, waits=[o_rdy], inc=s_dgps, inc16=True)
                        p.op("gps", "dma_start",
                             out=dbg_o[hp * 128 + 64:hp * 128 + 128, t0:t0 + 512],
                             in_=otB, inc=s_dgps, inc16=True)
                    p.op("act", "copy", otAh[:, :], otA, waits=[o_rdy, w(s_pe)])
                    p.op("act", "copy", otBh[:, :], otB, inc=s_act)
                    p.op("dve", "tensor_sub", otAl[:, :], otA, otAh[:, :],
                         waits=[w(s_act)])
                    p.op("dve", "tensor_sub", otBl[:, :], otB, otBh[:, :],
                         inc=s_dve)
                    osp = w(s_dve)
                    for oc in range(NCH):
                        obk = 4 + (oc % 2)
                        slot = oc % 2
                        sA = wps[0:64, slot * 512:slot * 512 + 512]
                        p.op("syn", "dma_start",
                             out=wpld[:, :].rearrange("q (h o) -> q h o", h=2),
                             in_=w_proj[hp * 128:hp * 128 + 128,
                                        oc * 128:(oc + 1) * 128].rearrange(
                                 "(h q) o -> q h o", h=2),
                             waits=[w(s_dve)], inc=s_dsyn, inc16=True)
                        p.op("dve", "tensor_copy", sA[:, 0:256],
                             wpld[:, :], waits=[w(s_dsyn), w(s_pe)])
                        p.op("dve", "tensor_sub", sA[:, 256:512],
                             wpld[:, :], sA[:, 0:256],
                             inc=s_dve)
                        wp_rdy = w(s_dve)
                        wpAh_ = sA[:, 0:128]
                        wpBh_ = sA[:, 128:256]
                        wpAl_ = sA[:, 256:384]
                        wpBl_ = sA[:, 384:512]
                        wso2 = banks.start(obk)
                        ws0 = [osp] if oc == 0 else []
                        p.op("pe", "matmul", pb[obk][:, :], wpAh_, otAh[:, :],
                             start=True, stop=False, waits=ws0 + wso2 + [wp_rdy])
                        p.op("pe", "matmul", pb[obk][:, :], wpAh_, otAl[:, :],
                             start=False, stop=False)
                        p.op("pe", "matmul", pb[obk][:, :], wpAl_, otAh[:, :],
                             start=False, stop=False)
                        p.op("pe", "matmul", pb[obk][:, :], wpBh_, otBh[:, :],
                             start=False, stop=False)
                        p.op("pe", "matmul", pb[obk][:, :], wpBh_, otBl[:, :],
                             start=False, stop=False)
                        p.op("pe", "matmul", pb[obk][:, :], wpBl_, otBh[:, :],
                             start=False, stop=True, inc=s_pe)
                        prj_done = w(s_pe)
                        if hp == 0:
                            p.op("dve", "scalar_tensor_tensor",
                                 xch(oc, t0, 512), pb[obk][:, :],
                                 bprj[:, oc:oc + 1], xch(oc, t0, 512),
                                 OP.add, OP.add,
                                 waits=[prj_done] + ([bprj_rdy] if (j == 0 and oc == 0) else []),
                                 inc=s_dve)
                        else:
                            p.op("dve", "tensor_add", xch(oc, t0, 512),
                                 xch(oc, t0, 512), pb[obk][:, :],
                                 waits=[prj_done], inc=s_dve)
                        banks.read_done(obk, s_dve)

        phase_barrier()
        p.default_stack = None
        pstk.close()

    def l0_ffn():
        pstk = contextlib.ExitStack()
        p.default_stack = pstk
        FT = 256
        NT = T // FT
        ln_stats()
        stats_rdy = w(s_dve)

        Ah = sb("Ah", [128, 32 * FT], F32R)
        Al = sb("Al", [128, 32 * FT], F32R)
        Br = sb("Br", [128, 32 * FT], F32)
        ghi_t = sb("ghi", [128, NCH * FT], F32R)
        glo_t = sb("glo", [128, NCH * FT], F32R)
        ghi = ghi_t[:, :]
        glo = glo_t[:, :]
        wstrip = sb("wstrip", [128, 3 * 1024], F32R)
        wpool = Pool([wstrip[:, 0:1024], wstrip[:, 1024:2048],
                      wstrip[:, 2048:3072]])
        scales = sb("scales", [128, 4 * 32 + 8], F32)
        sc_s1, sc_s2, sc_h1, sc_h2, sc_w3 = 0, 32, 64, 96, 128

        p.op("syn", "dma_start", out=scales[:, 0:32], in_=s1c, inc=s_dsyn,
             inc16=True)
        p.op("syn", "dma_start", out=scales[:, 32:64], in_=s2c, inc=s_dsyn,
             inc16=True)
        p.op("syn", "dma_start", out=scales[:, 64:96], in_=h1sc, inc=s_dsyn,
             inc16=True)
        p.op("syn", "dma_start", out=scales[:, 96:128], in_=h2sc, inc=s_dsyn,
             inc16=True)
        p.op("syn", "dma_start", out=scales[:, 128:136], in_=w3sc, inc=s_dsyn,
             inc16=True)
        scales_rdy = w(s_dsyn)

        def scol(base, o):
            return scales[:, base + o:base + o + 1]

        def stream_strip(dram, ocols, c0, ncw, slot_ap):
            p.op("syn", "dma_start",
                 out=slot_ap.rearrange("q (c o) -> q c o", c=ncw),
                 in_=dram[c0 * 128:(c0 + ncw) * 128,
                          ocols:ocols + 128].rearrange(
                     "(c q) o -> q c o", q=128),
                 inc=s_dsyn, inc16=True)
            return w(s_dsyn)

        for t in range(NT):
            t0 = t * FT
            gw = [w(s_pe), stats_rdy] if True else []
            for c in range(NCH):
                sq_i, scr, sg = sqpool.next()
                scr = scr[:, 0:FT]
                p.op("dve", "tensor_sub", scr, xch(c, t0, FT),
                     meanb[:, t0:t0 + FT], waits=gw + sg)
                gw = []
                p.op("dve", "tensor_mul", scr, scr, rstdb[:, t0:t0 + FT],
                     inc=s_dve)
                p.op("act", "copy", ghi[:, c * FT:(c + 1) * FT], scr,
                     waits=[w(s_dve)], inc=s_act)
                p.op("dve", "tensor_sub", glo[:, c * FT:(c + 1) * FT], scr,
                     ghi[:, c * FT:(c + 1) * FT], waits=[w(s_act)], inc=s_dve)
                sqpool.used(sq_i, s_dve)
            g_rdy = w(s_dve)

            for o in range(32):
                s1i, w1s, sg1 = wpool.next()
                p.wait("syn", *sg1)
                r1 = stream_strip(w1t, o * 128, 0, 8, w1s)
                s2i, w2s, sg2 = wpool.next()
                p.wait("syn", *sg2)
                r2 = stream_strip(w2t, o * 128, 0, 8, w2s)
                ba = o % 2
                bb = 2 + o % 2
                wsa = banks.start(ba)
                wsb = banks.start(bb)
                for c in range(NCH):
                    gh = ghi[:, c * FT:(c + 1) * FT]
                    gl = glo[:, c * FT:(c + 1) * FT]
                    w1c = w1s[:, c * 128:(c + 1) * 128]
                    w2c = w2s[:, c * 128:(c + 1) * 128]
                    p.op("pe", "matmul", pb[ba][:, 0:FT], w1c, gh,
                         start=(c == 0), stop=False,
                         waits=([g_rdy, r2] if c == 0 else []) +
                               (wsa if c == 0 else []))
                    p.op("pe", "matmul", pb[ba][:, 0:FT], w1c, gl,
                         start=False, stop=(c == NCH - 1))
                    p.op("pe", "matmul", pb[bb][:, 0:FT], w2c, gh,
                         start=(c == 0), stop=False,
                         waits=(wsb if c == 0 else []))
                    p.op("pe", "matmul", pb[bb][:, 0:FT], w2c, gl,
                         start=False, stop=(c == NCH - 1),
                         inc=s_pe)
                wpool.used(s1i, s_pe)
                wpool.used(s2i, s_pe)
                mmd = w(s_pe)
                sq_i, scr, sg = sqpool.next()
                scr = scr[:, 0:FT]
                p.op("act", "activation", scr, pb[ba][:, 0:FT], AF.Silu,
                     scale=scol(sc_s1, o),
                     waits=[mmd, scales_rdy] + sg, inc=s_act)
                banks.read_done(ba, s_act)
                sq_i2, scr2, sg2 = sqpool.next()
                scr2 = scr2[:, 0:FT]
                p.op("dve", "scalar_tensor_tensor", scr2, pb[bb][:, 0:FT],
                     scol(sc_s2, o), scr, OP.mult, OP.mult,
                     waits=[w(s_act)] + sg2, inc=s_dve)
                banks.read_done(bb, s_dve)
                p.op("act", "copy", Ah[:, o * FT:(o + 1) * FT], scr2,
                     waits=[w(s_dve), w(s_pe)], inc=s_act)
                p.op("dve", "tensor_sub", Al[:, o * FT:(o + 1) * FT], scr2,
                     Ah[:, o * FT:(o + 1) * FT], waits=[w(s_act)], inc=s_dve)
                sqpool.used(sq_i, s_act)
                sqpool.used(sq_i2, s_dve)
            a_rdy = w(s_dve)

            for stage, (hdram, hsc) in enumerate(((h1t, sc_h1), (h2t, sc_h2))):
                for o in range(32):
                    bk_ = 4 + o % 2
                    wsk = banks.start(bk_)
                    first = [a_rdy] if o == 0 else []
                    for quarter_ in range(4):
                        si, slot, sg = wpool.next()
                        p.wait("syn", *sg)
                        rs = stream_strip(hdram, o * 128, quarter_ * 8, 8, slot)
                        for c in range(8):
                            cc = quarter_ * 8 + c
                            hc = slot[:, c * 128:(c + 1) * 128]
                            p.op("pe", "matmul", pb[bk_][:, 0:FT], hc,
                                 Ah[:, cc * FT:(cc + 1) * FT],
                                 start=(cc == 0), stop=False,
                                 waits=([rs] if c == 0 else []) + first +
                                       (wsk if cc == 0 else []))
                            first = []
                            p.op("pe", "matmul", pb[bk_][:, 0:FT], hc,
                                 Al[:, cc * FT:(cc + 1) * FT],
                                 start=False, stop=(cc == 31),
                                 inc=s_pe)
                        wpool.used(si, s_pe)
                    mmd = w(s_pe)
                    p.op("act", "activation", Br[:, o * FT:(o + 1) * FT],
                         pb[bk_][:, 0:FT], AF.Gelu, scale=scol(hsc, o),
                         waits=[mmd, w(s_dve)], inc=s_act)
                    banks.read_done(bk_, s_act)
                br_rdy = w(s_act)
                for o in range(32):
                    p.op("act", "copy", Ah[:, o * FT:(o + 1) * FT],
                         Br[:, o * FT:(o + 1) * FT],
                         waits=([br_rdy, w(s_pe)] if o == 0 else []),
                         inc=s_act)
                    p.op("dve", "tensor_sub", Al[:, o * FT:(o + 1) * FT],
                         Br[:, o * FT:(o + 1) * FT],
                         Ah[:, o * FT:(o + 1) * FT], waits=[w(s_act)],
                         inc=s_dve)
                a_rdy = w(s_dve)

            for o in range(NCH):
                bk_ = 6 + o % 2
                wsk = banks.start(bk_)
                first = [a_rdy] if o == 0 else []
                for quarter_ in range(4):
                    si, slot, sg = wpool.next()
                    p.wait("syn", *sg)
                    rs = stream_strip(w3t, o * 128, quarter_ * 8, 8, slot)
                    for c in range(8):
                        cc = quarter_ * 8 + c
                        hc = slot[:, c * 128:(c + 1) * 128]
                        p.op("pe", "matmul", pb[bk_][:, 0:FT], hc,
                             Ah[:, cc * FT:(cc + 1) * FT],
                             start=(cc == 0), stop=False,
                             waits=([rs] if c == 0 else []) + first +
                                   (wsk if cc == 0 else []))
                        first = []
                        p.op("pe", "matmul", pb[bk_][:, 0:FT], hc,
                             Al[:, cc * FT:(cc + 1) * FT],
                             start=False, stop=(cc == 31),
                             inc=s_pe)
                    wpool.used(si, s_pe)
                mmd = w(s_pe)
                p.op("dve", "scalar_tensor_tensor", xch(o, t0, FT),
                     pb[bk_][:, 0:FT], scol(sc_w3, o), xch(o, t0, FT),
                     OP.mult, OP.add, waits=[mmd], inc=s_dve)
                banks.read_done(bk_, s_dve)

        phase_barrier()
        p.default_stack = None
        pstk.close()

    def l1_attention():
        pstk = contextlib.ExitStack()
        p.default_stack = pstk
        ln_src_dram = xs_hbm
        ln_stats()
        stats_rdy = w(s_dve)
        xbv2 = xb[:, :].rearrange("q (c t) -> q c t", c=NCH)
        xsv = xs_hbm.rearrange("(c q) t -> q c t", q=128)
        p.op("gps", "dma_start", out=xsv, in_=xbv2, waits=[w(s_dve), w(s_pe)],
             inc=s_dgps, inc16=True)
        spill_rdy = w(s_dgps)

        rows = sb("rows", [1, 2304], F32)
        R_VB, R_QKB, R_K2, R_RINV = 0, 256, 512, 1536

        def row(a, n):
            return rows[0:1, a:a + n]
        lnsrc = sb("lnsrc", [128, 2 * 512], F32)
        lspool = Pool([lnsrc[:, 0:512], lnsrc[:, 512:1024]])
        sql = sb("sql", [128, 2 * 512], F32)
        sqlpool = Pool([sql[:, 0:512], sql[:, 512:1024]])
        wldA = sb("wldA", [128, 1024], F32)
        wqkh = sb("wqkh", [128, 2 * 1024], F32R)
        wqkl = sb("wqkl", [128, 2 * 1024], F32R)
        wvh = sb("wvh", [128, 8 * 256], F32R)
        wvl = sb("wvl", [128, 8 * 256], F32R)
        lnhp = [sb(f"l1lnh_{i}", [128, 512], F32R) for i in range(2)]
        lnlp = [sb(f"l1lnl_{i}", [128, 512], F32R) for i in range(2)]
        lnpool = Pool(list(zip(lnhp, lnlp)))
        vq = sb("vq", [128, 16 * 130], F32)
        qhi = sb("qhi", [128, T], F32R)
        qlo = sb("qlo", [128, T], F32R)
        khi = sb("khi", [128, T], F32R)
        klo = sb("klo", [128, T], F32R)
        ptile = sb("ptile", [128, 2 * 512], F32)
        q2s = sb("q2s", [128, 512], F32)
        otAh = sb("otAh", [64, 512], F32R)
        otAl = sb("otAl", [64, 512], F32R)
        otBh = sb("otBh", [64, 512], F32R)
        otBl = sb("otBl", [64, 512], F32R)
        wps = sb("wps", [128, 1024], F32R)
        wpld = sb("wpld", [64, 256], F32)
        bprj = sb("bprj", [128, 8], F32)
        mcols = sb("mcols", [128, 32], F32)
        k2cols = sb("k2cols", [128, 32], F32)

        p.op("syn", "dma_start", out=bprj[:, 0:8], in_=bo_c,
             inc=s_dsyn, inc16=True)
        bprj_rdy = w(s_dsyn)

        def ln512(c, j, extra=()):
            si, (lh, ll), g = lnpool.next()
            t0 = j * 512
            ls_i, lsrc, lg = lspool.next()
            p.op("syn", "dma_start", out=lsrc,
                 in_=ln_src_dram[c * 128:(c + 1) * 128, t0:t0 + 512],
                 waits=lg + [spill_rdy], inc=s_dsyn, inc16=True)
            sq_i, scr, sg = sqlpool.next()
            p.op("dve", "tensor_sub", scr, lsrc, meanb[:, t0:t0 + 512],
                 waits=list(extra) + g + sg + [w(s_dsyn)])
            p.op("dve", "tensor_mul", scr, scr, rstdb[:, t0:t0 + 512],
                 inc=s_dve)
            lspool.used(ls_i, s_dve)
            p.op("act", "copy", lh[:, :], scr, waits=[w(s_dve)], inc=s_act)
            p.op("dve", "tensor_sub", ll[:, :], scr, lh[:, :], waits=[w(s_act)],
                 inc=s_dve)
            sqlpool.used(sq_i, s_dve)
            return si, lh, ll, w(s_dve)

        for quarter in range(4):
            vcol0 = 2 * D + quarter * 256
            for c in range(NCH):
                slot = c % 2
                p.op("syn", "dma_start", out=wldA[:, slot * 256:slot * 256 + 256],
                     in_=wqkvo[c * 128:(c + 1) * 128, vcol0:vcol0 + 256],
                     waits=[w(s_dve)], inc=s_dsyn, inc16=True)
                p.op("dve", "tensor_copy", wvh[:, c * 256:(c + 1) * 256],
                     wldA[:, slot * 256:slot * 256 + 256],
                     waits=[w(s_dsyn), w(s_pe)])
                p.op("dve", "tensor_sub", wvl[:, c * 256:(c + 1) * 256],
                     wldA[:, slot * 256:slot * 256 + 256],
                     wvh[:, c * 256:(c + 1) * 256], inc=s_dve)
            wv_rdy = w(s_dve)
            p.op("syn", "dma_start", out=row(R_VB, 256),
                 in_=bqkv[:, vcol0:vcol0 + 256], waits=[w(s_pe)],
                 inc=s_dsyn, inc16=True)
            brow_rdy = w(s_dsyn)

            for hp in (2 * quarter, 2 * quarter + 1):
                hloc = hp % 2
                qc0 = hp * 128
                wldAv = wldA[:, :].rearrange("q (c o) -> q c o", c=NCH)
                p.op("syn", "dma_start", out=wldAv,
                     in_=wqkvo[:, qc0:qc0 + 128].rearrange(
                         "(c q) o -> q c o", q=128),
                     waits=[w(s_dve)], inc=s_dsyn, inc16=True)
                split_into(wldA[:, :], wqkh[:, 0:1024], wqkl[:, 0:1024],
                           waits=[w(s_dsyn), w(s_pe)])
                p.op("syn", "dma_start", out=wldAv,
                     in_=wqkvo[:, D + qc0:D + qc0 + 128].rearrange(
                         "(c q) o -> q c o", q=128),
                     waits=[w(s_dve)], inc=s_dsyn, inc16=True)
                wqk_rdy = split_into(wldA[:, :], wqkh[:, 1024:2048],
                                     wqkl[:, 1024:2048],
                                     waits=[w(s_dsyn), w(s_pe)])
                p.op("syn", "dma_start", out=row(R_QKB, 128),
                     in_=bqkv[:, qc0:qc0 + 128],
                     waits=[w(s_pe)], inc=s_dsyn, inc16=True)
                p.op("syn", "dma_start", out=row(R_QKB + 128, 128),
                     in_=bqkv[:, D + qc0:D + qc0 + 128], inc=s_dsyn,
                     inc16=True)
                p.wait("pe", w(s_dsyn), w(s_act))
                colize(0, row(R_QKB, 128))
                bias_rdy0 = colize(1, row(R_QKB + 128, 128))
                p.op("dve", "tensor_scalar", bcol[:, 0:1], bcol[:, 0:1],
                     -2.0, None, OP.mult, inc=s_dve)
                bias_rdy = w(s_dve)
                zv = vq[:, :].rearrange(
                    "q (i h c) -> q i h c", i=16, h=2)[:, :, :, 64:65]

                for j in range(4):
                    t0 = j * 512
                    wsv = [banks.start(b) for b in range(4)]
                    wsq = banks.start(4)
                    wsk = banks.start(5)
                    for c in range(NCH):
                        extra = [stats_rdy] if (j == 0 and c == 0) else []
                        si, lh, ll, lt = ln512(c, j, extra=extra)
                        for sub in range(4):
                            st_ = lh[:, sub * 128:(sub + 1) * 128]
                            sl_ = ll[:, sub * 128:(sub + 1) * 128]
                            p.op("pe", "matmul", pb[sub][0:128, 0:256], st_,
                                 wvh[:, c * 256:(c + 1) * 256], start=(c == 0),
                                 stop=False,
                                 waits=[lt] + ([wv_rdy] if c == 0 else []) +
                                       (wsv[sub] if c == 0 else []))
                            p.op("pe", "matmul", pb[sub][0:128, 0:256], sl_,
                                 wvh[:, c * 256:(c + 1) * 256], start=False,
                                 stop=False)
                            p.op("pe", "matmul", pb[sub][0:128, 0:256], st_,
                                 wvl[:, c * 256:(c + 1) * 256], start=False,
                                 stop=False, inc=(s_pe if sub == 3 else None))
                        qh_ = wqkh[:, c * 128:(c + 1) * 128]
                        ql_ = wqkl[:, c * 128:(c + 1) * 128]
                        kh_ = wqkh[:, 1024 + c * 128:1024 + (c + 1) * 128]
                        kl_ = wqkl[:, 1024 + c * 128:1024 + (c + 1) * 128]
                        p.op("pe", "matmul", pb[4][:, :], qh_, lh[:, :],
                             start=(c == 0), stop=False,
                             waits=([wqk_rdy] if (j == 0 and c == 0) else []) +
                                   (wsq if c == 0 else []))
                        p.op("pe", "matmul", pb[4][:, :], qh_, ll[:, :],
                             start=False, stop=False)
                        p.op("pe", "matmul", pb[4][:, :], ql_, lh[:, :],
                             start=False, stop=(c == NCH - 1))
                        p.op("pe", "matmul", pb[5][:, :], kh_, lh[:, :],
                             start=(c == 0), stop=False,
                             waits=(wsk if c == 0 else []))
                        p.op("pe", "matmul", pb[5][:, :], kh_, ll[:, :],
                             start=False, stop=False)
                        p.op("pe", "matmul", pb[5][:, :], kl_, lh[:, :],
                             start=False, stop=(c == NCH - 1),
                             inc=s_pe)
                        lnpool.used(si, s_pe)
                    for sub in range(4):
                        i = 4 * j + sub
                        p.op("pe", "matmul", pb[sub][0:128, 0:256], onescol[:, :],
                             row(R_VB, 256), start=False, stop=True,
                             waits=([brow_rdy] if i == 0 else []), inc=s_pe)
                        vdone = w(s_pe)
                        for hh in range(2):
                            p.op("act", "copy",
                                 vq[:, i * 130 + hh * 65:i * 130 + hh * 65 + 64],
                                 pb[sub][0:128, hloc * 128 + hh * 64:
                                         hloc * 128 + (hh + 1) * 64],
                                 waits=([vdone] if hh == 0 else []),
                                 inc=(s_act if hh == 1 else None))
                        banks.read_done(sub, s_act)
                    for (bank, bci, dsth, dstl, scl) in (
                            (4, 0, qhi[:, t0:t0 + 512], qlo[:, t0:t0 + 512],
                             -2.0),
                            (5, 1, khi[:, t0:t0 + 512], klo[:, t0:t0 + 512],
                             1.0)):
                        qk_done = w(s_pe)
                        sq_i, scr, sg = sqlpool.next()
                        if scl == -2.0:
                            p.op("act", "activation", scr, pb[bank][:, :],
                                 AF.Identity, bias=bcol[:, bci:bci + 1],
                                 scale=cpool[:, C_NEG2:C_NEG2 + 1],
                                 waits=[qk_done, bias_rdy, w(s_dve)] + sg,
                                 inc=s_act)
                        else:
                            p.op("act", "activation", scr, pb[bank][:, :],
                                 AF.Identity, bias=bcol[:, bci:bci + 1],
                                 waits=[qk_done, bias_rdy, w(s_dve)] + sg,
                                 inc=s_act)
                        banks.read_done(bank, s_act)
                        p.op("act", "copy", dsth, scr, inc=s_act)
                        p.op("dve", "tensor_sub", dstl, scr, dsth,
                             waits=[w(s_act)], inc=s_dve)
                        sqlpool.used(sq_i, s_dve)
                    ksq_i, ksq, ksg = sqlpool.next()
                    p.op("dve", "tensor_add", ksq, khi[:, t0:t0 + 512],
                         klo[:, t0:t0 + 512], waits=[w(s_dve)] + ksg)
                    p.op("dve", "tensor_mul", ksq, ksq, ksq, inc=s_dve)
                    for hh, msk in ((0, onesAc), (1, onesBc)):
                        wb6 = banks.start(6)
                        p.op("pe", "matmul", pb[6][0:1, :], msk[:, :], ksq,
                             start=True, stop=True, waits=[w(s_dve)] + wb6,
                             inc=s_pe)
                        p.op("dve", "tensor_copy", row(R_K2, 512),
                             pb[6][0:1, :], waits=[w(s_pe)], inc=s_dve)
                        banks.read_done(6, s_dve)
                        p.op("act", "activation", row(R_K2 + 512, 512),
                             row(R_K2, 512), AF.Sqrt, waits=[w(s_dve)],
                             inc=s_act)
                        p.wait("pe", w(s_act))
                        for sub in range(4):
                            i = 4 * j + sub
                            mc = 2 * i + hh
                            wb7 = banks.start(7)
                            p.op("pe", "matmul", pb[7][:, 0:1],
                                 row(R_K2 + sub * 128, 128),
                                 cpool[0:1, C_ONE:C_ONE + 1],
                                 start=True, stop=False, waits=wb7)
                            p.op("pe", "matmul", pb[7][:, 1:2],
                                 row(R_K2 + 512 + sub * 128, 128),
                                 cpool[0:1, C_ONE:C_ONE + 1],
                                 start=False, stop=True, inc=s_pe)
                            p.op("dve", "tensor_copy",
                                 k2cols[:, mc:mc + 1], pb[7][:, 0:1],
                                 waits=[w(s_pe)])
                            p.op("dve", "tensor_copy",
                                 mcols[:, mc:mc + 1], pb[7][:, 1:2],
                                 inc=s_dve)
                            banks.read_done(7, s_dve)
                    sqlpool.used(ksq_i, s_pe)
                    for sub in range(4):
                        i = 4 * j + sub
                        for hh in range(2):
                            mc = mcols[:, 2 * i + hh:2 * i + hh + 1]
                            vsl = vq[:, i * 130 + hh * 65:i * 130 + hh * 65 + 64]
                            p.op("dve", "tensor_scalar", vsl, vsl, mc, None,
                                 OP.mult, waits=[w(s_act), w(s_dve)])
                            p.op("dve", "tensor_copy",
                                 vq[:, i * 130 + hh * 65 + 64:
                                    i * 130 + hh * 65 + 65], mc, inc=s_dve)
                vq_rdy = w(s_dve)

                for j in range(4):
                    t0 = j * 512
                    qsq_i, qsq, qsg = sqlpool.next()
                    p.op("dve", "tensor_add", qsq, qhi[:, t0:t0 + 512],
                         qlo[:, t0:t0 + 512], waits=[vq_rdy] + qsg)
                    p.op("dve", "tensor_mul", qsq, qsq, qsq, inc=s_dve)
                    for half in range(2):
                        hrow0 = 64 * half
                        obank = 2 + half
                        msk = onesAc if half == 0 else onesBc
                        wb6 = banks.start(6)
                        p.op("pe", "matmul", pb[6][0:1, :], msk[:, :], qsq,
                             start=True, stop=True, waits=[w(s_dve)] + wb6,
                             inc=s_pe)
                        p.op("dve", "tensor_scalar", row(R_RINV, 512),
                             pb[6][0:1, :], 0.25, None, OP.mult,
                             waits=[w(s_pe)], inc=s_dve)
                        banks.read_done(6, s_dve)
                        wb6 = banks.start(6)
                        p.op("pe", "matmul", pb[6][:, :], onescol[:, :],
                             row(R_RINV, 512), start=True, stop=True,
                             waits=[w(s_dve)] + wb6, inc=s_pe)
                        p.op("dve", "tensor_copy", q2s[:, :], pb[6][:, :],
                             waits=[w(s_pe)], inc=s_dve)
                        banks.read_done(6, s_dve)
                        q2_rdy = w(s_dve)
                        wso = banks.start(obank)
                        for i in range(16):
                            k0 = i * 128
                            sbk = i % 2
                            wss = banks.start(sbk)
                            p.op("pe", "matmul", pb[sbk][:, :],
                                 khi[hrow0:hrow0 + 64, k0:k0 + 128],
                                 qhi[hrow0:hrow0 + 64, t0:t0 + 512],
                                 start=True, stop=False, waits=wss)
                            p.op("pe", "matmul", pb[sbk][:, :],
                                 khi[hrow0:hrow0 + 64, k0:k0 + 128],
                                 qlo[hrow0:hrow0 + 64, t0:t0 + 512],
                                 start=False, stop=False)
                            p.op("pe", "matmul", pb[sbk][:, :],
                                 klo[hrow0:hrow0 + 64, k0:k0 + 128],
                                 qhi[hrow0:hrow0 + 64, t0:t0 + 512],
                                 start=False, stop=True, inc=s_pe)
                            s_done = w(s_pe)
                            pt = ptile[:, (i % 2) * 512:(i % 2) * 512 + 512]
                            k2c_i = 2 * i + half
                            p.op("dve", "scalar_tensor_tensor", pt,
                                 pb[sbk][:, :],
                                 k2cols[:, k2c_i:k2c_i + 1], q2s[:, :],
                                 OP.add, OP.add,
                                 waits=[s_done, q2_rdy], inc=s_dve)
                            banks.read_done(sbk, s_dve)
                            p.op("act", "activation", pt, pt, AF.Relu,
                                 waits=[w(s_dve)])
                            p.op("act", "activation", pt, pt, AF.Sqrt)
                            p.op("act", "activation", pt, pt, AF.Square,
                                 bias=cpool[:, C_EPS2:C_EPS2 + 1], inc=s_act)
                            p.op("dve", "reciprocal", pt, pt,
                                 waits=[w(s_act)], inc=s_dve)
                            vstat = vq[:, i * 130 + half * 65:
                                       i * 130 + half * 65 + 65]
                            p.op("pe", "matmul", pb[obank][0:65, :], vstat, pt,
                                 start=(i == 0), stop=(i == 15),
                                 waits=[w(s_dve)] + (wso if i == 0 else []),
                                 inc=(s_pe if i == 15 else None))
                    pv_done = w(s_pe)
                    otA = ptile[0:64, 0:512]
                    otB = ptile[0:64, 512:1024]
                    for half in range(2):
                        obank = 2 + half
                        ot = otA if half == 0 else otB
                        p.op("dve", "reciprocal", row(R_RINV, 512),
                             pb[obank][64:65, :], waits=[pv_done], inc=s_dve)
                        wb6 = banks.start(6)
                        p.op("pe", "matmul", pb[6][0:64, :], onescol[0:1, 0:64],
                             row(R_RINV, 512), start=True, stop=True,
                             waits=[w(s_dve)] + wb6, inc=s_pe)
                        si_r, scr_r, sg_r = sqlpool.next()
                        p.op("dve", "tensor_copy", scr_r[0:64, :],
                             pb[6][0:64, :], waits=[w(s_pe)] + sg_r, inc=s_dve)
                        banks.read_done(6, s_dve)
                        p.op("dve", "tensor_mul", ot, pb[obank][0:64, :],
                             scr_r[0:64, :], inc=s_dve)
                        sqlpool.used(si_r, s_dve)
                        banks.read_done(obank, s_dve)
                    o_rdy = w(s_dve)
                    if DEBUG_O:
                        p.op("gps", "dma_start",
                             out=dbg_o[hp * 128:hp * 128 + 64, t0:t0 + 512],
                             in_=otA, waits=[o_rdy], inc=s_dgps, inc16=True)
                        p.op("gps", "dma_start",
                             out=dbg_o[hp * 128 + 64:hp * 128 + 128,
                                       t0:t0 + 512],
                             in_=otB, inc=s_dgps, inc16=True)
                    p.op("act", "copy", otAh[:, :], otA, waits=[o_rdy, w(s_pe)])
                    p.op("act", "copy", otBh[:, :], otB, inc=s_act)
                    p.op("dve", "tensor_sub", otAl[:, :], otA, otAh[:, :],
                         waits=[w(s_act)])
                    p.op("dve", "tensor_sub", otBl[:, :], otB, otBh[:, :],
                         inc=s_dve)
                    osp = w(s_dve)
                    for oc in range(NCH):
                        obk = 4 + (oc % 2)
                        slot = oc % 2
                        sA = wps[0:64, slot * 512:slot * 512 + 512]
                        p.op("syn", "dma_start",
                             out=wpld[:, :].rearrange("q (h o) -> q h o", h=2),
                             in_=wqkvo[hp * 128:hp * 128 + 128,
                                       3 * D + oc * 128:3 * D + (oc + 1) * 128
                                       ].rearrange("(h q) o -> q h o", h=2),
                             waits=[w(s_dve)], inc=s_dsyn, inc16=True)
                        p.op("dve", "tensor_copy", sA[:, 0:256],
                             wpld[:, :], waits=[w(s_dsyn), w(s_pe)])
                        p.op("dve", "tensor_sub", sA[:, 256:512],
                             wpld[:, :], sA[:, 0:256],
                             inc=s_dve)
                        wp_rdy = w(s_dve)
                        wpAh_ = sA[:, 0:128]
                        wpBh_ = sA[:, 128:256]
                        wpAl_ = sA[:, 256:384]
                        wpBl_ = sA[:, 384:512]
                        wso2 = banks.start(obk)
                        ws0 = [osp] if oc == 0 else []
                        p.op("pe", "matmul", pb[obk][:, :], wpAh_, otAh[:, :],
                             start=True, stop=False, waits=ws0 + wso2 + [wp_rdy])
                        p.op("pe", "matmul", pb[obk][:, :], wpAh_, otAl[:, :],
                             start=False, stop=False)
                        p.op("pe", "matmul", pb[obk][:, :], wpAl_, otAh[:, :],
                             start=False, stop=False)
                        p.op("pe", "matmul", pb[obk][:, :], wpBh_, otBh[:, :],
                             start=False, stop=False)
                        p.op("pe", "matmul", pb[obk][:, :], wpBh_, otBl[:, :],
                             start=False, stop=False)
                        p.op("pe", "matmul", pb[obk][:, :], wpBl_, otBh[:, :],
                             start=False, stop=True, inc=s_pe)
                        prj_done = w(s_pe)
                        if hp == 0:
                            p.op("dve", "scalar_tensor_tensor",
                                 xch(oc, t0, 512), pb[obk][:, :],
                                 bprj[:, oc:oc + 1], xch(oc, t0, 512),
                                 OP.add, OP.add,
                                 waits=[prj_done] + ([bprj_rdy] if (j == 0 and oc == 0) else []),
                                 inc=s_dve)
                        else:
                            p.op("dve", "tensor_add", xch(oc, t0, 512),
                                 xch(oc, t0, 512), pb[obk][:, :],
                                 waits=[prj_done], inc=s_dve)
                        banks.read_done(obk, s_dve)

        phase_barrier()
        p.default_stack = None
        pstk.close()

    def l1_ffn():
        pstk = contextlib.ExitStack()
        p.default_stack = pstk
        sident = sb("sident", [128, 128], F32)
        p.op("syn", "dma_start", out=sident[:, :], in_=ident, inc=s_dsyn,
             inc16=True)
        ident_rdy = w(s_dsyn)
        colb = sb("colb", [128, 4], F32)
        rowsf = sb("rowsf", [1, 512], F32)

        def part_max_bcast(src_t, width):
            nch_ = (width + 127) // 128
            for ch in range(nch_):
                w_ = min(128, width - ch * 128)
                wb7 = banks.start(7)
                p.op("pe", "transpose", pb[7][0:w_, 0:128],
                     src_t[:, ch * 128:ch * 128 + w_], sident[:, :],
                     waits=[w(s_dve), ident_rdy] + wb7, inc=s_pe)
                p.op("dve", "reduce_max", colb[0:w_, ch:ch + 1],
                     pb[7][0:w_, 0:128], AX, waits=[w(s_pe)], inc=s_dve)
                banks.read_done(7, s_dve)
            off = 0
            for ch in range(nch_):
                w_ = min(128, width - ch * 128)
                wb7 = banks.start(7)
                p.op("pe", "transpose", pb[7][0:1, 0:128],
                     colb[:, ch:ch + 1], sident[:, :],
                     waits=[w(s_dve)] + wb7, inc=s_pe)
                p.op("dve", "tensor_copy", rowsf[0:1, off:off + w_],
                     pb[7][0:1, 0:w_], waits=[w(s_pe)], inc=s_dve)
                banks.read_done(7, s_dve)
                off += w_
            wb6 = banks.start(6)
            p.op("pe", "matmul", pb[6][:, 0:width], onescol[:, :],
                 rowsf[0:1, 0:width], start=True, stop=True,
                 waits=[w(s_dve)] + wb6, inc=s_pe)
            return pb[6], w(s_pe)
        FT = 256
        FW = FT + 2
        NT = T // FT
        ln_stats()
        stats_rdy = w(s_dve)
        xbv3 = xb[:, :].rearrange("q (c t) -> q c t", c=NCH)
        xsv3 = xs_hbm.rearrange("(c q) t -> q c t", q=128)
        p.op("gps", "dma_start", out=xsv3, in_=xbv3, waits=[w(s_dve), w(s_pe)],
             inc=s_dgps, inc16=True)
        spill_rdy = w(s_dgps)

        qstk = contextlib.ExitStack()
        p.default_stack = qstk
        wqld = sb("wqld", [128, 2 * 512], F32)
        wqpool = Pool([wqld[:, 0:512], wqld[:, 512:1024]])
        qscr = sb("qscr", [128, 2 * 512], F32)
        tqb = sb("tqb", [128, 2 * 512], BF16)
        tqpool = Pool([tqb[:, 0:512], tqb[:, 512:1024]])
        acc = sb("acc1", [128, 2], F32)
        p.default_stack = pstk

        def quantize_weights(src_dram, dst_dram, nrow, ncol, tau_col_i):
            ntr, ntc = nrow // 128, ncol // 512
            p.op("dve", "memset", acc[:, 0:1], 0.0)
            for r_ in range(ntr):
                for c_ in range(ntc):
                    si, slot, sg = wqpool.next()
                    p.op("syn", "dma_start", out=slot,
                         in_=src_dram[r_ * 128:(r_ + 1) * 128,
                                      c_ * 512:(c_ + 1) * 512],
                         waits=sg, inc=s_dsyn, inc16=True)
                    p.op("dve", "tensor_reduce", acc[:, 1:2], slot, AX, OP.add,
                         apply_absolute_value=True, waits=[w(s_dsyn)])
                    p.op("dve", "tensor_add", acc[:, 0:1], acc[:, 0:1],
                         acc[:, 1:2], inc=s_dve)
                    wqpool.used(si, s_dve)
            wb7 = banks.start(7)
            p.op("pe", "matmul", pb[7][0:1, 0:1], cpool[:, C_ONE:C_ONE + 1],
                 acc[:, 0:1],
                 start=True, stop=True, waits=[w(s_dve)] + wb7, inc=s_pe)
            p.op("dve", "tensor_scalar", cell(0), pb[7][0:1, 0:1],
                 1.0 / (nrow * ncol), 1e-5, OP.mult, OP.max,
                 waits=[w(s_pe)], inc=s_dve)
            banks.read_done(7, s_dve)
            p.op("dve", "tensor_scalar", cell(1), cell(0), 0.5, None, OP.mult)
            p.op("dve", "tensor_scalar", cell(2), cell(1), -1.0, None, OP.mult,
                 inc=s_dve)
            p.wait("pe", w(s_dve))
            colize(tau_col_i, cell(1))
            colize(tau_col_i + 1, cell(2))
            mrdy = colize(tau_col_i + 2, cell(0))
            for r_ in range(ntr):
                for c_ in range(ntc):
                    si, slot, sg = wqpool.next()
                    p.op("syn", "dma_start", out=slot,
                         in_=src_dram[r_ * 128:(r_ + 1) * 128,
                                      c_ * 512:(c_ + 1) * 512],
                         waits=sg, inc=s_dsyn, inc16=True)
                    p.op("dve", "tensor_scalar", qscr[:, 0:512], slot,
                         bcol[:, tau_col_i:tau_col_i + 1], None, OP.is_ge,
                         waits=[w(s_dsyn), mrdy])
                    p.op("dve", "tensor_scalar", qscr[:, 512:1024], slot,
                         bcol[:, tau_col_i + 1:tau_col_i + 2], None,
                         OP.is_le, inc=s_dve)
                    wqpool.used(si, s_dve)
                    ti, tslot, tg = tqpool.next()
                    p.op("dve", "tensor_sub", tslot, qscr[:, 0:512],
                         qscr[:, 512:1024], waits=tg, inc=s_dve)
                    p.op("gps", "dma_start",
                         out=dst_dram[r_ * 128:(r_ + 1) * 128,
                                      c_ * 512:(c_ + 1) * 512],
                         in_=tslot, waits=[w(s_dve)], inc=s_dgps, inc16=True)
                    tqpool.used(ti, s_dgps)

        quantize_weights(f1, t1_hbm, D, HID, 4)
        m1rdy = w(s_dve)
        quantize_weights(f2, t2_hbm, HID, D, 0)
        tq_done = w(s_dgps)
        phase_barrier()
        qstk.close()

        lnb = sb("lnb", [128, NCH * FW], F32)
        qx = sb("qx", [128, NCH * FW], BF16)
        y1h = sb("y1h", [128, 32 * FW], F32)
        h2b = sb("h2b", [128, 32 * FT], F32)
        qh = sb("qh", [128, 32 * FT], BF16)
        wtb = sb("wtb", [128, 2 * 2048], BF16)
        wtpool = Pool([wtb[:, 0:2048], wtb[:, 2048:4096]])
        sclb = sb("sclb", [128, 4 * FW], F32)
        s_b = sclb[:, 0:FW]
        osc_b = sclb[:, FW:2 * FW]
        s2_b = sclb[:, 2 * FW:2 * FW + FT]
        osc2_b = sclb[:, 3 * FW:3 * FW + FT]
        amx = sb("amx", [128, FW], F32)
        sconvw = sb("sconvw", [128, 96], F32)
        sconvb = sb("sconvb", [128, 32], F32)
        p.op("syn", "dma_start", out=sconvw[:, :], in_=convw, inc=s_dsyn,
             inc16=True)
        p.op("syn", "dma_start", out=sconvb[:, :], in_=convb, inc=s_dsyn,
             inc16=True)
        conv_rdy = w(s_dsyn)
        lnsrc = sb("lnsrc", [128, 2 * 512], F32)
        lspool = Pool([lnsrc[:, 0:512], lnsrc[:, 512:1024]])

        def cast_round_clip(dst_bf16, srcf, sb_scale, width, eng="dve"):
            i32 = cr_i32[:, 0:width]
            pr = cr_p[:, 0:width]
            rr = cr_r[:, 0:width]
            p.op(eng, "tensor_mul", pr, srcf, sb_scale)
            p.op(eng, "tensor_copy", i32, pr)
            p.op(eng, "tensor_copy", rr, i32)
            p.op(eng, "tensor_sub", pr, pr, rr)
            p.op(eng, "scalar_tensor_tensor", rr, pr, 0.5, rr, OP.is_ge,
                 OP.add)
            p.op(eng, "tensor_scalar", pr, pr, -0.5, -1.0, OP.is_le, OP.mult)
            p.op(eng, "tensor_add", rr, rr, pr)
            p.op(eng, "tensor_scalar", rr, rr, -128.0, 127.0, OP.max, OP.min)
            p.op(eng, "tensor_copy", dst_bf16, rr, inc=s_dve)

        cr_i32 = sb("cr_i32", [128, FW], I32)
        cr_p = sb("cr_p", [128, FW], F32)
        cr_r = sb("cr_r", [128, FW], F32)

        for t in range(NT):
            t0 = t * FT
            lo_pad = 1 if t == 0 else 0
            hi_pad = 1 if t == NT - 1 else 0
            src0 = t0 - 1 + lo_pad
            wsrc = FW - lo_pad - hi_pad
            for c in range(NCH):
                ls_i, lsrc, lg = lspool.next()
                p.op("syn", "dma_start", out=lsrc[:, 0:wsrc],
                     in_=xs_hbm[c * 128:(c + 1) * 128, src0:src0 + wsrc],
                     waits=lg + [spill_rdy], inc=s_dsyn, inc16=True)
                dst = lnb[:, c * FW:(c + 1) * FW]
                if lo_pad:
                    p.op("dve", "memset", dst[:, 0:1], 0.0)
                if hi_pad:
                    p.op("dve", "memset", dst[:, FW - 1:FW], 0.0)
                p.op("dve", "tensor_sub", dst[:, lo_pad:lo_pad + wsrc],
                     lsrc[:, 0:wsrc],
                     meanb[:, src0:src0 + wsrc],
                     waits=[w(s_dsyn)] +
                           ([stats_rdy] if (t == 0 and c == 0) else []) +
                           ([w(s_pe)] if c == 0 else []))
                p.op("dve", "tensor_mul", dst[:, lo_pad:lo_pad + wsrc],
                     dst[:, lo_pad:lo_pad + wsrc],
                     rstdb[:, src0:src0 + wsrc], inc=s_dve)
                lspool.used(ls_i, s_dve)
            if lo_pad or hi_pad:
                pass
            p.op("act", "activation", amx[:, :], lnb[:, 0:FW], AF.Abs,
                 waits=[w(s_dve)], inc=s_act)
            for c in range(1, NCH):
                scr_ab = cr_p[:, 0:FW] if c % 2 else cr_r[:, 0:FW]
                p.op("act", "activation", scr_ab,
                     lnb[:, c * FW:(c + 1) * FW], AF.Abs, inc=s_act)
                p.op("dve", "tensor_max", amx[:, :], amx[:, :], scr_ab,
                     waits=[w(s_act)], inc=s_dve)
            pmx, pmx_rdy = part_max_bcast(amx, FW)
            p.op("dve", "tensor_single_scalar", s_b, pmx[:, 0:FW], 1e-5,
                 OP.max, waits=[pmx_rdy])
            p.op("dve", "tensor_scalar", osc_b, s_b,
                 bcol[:, 6:7], None, OP.mult, waits=[m1rdy])
            p.op("dve", "tensor_scalar", osc_b, osc_b, 1.0 / 127.0, None,
                 OP.mult)
            p.op("dve", "reciprocal", s_b, s_b)
            p.op("dve", "tensor_scalar", s_b, s_b, 127.0, None, OP.mult,
                 inc=s_dve)
            banks.read_done(6, s_dve)
            for c in range(NCH):
                cast_round_clip(qx[:, c * FW:(c + 1) * FW],
                                lnb[:, c * FW:(c + 1) * FW], s_b, FW)
            qx_rdy = w(s_dve)

            for h in range(32):
                si, slot, sg = wtpool.next()
                p.wait("syn", *sg)
                p.op("syn", "dma_start",
                     out=slot[:, 0:1024].rearrange("q (c o) -> q c o", c=NCH),
                     in_=t1_hbm[:, h * 128:(h + 1) * 128].rearrange(
                         "(c q) o -> q c o", q=128),
                     waits=([tq_done] if (t == 0 and h == 0) else []),
                     inc=s_dsyn, inc16=True)
                srdy = w(s_dsyn)
                bk_ = h % 2
                wsb_ = banks.start(bk_)
                for c in range(NCH):
                    p.op("pe", "matmul", pb[bk_][:, 0:FW],
                         slot[:, c * 128:(c + 1) * 128],
                         qx[:, c * FW:(c + 1) * FW],
                         start=(c == 0), stop=(c == NCH - 1),
                         waits=([srdy, qx_rdy] if c == 0 else []) +
                               (wsb_ if c == 0 else []),
                         inc=(s_pe if c == NCH - 1 else None))
                wtpool.used(si, s_pe)
                mmd = w(s_pe)
                p.op("dve", "tensor_mul", y1h[:, h * FW:(h + 1) * FW],
                     pb[bk_][:, 0:FW], osc_b, waits=[mmd], inc=s_dve)
                banks.read_done(bk_, s_dve)
            for h in range(32):
                yh = y1h[:, h * FW:(h + 1) * FW]
                sq_i, scr, sg = sqpool.next()
                p.op("dve", "tensor_scalar", scr, yh[:, 0:FT],
                     sconvw[:, h * 3:h * 3 + 1], None, OP.mult,
                     waits=sg + ([conv_rdy] if (t == 0 and h == 0) else []))
                p.op("dve", "scalar_tensor_tensor", scr, yh[:, 1:1 + FT],
                     sconvw[:, h * 3 + 1:h * 3 + 2], scr, OP.mult, OP.add)
                p.op("dve", "scalar_tensor_tensor", scr, yh[:, 2:2 + FT],
                     sconvw[:, h * 3 + 2:h * 3 + 3], scr, OP.mult, OP.add,
                     inc=s_dve)
                p.op("act", "activation", h2b[:, h * FT:(h + 1) * FT], scr,
                     AF.Silu, bias=sconvb[:, h:h + 1], waits=[w(s_dve)],
                     inc=s_act)
                sqpool.used(sq_i, s_act)
            h2_rdy = w(s_act)
            p.op("act", "activation", amx[:, 0:FT], h2b[:, 0:FT], AF.Abs,
                 waits=[h2_rdy], inc=s_act)
            for h in range(1, 32):
                scr_ab = cr_p[:, 0:FT] if h % 2 else cr_r[:, 0:FT]
                p.op("act", "activation", scr_ab,
                     h2b[:, h * FT:(h + 1) * FT], AF.Abs, inc=s_act)
                p.op("dve", "tensor_max", amx[:, 0:FT], amx[:, 0:FT], scr_ab,
                     waits=[w(s_act)], inc=s_dve)
            pmx2, pmx2_rdy = part_max_bcast(amx[:, 0:FT], FT)
            p.op("dve", "tensor_single_scalar", s2_b, pmx2[:, 0:FT], 1e-5,
                 OP.max, waits=[pmx2_rdy])
            p.op("dve", "tensor_scalar", osc2_b, s2_b, bcol[:, 2:3], None,
                 OP.mult)
            p.op("dve", "tensor_scalar", osc2_b, osc2_b, 1.0 / 127.0, None,
                 OP.mult)
            p.op("dve", "reciprocal", s2_b, s2_b)
            p.op("dve", "tensor_scalar", s2_b, s2_b, 127.0, None, OP.mult,
                 inc=s_dve)
            banks.read_done(6, s_dve)
            for h in range(32):
                cast_round_clip(qh[:, h * FT:(h + 1) * FT],
                                h2b[:, h * FT:(h + 1) * FT], s2_b, FT)
            qh_rdy = w(s_dve)
            if DEBUG_O and t == 7:
                p.op("gps", "dma_start", out=dbg_c[:, 0:32 * FT],
                     in_=qh[:, :], waits=[qh_rdy], inc=s_dgps, inc16=True)
            for o in range(NCH):
                bk_ = 2 + o % 2
                wsb_ = banks.start(bk_)
                first = [qh_rdy] if o == 0 else []
                for half in range(2):
                    si, slot, sg = wtpool.next()
                    p.wait("syn", *sg)
                    p.op("syn", "dma_start",
                         out=slot[:, 0:2048].rearrange(
                             "q (c o) -> q c o", c=16),
                         in_=t2_hbm[half * 2048:(half + 1) * 2048,
                                    o * 128:(o + 1) * 128].rearrange(
                             "(c q) o -> q c o", q=128),
                         inc=s_dsyn, inc16=True)
                    srdy = w(s_dsyn)
                    if DEBUG_O and t == 7 and o == 0:
                        p.op("gps", "dma_start",
                             out=dbg_c[:, 8192 + half * 2048:
                                       8192 + half * 2048 + 2048],
                             in_=slot[:, 0:2048], waits=[srdy],
                             inc=s_dgps, inc16=True)
                        p.wait("pe", w(s_dgps))
                    for c in range(16):
                        cc = half * 16 + c
                        p.op("pe", "matmul", pb[bk_][:, 0:FT],
                             slot[:, c * 128:(c + 1) * 128],
                             qh[:, cc * FT:(cc + 1) * FT],
                             start=(cc == 0), stop=(cc == 31),
                             waits=([srdy] if c == 0 else []) + first +
                                   (wsb_ if cc == 0 else []),
                             inc=(s_pe if c == 15 else None))
                        first = []
                    wtpool.used(si, s_pe)
                mmd = w(s_pe)
                sq_i, scr, sg = sqpool.next()
                p.op("dve", "tensor_mul", scr, pb[bk_][:, 0:FT], osc2_b,
                     waits=[mmd] + sg, inc=s_dve)
                banks.read_done(bk_, s_dve)
                if DEBUG_O:
                    p.op("gps", "dma_start",
                         out=dbg_o[o * 128:(o + 1) * 128, t0:t0 + FT],
                         in_=scr, waits=[w(s_dve)], inc=s_dgps, inc16=True)
                p.op("dve", "tensor_add", xch(o, t0, FT), xch(o, t0, FT),
                     scr, inc=s_dve)
                sqpool.used(sq_i, s_dve)

        phase_barrier()
        p.default_stack = None
        pstk.close()

    if "l0a" in phases:
        l0_attention()
    if "l0f" in phases:
        l0_ffn()
    if "l1a" in phases:
        l1_attention()
    if "l1f" in phases:
        l1_ffn()

    yv = y_out.rearrange("(c q) t -> q c t", q=128)
    p.wait("gps", w(s_dve), w(s_pe), w(s_act))
    p.op("gps", "dma_start", out=yv, in_=xbv, inc=s_dgps, inc16=True)
    p.wait("gps", w(s_dgps))

    with contextlib.ExitStack() as stack:
        p.emit(stack)
    return nc


def _host_prep(inputs):
    f32 = np.float32
    x = np.ascontiguousarray(inputs["L0_attn_x"])
    com = {}
    com["w_attn"] = np.ascontiguousarray(inputs["L0_attn_W_attn"])
    com["b_attn"] = np.ascontiguousarray(inputs["L0_attn_b_attn"].reshape(1, -1))
    com["w_proj"] = np.ascontiguousarray(inputs["L0_attn_W_proj"])
    com["b_proj"] = np.ascontiguousarray(inputs["L0_attn_b_proj"].reshape(8, 128).T)
    com["w1t"] = np.ascontiguousarray(inputs["L0_ffn_w1_w"].T)
    com["w2t"] = np.ascontiguousarray(inputs["L0_ffn_w2_w"].T)
    com["h1t"] = np.ascontiguousarray(inputs["L0_ffn_h1_w"].T)
    com["h2t"] = np.ascontiguousarray(inputs["L0_ffn_h2_w"].T)
    com["w3t"] = np.ascontiguousarray(inputs["L0_ffn_w3_w"].T)
    com["s1c"] = np.ascontiguousarray(inputs["L0_ffn_w1_s"].reshape(32, 128).T)
    com["s2c"] = np.ascontiguousarray(inputs["L0_ffn_w2_s"].reshape(32, 128).T)
    com["h1sc"] = np.ascontiguousarray(inputs["L0_ffn_h1_s"].reshape(32, 128).T)
    com["h2sc"] = np.ascontiguousarray(inputs["L0_ffn_h2_s"].reshape(32, 128).T)
    com["w3sc"] = np.ascontiguousarray(inputs["L0_ffn_w3_s"].reshape(8, 128).T)
    com["wqkvo"] = np.ascontiguousarray(np.concatenate(
        [inputs["L1_attn_Wq"].T, inputs["L1_attn_Wk"].T,
         inputs["L1_attn_Wv"].T, inputs["L1_attn_Wo"].T], axis=1))
    com["bqkv"] = np.ascontiguousarray(np.concatenate(
        [inputs["L1_attn_bq"], inputs["L1_attn_bk"],
         inputs["L1_attn_bv"]]).reshape(1, -1))
    com["bo_c"] = np.ascontiguousarray(inputs["L1_attn_bo"].reshape(8, 128).T)
    com["f1"] = np.ascontiguousarray(inputs["L1_ffn_w1"].T)
    com["f2"] = np.ascontiguousarray(inputs["L1_ffn_w2"].T)
    com["convw"] = np.ascontiguousarray(
        inputs["L1_ffn_conv_w"][:, 0, :].reshape(32, 128, 3).transpose(1, 0, 2)
        .reshape(128, 96))
    com["convb"] = np.ascontiguousarray(inputs["L1_ffn_conv_b"].reshape(32, 128).T)
    com["ident"] = np.eye(128, dtype=f32)
    permm = np.zeros((128, 128), f32)
    for i in range(0, 128, 2):
        permm[i, i + 1] = 1.0
        permm[i + 1, i] = 1.0
    com["perm"] = permm
    pidx = np.arange(128)
    freq = (pidx % 64) // 2
    fb = np.zeros((128, 5), f32)
    for k in range(5):
        fb[:, k] = (freq >> k) & 1
    com["fbits"] = fb
    com["ssign"] = np.where(pidx % 2 == 0, -1.0, 1.0).astype(f32).reshape(-1, 1)

    in_maps = []
    for c in range(8):
        m = dict(com)
        m["x0"] = np.ascontiguousarray(x[c].T)
        in_maps.append(m)
    return in_maps


def kernel(**inputs):
    in_maps = _host_prep(inputs)
    nc = build_nc()
    res = run_bass_kernel_spmd(nc, in_maps, core_ids=list(range(8)))
    out = np.empty((8, T, D), np.float32)
    for c in range(8):
        out[c] = res.results[c]["y"].T
    return out


# version 02
# speedup vs baseline: 1.0044x; 1.0044x over previous
import math
import contextlib
import numpy as np
import jax
import jax.numpy as jnp
import concourse.bass as bass
import concourse.mybir as mybir
from concourse.bass_utils import run_bass_kernel_spmd

F32 = mybir.dt.float32
F32R = mybir.dt.float32r
BF16 = mybir.dt.bfloat16
I32 = mybir.dt.int32
OP = mybir.AluOpType
AF = mybir.ActivationFunctionType
AX = mybir.AxisListType.X

T = 2048
D = 1024
HID = 4096
NHEAD = 16
DH = 64
NCH = D // 128
LN_EPS = 1e-5
ALL_PHASES = ("l0a", "l0f", "l1a", "l1f")
DEBUG_O = False


class Sem:
    __slots__ = ("name", "h", "v")

    def __init__(self, name):
        self.name = name
        self.h = None
        self.v = 0


def w(sem):
    return (sem, sem.v)


class Prog:
    ENGINES = ("pe", "act", "dve", "gps", "syn")

    def __init__(self, nc):
        self.nc = nc
        self.items = {e: [] for e in self.ENGINES}
        self.sems = []
        self.sbuf_bytes = 0
        self.hz = {e: {"cycle": 0, "writes": []} for e in self.ENGINES}
        self.default_stack = None
        self.sb_names = set()

    def sem(self, name):
        s = Sem(name)
        self.sems.append(s)
        return s

    def sb(self, name, shape, dt):
        if name in self.sb_names:
            i = 2
            while f"{name}_{i}" in self.sb_names:
                i += 1
            name = f"{name}_{i}"
        self.sb_names.add(name)
        self.sbuf_bytes += shape[-1] * mybir.dt.size(dt)
        side = "right" if dt == F32R else "left"
        if self.default_stack is not None:
            return self.default_stack.enter_context(
                self.nc.sbuf_tensor(name, shape, dt, side=side))
        return self.nc.alloc_sbuf_tensor(name, shape, dt, side=side)

    HZ_SLACK = {"dve": 96, "act": 640, "gps": 256}
    HZ_DEPTH = {"dve": 24, "act": 420, "gps": 64}

    @staticmethod
    def _ap_info(x):
        try:
            name = x.tensor.name
            span = 1
            for step, count in list(x.ap)[1:]:
                span += abs(int(step)) * (int(count) - 1)
            off = int(x.offset)
            return (name, off, off + span, int(x.free_size()))
        except Exception:
            return None

    def _hazard(self, eng, meth, a, kw):
        if eng not in self.HZ_SLACK:
            return False
        st = self.hz[eng]
        if meth == "dma_start":
            return False
        aps = [x for x in a if self._ap_info(x) is not None]
        for k in ("bias", "scale", "accum_out"):
            v = kw.get(k)
            if v is not None and self._ap_info(v) is not None:
                aps.append(v)
        infos = [self._ap_info(x) for x in aps]
        outs, ins = infos[:1], infos[1:]
        if kw.get("accum_out") is not None and infos:
            outs = [infos[0], infos[-1]]
            ins = infos[1:-1]
        slack = self.HZ_SLACK[eng]
        depth = self.HZ_DEPTH[eng]
        now = st["cycle"]
        st["writes"] = [wr for wr in st["writes"] if now - wr[3] < slack]
        need = False
        for (n, lo, hi, sz) in ins:
            for (wn, wlo, whi, wend, wsz) in st["writes"]:
                if wn != n or lo >= whi or wlo >= hi:
                    continue
                if now - wend >= depth:
                    continue
                if wsz >= depth and sz >= wsz and lo == wlo and hi == whi:
                    continue
                need = True
                break
            if need:
                break
        if need:
            st["writes"] = []
        dur = max((i[3] for i in infos), default=1)
        st["cycle"] = now + dur
        for o in outs:
            if o is not None:
                st["writes"].append((o[0], o[1], o[2], st["cycle"], o[3]))
        return need

    def op(self, eng, meth, *a, waits=(), inc=None, inc16=False, **kw):
        waits = tuple((s, v) for (s, v) in waits if v > 0)
        inc_n = 16 if inc16 else 1
        need_drain = self._hazard(eng, meth, a, kw)

        def f(e, _m=meth, _a=a, _kw=kw, _w=waits, _i=inc, _n=inc_n,
              _d=need_drain):
            if _d:
                e.drain()
            for (s, v) in _w:
                e.wait_ge(s.h, v)
            ins = getattr(e, _m)(*_a, **_kw)
            if _i is not None:
                ins.then_inc(_i.h, _n)

        self.items[eng].append(f)
        if inc is not None:
            inc.v += inc_n

    def wait(self, eng, *waits):
        waits = tuple((s, v) for (s, v) in waits if v > 0)

        def f(e, _w=waits):
            for (s, v) in _w:
                e.wait_ge(s.h, v)
        self.items[eng].append(f)

    def emit(self, stack):
        nc = self.nc
        for s in self.sems:
            s.h = stack.enter_context(nc.semaphore(s.name))
        block = stack.enter_context(nc.Block())
        eng_map = {"pe": block.tensor, "act": block.scalar, "dve": block.vector,
                   "gps": block.gpsimd, "syn": block.sync}
        for name in self.ENGINES:
            fns = self.items[name]
            if not fns:
                continue

            def body(eng, _fns=fns):
                for f in _fns:
                    f(eng)
            eng_map[name](body)


class Banks:
    def __init__(self, pbs):
        self.pb = pbs
        self.gate = [None] * len(pbs)

    def start(self, i):
        g = self.gate[i]
        return [g] if g else []

    def read_done(self, i, sem):
        self.gate[i] = (sem, sem.v)


class Pool:
    def __init__(self, aps):
        self.aps = aps
        self.gate = [None] * len(aps)
        self.i = 0

    def next(self):
        i = self.i
        self.i = (i + 1) % len(self.aps)
        g = self.gate[i]
        return i, self.aps[i], ([g] if g else [])

    def used(self, i, sem):
        self.gate[i] = (sem, sem.v)


def hi_lo_const(v):
    hi = np.float32(v)
    lo = np.float32(v - float(hi))
    return float(hi), float(lo)


def build_nc(phases=ALL_PHASES):
    nc = bass.Bass("TRN2", target_bir_lowering=False,
                   detect_race_conditions=False)

    def din(name, shape, dt=F32):
        return nc.dram_tensor(name, shape, dt, kind="ExternalInput").ap()

    x0 = din("x0", [D, T])
    w_attn = din("w_attn", [D, 3 * D])
    b_attn = din("b_attn", [1, 3 * D])
    w_proj = din("w_proj", [D, D])
    b_proj = din("b_proj", [128, NCH])
    w1t = din("w1t", [D, HID], F32R)
    w2t = din("w2t", [D, HID], F32R)
    h1t = din("h1t", [HID, HID], F32R)
    h2t = din("h2t", [HID, HID], F32R)
    w3t = din("w3t", [HID, D], F32R)
    s1c = din("s1c", [128, 32])
    s2c = din("s2c", [128, 32])
    h1sc = din("h1sc", [128, 32])
    h2sc = din("h2sc", [128, 32])
    w3sc = din("w3sc", [128, 8])
    wqkvo = din("wqkvo", [D, 4 * D])
    bqkv = din("bqkv", [1, 3 * D])
    bo_c = din("bo_c", [128, NCH])
    f1 = din("f1", [D, HID])
    f2 = din("f2", [HID, D])
    convw = din("convw", [128, 96])
    convb = din("convb", [128, 32])
    perm = din("perm", [128, 128])
    ident = din("ident", [128, 128])
    ropet = din("ropet", [128, 32])

    y_out = nc.dram_tensor("y", [D, T], F32, kind="ExternalOutput").ap()
    t1_hbm = nc.dram_tensor("t1_hbm", [D, HID], BF16).ap()
    xs_hbm = nc.dram_tensor("xs_hbm", [D, T], F32).ap()
    dbg_o = nc.dram_tensor("dbg_o", [D, T], F32, kind="ExternalOutput").ap() if DEBUG_O else None
    dbg_b = nc.dram_tensor("dbg_b", [128, 8192], F32, kind="ExternalOutput").ap() if DEBUG_O else None
    dbg_c = nc.dram_tensor("dbg_c", [128, 16384], BF16, kind="ExternalOutput").ap() if DEBUG_O else None
    t2_hbm = nc.dram_tensor("t2_hbm", [HID, D], BF16).ap()

    p = Prog(nc)
    sb = p.sb

    s_pe = p.sem("s_pe")
    s_act = p.sem("s_act")
    s_dve = p.sem("s_dve")
    s_gps = p.sem("s_gps")
    s_dsyn = p.sem("s_dsyn")
    s_dgps = p.sem("s_dgps")

    xb = sb("xb", [128, NCH * T], F32)
    meanb = sb("meanb", [128, T], F32)
    rstdb = sb("rstdb", [128, T], F32)
    ones14 = sb("ones14", [128, 128], F32)
    onescol = sb("onescol", [1, 128], F32)
    onesAc = sb("onesAc", [128, 1], F32)
    onesBc = sb("onesBc", [128, 1], F32)
    cpool = sb("cpool", [128, 8], F32)
    C_EIGHTH, C_EPS, C_ONE, C_NEG2, C_EPS2 = 0, 1, 2, 3, 4
    cells = sb("cells", [1, 32], F32)
    bcol = sb("bcol", [128, 8], F32)
    sqs = sb("sqs", [128, 2 * 256], F32)

    pb = [nc.alloc_psum_tensor(f"pb{i}", [128, 512], F32) for i in range(8)]
    banks = Banks(pb)


    def row(a, n):
        return rows[0:1, a:a + n]

    def cell(i, n=1):
        return cells[0:1, i:i + n]

    p.op("dve", "memset", ones14[:, :], 1.0 / 1024.0)
    p.op("dve", "memset", onescol[:, :], 1.0)
    p.op("dve", "memset", onesAc[0:64, :], 1.0)
    p.op("dve", "memset", onesAc[64:128, :], 0.0)
    p.op("dve", "memset", onesBc[0:64, :], 0.0)
    p.op("dve", "memset", onesBc[64:128, :], 1.0)
    p.op("dve", "memset", cpool[:, C_EIGHTH:C_EIGHTH + 1], 0.125)
    p.op("dve", "memset", cpool[:, C_EPS:C_EPS + 1], LN_EPS)
    p.op("dve", "memset", bcol[:, :], 0.0)
    p.op("dve", "memset", cpool[:, C_ONE:C_ONE + 1], 1.0)
    p.op("dve", "memset", cpool[:, C_NEG2:C_NEG2 + 1], -2.0)
    p.op("dve", "memset", cpool[:, C_EPS2:C_EPS2 + 1], 1e-8, inc=s_dve)

    x0v = x0.rearrange("(c q) t -> q c t", q=128)
    xbv = xb[:, :].rearrange("q (c t) -> q c t", c=NCH)
    p.op("syn", "dma_start", out=xbv, in_=x0v, inc=s_dsyn, inc16=True)
    init_dma = w(s_dsyn)


    def xch(c, t0=0, tw=T):
        return xb[:, c * T + t0:c * T + t0 + tw]

    def phase_barrier():
        snaps = [w(s_pe), w(s_act), w(s_dve), w(s_gps), w(s_dsyn), w(s_dgps)]
        for e in ("pe", "act", "dve", "gps", "syn"):
            p.wait(e, *snaps)

    sqpool = Pool([sqs[:, 0:256], sqs[:, 256:512]])

    def ln_stats(input_rdy=()):
        for j in range(8):
            t0 = j * 256
            wm = banks.start(4)
            ws = banks.start(5)
            for c in range(NCH):
                si, sl, g = sqpool.next()
                p.op("dve", "tensor_mul", sl, xch(c, t0, 256), xch(c, t0, 256),
                     waits=g + list(input_rdy), inc=s_dve)
                p.op("pe", "matmul", pb[4][:, 0:256], ones14[:, :],
                     xch(c, t0, 256),
                     start=(c == 0), stop=(c == NCH - 1),
                     waits=(wm if c == 0 else []) + list(input_rdy))
                p.op("pe", "matmul", pb[5][:, 0:256], ones14[:, :], sl,
                     start=(c == 0), stop=(c == NCH - 1),
                     waits=[w(s_dve)] + (ws if c == 0 else []),
                     inc=s_pe)
                sqpool.used(si, s_pe)
            mmd = w(s_pe)
            p.op("dve", "tensor_copy", meanb[:, t0:t0 + 256], pb[4][:, 0:256],
                 waits=[mmd], inc=s_dve)
            banks.read_done(4, s_dve)
            si1, sl1, g1 = sqpool.next()
            p.op("dve", "tensor_mul", sl1, meanb[:, t0:t0 + 256],
                 meanb[:, t0:t0 + 256], waits=g1)
            p.op("dve", "tensor_sub", sl1, pb[5][:, 0:256], sl1, inc=s_dve)
            banks.read_done(5, s_dve)
            si2, sl2, g2 = sqpool.next()
            p.op("act", "activation", sl2, sl1, AF.Sqrt,
                 bias=cpool[:, C_EPS:C_EPS + 1], waits=[w(s_dve)] + g2,
                 inc=s_act)
            sqpool.used(si1, s_act)
            p.op("dve", "reciprocal", rstdb[:, t0:t0 + 256], sl2,
                 waits=[w(s_act)], inc=s_dve)
            sqpool.used(si2, s_dve)

    def ln_tile_into(dsth, dstl, scratch, c, t0, tw, extra_waits=()):
        p.op("dve", "tensor_sub", scratch, xch(c, t0, tw), meanb[:, t0:t0 + tw],
             waits=list(extra_waits))
        p.op("dve", "tensor_mul", scratch, scratch, rstdb[:, t0:t0 + tw])
        p.op("dve", "tensor_copy", dsth, scratch)
        p.op("dve", "tensor_sub", dstl, scratch, dsth, inc=s_dve)
        return w(s_dve)

    def split_into(src, dsth, dstl, waits=()):
        p.op("dve", "tensor_copy", dsth, src, waits=list(waits))
        p.op("dve", "tensor_sub", dstl, src, dsth, inc=s_dve)
        return w(s_dve)

    def colize(dst_col_idx, src_row_ap, bank=7):
        wb = banks.start(bank)
        if src_row_ap.shape[-1] == 1:
            lhsT, rhs = onescol[:, :], src_row_ap
        else:
            lhsT, rhs = src_row_ap, cpool[0:1, C_ONE:C_ONE + 1]
        p.op("pe", "matmul", pb[bank][:, dst_col_idx:dst_col_idx + 1],
             lhsT, rhs, start=True, stop=True,
             waits=wb, inc=s_pe)
        p.op("dve", "tensor_copy", bcol[:, dst_col_idx:dst_col_idx + 1],
             pb[bank][:, dst_col_idx:dst_col_idx + 1], waits=[w(s_pe)],
             inc=s_dve)
        banks.read_done(bank, s_dve)
        return w(s_dve)

    def l0_attention():
        pstk = contextlib.ExitStack()
        p.default_stack = pstk
        ln_src_dram = x0
        rows = sb("rows", [1, 1280], F32)
        R_BIAS, R_QKB, R_RINV = 0, 256, 512

        def row(a, n):
            return rows[0:1, a:a + n]
        lnsrc = sb("lnsrc", [128, 2 * 512], F32)
        lspool = Pool([lnsrc[:, 0:512], lnsrc[:, 512:1024]])
        sql = sb("sql", [128, 2 * 512], F32)
        sqlpool = Pool([sql[:, 0:512], sql[:, 512:1024]])
        lcos = sb("lcos", [128, 16], F32)
        lsin = sb("lsin", [128, 16], F32)
        sperm = sb("sperm", [128, 128], F32)
        p.op("syn", "dma_start", out=sperm[:, :], in_=perm, inc=s_dsyn,
             inc16=True)
        p.op("syn", "dma_start", out=lcos[:, :], in_=ropet[:, 0:16],
             inc=s_dsyn, inc16=True)
        p.op("syn", "dma_start", out=lsin[:, :], in_=ropet[:, 16:32],
             inc=s_dsyn, inc16=True)
        ropecs = sb("ropecs", [128, 2], F32)

        ln_stats(input_rdy=[init_dma])
        stats_rdy = w(s_dve)

        wldA = sb("wldA", [128, 1024], F32)
        wqkh = sb("wqkh", [128, 2 * 1024], F32R)
        wqkl = sb("wqkl", [128, 2 * 1024], F32R)
        wvh = sb("wvh", [128, 8 * 256], F32R)
        wvl = sb("wvl", [128, 8 * 256], F32R)
        lnhp = [sb(f"lnh4_{i}", [128, 512], F32R) for i in range(2)]
        lnlp = [sb(f"lnl4_{i}", [128, 512], F32R) for i in range(2)]
        lnpool = Pool(list(zip(lnhp, lnlp)))
        vq = sb("vq", [128, 16 * 130], F32)
        qth = sb("qth", [128, 512], F32R)
        qtl = sb("qtl", [128, 512], F32R)
        khi = sb("khi", [128, T], F32R)
        klo = sb("klo", [128, T], F32R)
        ptile = sb("ptile", [128, 2 * 512], F32)
        otAh = sb("otAh", [64, 512], F32R)
        otAl = sb("otAl", [64, 512], F32R)
        otBh = sb("otBh", [64, 512], F32R)
        otBl = sb("otBl", [64, 512], F32R)
        wps = sb("wps", [128, 1024], F32R)
        wpld = sb("wpld", [64, 256], F32)
        bprj = sb("bprj", [128, 8], F32)

        p.op("syn", "dma_start", out=bprj[:, 0:8], in_=b_proj,
             inc=s_dsyn, inc16=True)
        bprj_rdy = w(s_dsyn)
        rope_tab_rdy = w(s_dsyn)

        def ln512(c, j, extra=()):
            si, (lh, ll), g = lnpool.next()
            t0 = j * 512
            ls_i, lsrc, lg = lspool.next()
            p.op("syn", "dma_start", out=lsrc,
                 in_=ln_src_dram[c * 128:(c + 1) * 128, t0:t0 + 512],
                 waits=lg, inc=s_dsyn, inc16=True)
            sq_i, scr, sg = sqlpool.next()
            p.op("dve", "tensor_sub", scr, lsrc, meanb[:, t0:t0 + 512],
                 waits=list(extra) + g + sg + [w(s_dsyn)])
            p.op("dve", "tensor_mul", scr, scr, rstdb[:, t0:t0 + 512],
                 inc=s_dve)
            lspool.used(ls_i, s_dve)
            p.op("act", "copy", lh[:, :], scr, waits=[w(s_dve)], inc=s_act)
            p.op("dve", "tensor_sub", ll[:, :], scr, lh[:, :], waits=[w(s_act)],
                 inc=s_dve)
            sqlpool.used(sq_i, s_dve)
            return si, lh, ll, w(s_dve)

        first_ln = [True]

        for quarter in range(4):
            vcol0 = 2 * D + quarter * 256
            for c in range(NCH):
                slot = c % 2
                p.op("syn", "dma_start", out=wldA[:, slot * 256:slot * 256 + 256],
                     in_=w_attn[c * 128:(c + 1) * 128, vcol0:vcol0 + 256],
                     waits=[w(s_dve)], inc=s_dsyn, inc16=True)
                p.op("dve", "tensor_copy", wvh[:, c * 256:(c + 1) * 256],
                     wldA[:, slot * 256:slot * 256 + 256],
                     waits=[w(s_dsyn), w(s_pe)])
                p.op("dve", "tensor_sub", wvl[:, c * 256:(c + 1) * 256],
                     wldA[:, slot * 256:slot * 256 + 256],
                     wvh[:, c * 256:(c + 1) * 256], inc=s_dve)
            wv_rdy = w(s_dve)
            p.op("syn", "dma_start", out=row(R_BIAS, 256),
                 in_=b_attn[:, vcol0:vcol0 + 256], waits=[w(s_pe)],
                 inc=s_dsyn, inc16=True)
            brow_rdy = w(s_dsyn)

            for hp in (2 * quarter, 2 * quarter + 1):
                hloc = hp % 2
                qc0 = hp * 128
                wldAv = wldA[:, :].rearrange("q (c o) -> q c o", c=NCH)
                p.op("syn", "dma_start", out=wldAv,
                     in_=w_attn[:, qc0:qc0 + 128].rearrange(
                         "(c q) o -> q c o", q=128),
                     waits=[w(s_dve)], inc=s_dsyn, inc16=True)
                split_into(wldA[:, :], wqkh[:, 0:1024], wqkl[:, 0:1024],
                           waits=[w(s_dsyn), w(s_pe)])
                p.op("syn", "dma_start", out=wldAv,
                     in_=w_attn[:, D + qc0:D + qc0 + 128].rearrange(
                         "(c q) o -> q c o", q=128),
                     waits=[w(s_dve)], inc=s_dsyn, inc16=True)
                wqk_rdy = split_into(wldA[:, :], wqkh[:, 1024:2048],
                                     wqkl[:, 1024:2048],
                                     waits=[w(s_dsyn), w(s_pe)])
                p.op("syn", "dma_start", out=row(R_QKB, 128),
                     in_=b_attn[:, qc0:qc0 + 128],
                     waits=[w(s_pe)], inc=s_dsyn, inc16=True)
                p.op("syn", "dma_start", out=row(R_QKB + 128, 128),
                     in_=b_attn[:, D + qc0:D + qc0 + 128], inc=s_dsyn,
                     inc16=True)
                p.wait("pe", w(s_dsyn), w(s_act))
                colize(0, row(R_QKB, 128))
                bias_rdy = colize(1, row(R_QKB + 128, 128))
                zcols = vq[:, :].rearrange(
                    "q (i h c) -> q i h c", i=16, h=2)[:, :, :, 64:65]
                p.op("dve", "memset", zcols, 1.0, waits=[w(s_pe)], inc=s_dve)
                z_rdy = w(s_dve)
                p.op("dve", "memset", cells[0:1, 0:4], 0.0, inc=s_dve)
                hA, hB = 2 * hp, 2 * hp + 1
                p.op("dve", "tensor_copy", ropecs[0:64, 0:1],
                     lcos[0:64, hA:hA + 1], waits=[rope_tab_rdy, w(s_act)])
                p.op("dve", "tensor_copy", ropecs[64:128, 0:1],
                     lcos[64:128, hB:hB + 1])
                p.op("dve", "tensor_copy", ropecs[0:64, 1:2],
                     lsin[0:64, hA:hA + 1])
                p.op("dve", "tensor_copy", ropecs[64:128, 1:2],
                     lsin[64:128, hB:hB + 1], inc=s_dve)
                ropec_rdy = w(s_dve)

                for j in range(4):
                    t0 = j * 512
                    wsv = [banks.start(b) for b in range(4)]
                    wsq = banks.start(4)
                    wsk = banks.start(5)
                    for c in range(NCH):
                        extra = [stats_rdy] if first_ln[0] else []
                        first_ln[0] = False
                        si, lh, ll, lt = ln512(c, j, extra=extra)
                        for sub in range(4):
                            st_ = lh[:, sub * 128:(sub + 1) * 128]
                            sl_ = ll[:, sub * 128:(sub + 1) * 128]
                            p.op("pe", "matmul", pb[sub][0:128, 0:256], st_,
                                 wvh[:, c * 256:(c + 1) * 256], start=(c == 0),
                                 stop=False,
                                 waits=[lt] + ([wv_rdy] if c == 0 else []) +
                                       (wsv[sub] if c == 0 else []))
                            p.op("pe", "matmul", pb[sub][0:128, 0:256], sl_,
                                 wvh[:, c * 256:(c + 1) * 256], start=False,
                                 stop=False)
                            p.op("pe", "matmul", pb[sub][0:128, 0:256], st_,
                                 wvl[:, c * 256:(c + 1) * 256], start=False,
                                 stop=False)
                        qh_ = wqkh[:, c * 128:(c + 1) * 128]
                        ql_ = wqkl[:, c * 128:(c + 1) * 128]
                        kh_ = wqkh[:, 1024 + c * 128:1024 + (c + 1) * 128]
                        kl_ = wqkl[:, 1024 + c * 128:1024 + (c + 1) * 128]
                        p.op("pe", "matmul", pb[4][:, :], qh_, lh[:, :],
                             start=(c == 0), stop=False,
                             waits=([wqk_rdy] if (j == 0 and c == 0) else []) +
                                   (wsq if c == 0 else []))
                        p.op("pe", "matmul", pb[4][:, :], qh_, ll[:, :],
                             start=False, stop=False)
                        p.op("pe", "matmul", pb[4][:, :], ql_, lh[:, :],
                             start=False, stop=(c == NCH - 1))
                        p.op("pe", "matmul", pb[5][:, :], kh_, lh[:, :],
                             start=(c == 0), stop=False,
                             waits=(wsk if c == 0 else []))
                        p.op("pe", "matmul", pb[5][:, :], kh_, ll[:, :],
                             start=False, stop=False)
                        p.op("pe", "matmul", pb[5][:, :], kl_, lh[:, :],
                             start=False, stop=(c == NCH - 1),
                             inc=s_pe)
                        lnpool.used(si, s_pe)
                    for sub in range(4):
                        i = 4 * j + sub
                        p.op("pe", "matmul", pb[sub][0:128, 0:256], onescol[:, :],
                             row(R_BIAS, 256), start=False, stop=True,
                             waits=([brow_rdy] if i == 0 else []), inc=s_pe)
                        vdone = w(s_pe)
                        for hh in range(2):
                            p.op("act", "copy",
                                 vq[:, i * 130 + hh * 65:i * 130 + hh * 65 + 64],
                                 pb[sub][0:128, hloc * 128 + hh * 64:
                                         hloc * 128 + (hh + 1) * 64],
                                 waits=([vdone] if hh == 0 else []),
                                 inc=(s_act if hh == 1 else None))
                        banks.read_done(sub, s_act)
                    for (bank, bci, dsth, dstl) in (
                            (4, 0, qth[:, :], qtl[:, :]),
                            (5, 1, khi[:, t0:t0 + 512], klo[:, t0:t0 + 512])):
                        qk_done = w(s_pe)
                        sq_i, scr, sg = sqlpool.next()
                        p.op("act", "activation", scr, pb[bank][:, :],
                             AF.Identity, bias=bcol[:, bci:bci + 1],
                             waits=[qk_done, bias_rdy, w(s_dve)] + sg,
                             inc=s_act)
                        banks.read_done(bank, s_act)
                        wb7 = banks.start(7)
                        p.op("pe", "matmul", pb[7][:, :], sperm[:, :], scr,
                             start=True, stop=True, waits=[w(s_act)] + wb7,
                             inc=s_pe)
                        swp = w(s_pe)
                        sq_i2, scr2, sg2 = sqlpool.next()
                        p.op("dve", "tensor_scalar", scr2, pb[7][:, :],
                             ropecs[:, 1:2], None, OP.mult,
                             waits=[swp, ropec_rdy] + sg2, inc=s_dve)
                        banks.read_done(7, s_dve)
                        p.op("dve", "tensor_scalar", scr, scr,
                             ropecs[:, 0:1], None, OP.mult)
                        p.op("dve", "tensor_add", scr, scr, scr2, inc=s_dve)
                        sqlpool.used(sq_i2, s_dve)
                        p.op("act", "copy", dsth, scr,
                             waits=[w(s_dve), w(s_pe)], inc=s_act)
                        p.op("dve", "tensor_sub", dstl, scr, dsth,
                             waits=[w(s_act)], inc=s_dve)
                        sqlpool.used(sq_i, s_dve)
                    qk_split_rdy = w(s_dve)
                    if DEBUG_O and hp == 0:
                        p.op("gps", "dma_start", out=dbg_b[:, 0:2048],
                             in_=meanb[:, :], waits=[qk_split_rdy],
                             inc=s_dgps, inc16=True)
                        p.op("gps", "dma_start", out=dbg_b[:, 2048:4096],
                             in_=rstdb[:, :], inc=s_dgps, inc16=True)
                        p.op("gps", "dma_start",
                             out=dbg_b[:, 4096 + t0:4096 + t0 + 512],
                             in_=khi[:, t0:t0 + 512], inc=s_dgps, inc16=True)
                        p.op("gps", "dma_start",
                             out=dbg_b[:, 6144 + t0 // 4:6144 + t0 // 4 + 128],
                             in_=qth[:, 0:128], inc=s_dgps, inc16=True)
                        p.op("gps", "dma_start", out=dbg_b[:, 6656:6658],
                             in_=ropecs[:, :], inc=s_dgps, inc16=True)
                        p.op("gps", "dma_start", out=dbg_b[:, 6658:6674],
                             in_=lcos[:, :], inc=s_dgps, inc16=True)
                        p.op("gps", "dma_start", out=dbg_b[:, 6674:6690],
                             in_=lsin[:, :], inc=s_dgps, inc16=True)
                        p.op("gps", "dma_start", out=dbg_b[:, 6690:6698],
                             in_=bcol[:, :], inc=s_dgps, inc16=True)

                    for r, (bhi, blo, msk, acc) in enumerate(
                            ((qth[:, :], qtl[:, :], onesAc, 0),
                             (qth[:, :], qtl[:, :], onesBc, 1),
                             (khi[:, t0:t0 + 512], klo[:, t0:t0 + 512],
                              onesAc, 2),
                             (khi[:, t0:t0 + 512], klo[:, t0:t0 + 512],
                              onesBc, 3))):
                        msk = msk[:, :]
                        si, sl, g = sqlpool.next()
                        p.op("dve", "tensor_add", sl, bhi, blo,
                             waits=[qk_split_rdy] + g)
                        p.op("dve", "tensor_mul", sl, sl, sl, inc=s_dve)
                        wb6 = banks.start(6)
                        p.op("pe", "matmul", pb[6][0:1, :], msk, sl,
                             start=True, stop=True, waits=[w(s_dve)] + wb6,
                             inc=s_pe)
                        sqlpool.used(si, s_pe)
                        p.op("dve", "reduce_max", cell(16), pb[6][0:1, :], AX,
                             waits=[w(s_pe)], inc=s_dve)
                        banks.read_done(6, s_dve)
                        if r < 2:
                            p.op("dve", "tensor_copy", cell(acc), cell(16))
                        else:
                            p.op("dve", "tensor_max", cell(acc), cell(acc),
                                 cell(16))
                    for hh in range(2):
                        p.op("dve", "tensor_mul", cell(4 + hh), cell(hh),
                             cell(2 + hh))
                    p.op("dve", "tensor_copy", cell(8, 2), cell(4, 2),
                         inc=s_dve)
                    p.op("act", "activation", cell(6, 2), cell(8, 2), AF.Sqrt,
                         waits=[w(s_dve)], inc=s_act)
                    p.op("dve", "tensor_scalar", cell(10, 2), cell(6, 2),
                         -0.125, None, OP.mult, waits=[w(s_act)], inc=s_dve)
                    p.wait("pe", w(s_dve), w(s_act))
                    colize(2, cell(10))
                    smax_rdy = colize(3, cell(11))

                    ntk = 4 * j + 4
                    for half in range(2):
                        hrow0 = 64 * half
                        obank = 2 + half
                        wso = banks.start(obank)
                        for i in range(ntk):
                            k0 = i * 128
                            sbk = i % 2
                            wss = banks.start(sbk)
                            p.op("pe", "matmul", pb[sbk][:, :],
                                 khi[hrow0:hrow0 + 64, k0:k0 + 128],
                                 qth[hrow0:hrow0 + 64, :],
                                 start=True, stop=False,
                                 waits=[qk_split_rdy] + wss)
                            p.op("pe", "matmul", pb[sbk][:, :],
                                 khi[hrow0:hrow0 + 64, k0:k0 + 128],
                                 qtl[hrow0:hrow0 + 64, :],
                                 start=False, stop=False)
                            p.op("pe", "matmul", pb[sbk][:, :],
                                 klo[hrow0:hrow0 + 64, k0:k0 + 128],
                                 qth[hrow0:hrow0 + 64, :],
                                 start=False, stop=True, inc=s_pe)
                            s_done = w(s_pe)
                            pt = ptile[:, (i % 2) * 512:(i % 2) * 512 + 512]
                            p.op("act", "activation", pt, pb[sbk][:, :], AF.Exp,
                                 bias=bcol[:, 2 + half:3 + half],
                                 scale=cpool[:, C_EIGHTH:C_EIGHTH + 1],
                                 waits=[s_done, smax_rdy], inc=s_act)
                            banks.read_done(sbk, s_act)
                            pw = [w(s_act)]
                            if k0 >= t0:
                                p.op("gps", "affine_select", pt, pt,
                                     [[1, 512]], OP.is_ge, 0.0,
                                     base=t0 - k0, channel_multiplier=-1,
                                     waits=[w(s_act)], inc=s_gps)
                                pw = [w(s_gps)]
                            vstat = vq[:, i * 130 + half * 65:
                                       i * 130 + half * 65 + 65]
                            p.op("pe", "matmul", pb[obank][0:65, :], vstat, pt,
                                 start=(i == 0), stop=(i == ntk - 1),
                                 waits=pw + ([z_rdy] if i == 0 else []) +
                                       (wso if i == 0 else []),
                                 inc=(s_pe if i == ntk - 1 else None))
                    pv_done = w(s_pe)
                    otA = ptile[0:64, 0:512]
                    otB = ptile[0:64, 512:1024]
                    for half in range(2):
                        obank = 2 + half
                        ot = otA if half == 0 else otB
                        p.op("dve", "reciprocal", row(R_RINV, 512),
                             pb[obank][64:65, :],
                             waits=[pv_done] + ([w(s_gps)] if half == 0 else []),
                             inc=s_dve)
                        wb6 = banks.start(6)
                        p.op("pe", "matmul", pb[6][0:64, :], onescol[0:1, 0:64],
                             row(R_RINV, 512), start=True, stop=True,
                             waits=[w(s_dve)] + wb6, inc=s_pe)
                        si_r, scr_r, sg_r = sqlpool.next()
                        p.op("dve", "tensor_copy", scr_r[0:64, :],
                             pb[6][0:64, :], waits=[w(s_pe)] + sg_r, inc=s_dve)
                        banks.read_done(6, s_dve)
                        p.op("dve", "tensor_mul", ot, pb[obank][0:64, :],
                             scr_r[0:64, :], inc=s_dve)
                        sqlpool.used(si_r, s_dve)
                        banks.read_done(obank, s_dve)
                    o_rdy = w(s_dve)
                    if DEBUG_O:
                        p.op("gps", "dma_start",
                             out=dbg_o[hp * 128:hp * 128 + 64, t0:t0 + 512],
                             in_=otA, waits=[o_rdy], inc=s_dgps, inc16=True)
                        p.op("gps", "dma_start",
                             out=dbg_o[hp * 128 + 64:hp * 128 + 128, t0:t0 + 512],
                             in_=otB, inc=s_dgps, inc16=True)
                    p.op("act", "copy", otAh[:, :], otA, waits=[o_rdy, w(s_pe)])
                    p.op("act", "copy", otBh[:, :], otB, inc=s_act)
                    p.op("dve", "tensor_sub", otAl[:, :], otA, otAh[:, :],
                         waits=[w(s_act)])
                    p.op("dve", "tensor_sub", otBl[:, :], otB, otBh[:, :],
                         inc=s_dve)
                    osp = w(s_dve)
                    for oc in range(NCH):
                        obk = 4 + (oc % 2)
                        slot = oc % 2
                        sA = wps[0:64, slot * 512:slot * 512 + 512]
                        p.op("syn", "dma_start",
                             out=wpld[:, :].rearrange("q (h o) -> q h o", h=2),
                             in_=w_proj[hp * 128:hp * 128 + 128,
                                        oc * 128:(oc + 1) * 128].rearrange(
                                 "(h q) o -> q h o", h=2),
                             waits=[w(s_dve)], inc=s_dsyn, inc16=True)
                        p.op("dve", "tensor_copy", sA[:, 0:256],
                             wpld[:, :], waits=[w(s_dsyn), w(s_pe)])
                        p.op("dve", "tensor_sub", sA[:, 256:512],
                             wpld[:, :], sA[:, 0:256],
                             inc=s_dve)
                        wp_rdy = w(s_dve)
                        wpAh_ = sA[:, 0:128]
                        wpBh_ = sA[:, 128:256]
                        wpAl_ = sA[:, 256:384]
                        wpBl_ = sA[:, 384:512]
                        wso2 = banks.start(obk)
                        ws0 = [osp] if oc == 0 else []
                        p.op("pe", "matmul", pb[obk][:, :], wpAh_, otAh[:, :],
                             start=True, stop=False, waits=ws0 + wso2 + [wp_rdy])
                        p.op("pe", "matmul", pb[obk][:, :], wpAh_, otAl[:, :],
                             start=False, stop=False)
                        p.op("pe", "matmul", pb[obk][:, :], wpAl_, otAh[:, :],
                             start=False, stop=False)
                        p.op("pe", "matmul", pb[obk][:, :], wpBh_, otBh[:, :],
                             start=False, stop=False)
                        p.op("pe", "matmul", pb[obk][:, :], wpBh_, otBl[:, :],
                             start=False, stop=False)
                        p.op("pe", "matmul", pb[obk][:, :], wpBl_, otBh[:, :],
                             start=False, stop=True, inc=s_pe)
                        prj_done = w(s_pe)
                        if hp == 0:
                            p.op("dve", "scalar_tensor_tensor",
                                 xch(oc, t0, 512), pb[obk][:, :],
                                 bprj[:, oc:oc + 1], xch(oc, t0, 512),
                                 OP.add, OP.add,
                                 waits=[prj_done] + ([bprj_rdy] if (j == 0 and oc == 0) else []),
                                 inc=s_dve)
                        else:
                            p.op("dve", "tensor_add", xch(oc, t0, 512),
                                 xch(oc, t0, 512), pb[obk][:, :],
                                 waits=[prj_done], inc=s_dve)
                        banks.read_done(obk, s_dve)

        phase_barrier()
        p.default_stack = None
        pstk.close()

    def l0_ffn():
        pstk = contextlib.ExitStack()
        p.default_stack = pstk
        FT = 256
        NT = T // FT
        ln_stats()
        stats_rdy = w(s_dve)

        Ah = sb("Ah", [128, 32 * FT], F32R)
        Al = sb("Al", [128, 32 * FT], F32R)
        Br = sb("Br", [128, 32 * FT], F32)
        ghi_t = sb("ghi", [128, NCH * FT], F32R)
        glo_t = sb("glo", [128, NCH * FT], F32R)
        ghi = ghi_t[:, :]
        glo = glo_t[:, :]
        wstrip = sb("wstrip", [128, 3 * 1024], F32R)
        wpool = Pool([wstrip[:, 0:1024], wstrip[:, 1024:2048],
                      wstrip[:, 2048:3072]])
        scales = sb("scales", [128, 4 * 32 + 8], F32)
        sc_s1, sc_s2, sc_h1, sc_h2, sc_w3 = 0, 32, 64, 96, 128

        p.op("syn", "dma_start", out=scales[:, 0:32], in_=s1c, inc=s_dsyn,
             inc16=True)
        p.op("syn", "dma_start", out=scales[:, 32:64], in_=s2c, inc=s_dsyn,
             inc16=True)
        p.op("syn", "dma_start", out=scales[:, 64:96], in_=h1sc, inc=s_dsyn,
             inc16=True)
        p.op("syn", "dma_start", out=scales[:, 96:128], in_=h2sc, inc=s_dsyn,
             inc16=True)
        p.op("syn", "dma_start", out=scales[:, 128:136], in_=w3sc, inc=s_dsyn,
             inc16=True)
        scales_rdy = w(s_dsyn)

        def scol(base, o):
            return scales[:, base + o:base + o + 1]

        def stream_strip(dram, ocols, c0, ncw, slot_ap):
            p.op("syn", "dma_start",
                 out=slot_ap.rearrange("q (c o) -> q c o", c=ncw),
                 in_=dram[c0 * 128:(c0 + ncw) * 128,
                          ocols:ocols + 128].rearrange(
                     "(c q) o -> q c o", q=128),
                 inc=s_dsyn, inc16=True)
            return w(s_dsyn)

        for t in range(NT):
            t0 = t * FT
            gw = [w(s_pe), stats_rdy] if True else []
            for c in range(NCH):
                sq_i, scr, sg = sqpool.next()
                scr = scr[:, 0:FT]
                p.op("dve", "tensor_sub", scr, xch(c, t0, FT),
                     meanb[:, t0:t0 + FT], waits=gw + sg)
                gw = []
                p.op("dve", "tensor_mul", scr, scr, rstdb[:, t0:t0 + FT],
                     inc=s_dve)
                p.op("act", "copy", ghi[:, c * FT:(c + 1) * FT], scr,
                     waits=[w(s_dve)], inc=s_act)
                p.op("dve", "tensor_sub", glo[:, c * FT:(c + 1) * FT], scr,
                     ghi[:, c * FT:(c + 1) * FT], waits=[w(s_act)], inc=s_dve)
                sqpool.used(sq_i, s_dve)
            g_rdy = w(s_dve)

            for o in range(32):
                s1i, w1s, sg1 = wpool.next()
                p.wait("syn", *sg1)
                r1 = stream_strip(w1t, o * 128, 0, 8, w1s)
                s2i, w2s, sg2 = wpool.next()
                p.wait("syn", *sg2)
                r2 = stream_strip(w2t, o * 128, 0, 8, w2s)
                ba = o % 2
                bb = 2 + o % 2
                wsa = banks.start(ba)
                wsb = banks.start(bb)
                for c in range(NCH):
                    gh = ghi[:, c * FT:(c + 1) * FT]
                    gl = glo[:, c * FT:(c + 1) * FT]
                    w1c = w1s[:, c * 128:(c + 1) * 128]
                    w2c = w2s[:, c * 128:(c + 1) * 128]
                    p.op("pe", "matmul", pb[ba][:, 0:FT], w1c, gh,
                         start=(c == 0), stop=False,
                         waits=([g_rdy, r2] if c == 0 else []) +
                               (wsa if c == 0 else []))
                    p.op("pe", "matmul", pb[ba][:, 0:FT], w1c, gl,
                         start=False, stop=(c == NCH - 1))
                    p.op("pe", "matmul", pb[bb][:, 0:FT], w2c, gh,
                         start=(c == 0), stop=False,
                         waits=(wsb if c == 0 else []))
                    p.op("pe", "matmul", pb[bb][:, 0:FT], w2c, gl,
                         start=False, stop=(c == NCH - 1),
                         inc=s_pe)
                wpool.used(s1i, s_pe)
                wpool.used(s2i, s_pe)
                mmd = w(s_pe)
                sq_i, scr, sg = sqpool.next()
                scr = scr[:, 0:FT]
                p.op("act", "activation", scr, pb[ba][:, 0:FT], AF.Silu,
                     scale=scol(sc_s1, o),
                     waits=[mmd, scales_rdy] + sg, inc=s_act)
                banks.read_done(ba, s_act)
                sq_i2, scr2, sg2 = sqpool.next()
                scr2 = scr2[:, 0:FT]
                p.op("dve", "scalar_tensor_tensor", scr2, pb[bb][:, 0:FT],
                     scol(sc_s2, o), scr, OP.mult, OP.mult,
                     waits=[w(s_act)] + sg2, inc=s_dve)
                banks.read_done(bb, s_dve)
                p.op("act", "copy", Ah[:, o * FT:(o + 1) * FT], scr2,
                     waits=[w(s_dve), w(s_pe)], inc=s_act)
                p.op("dve", "tensor_sub", Al[:, o * FT:(o + 1) * FT], scr2,
                     Ah[:, o * FT:(o + 1) * FT], waits=[w(s_act)], inc=s_dve)
                sqpool.used(sq_i, s_act)
                sqpool.used(sq_i2, s_dve)
            a_rdy = w(s_dve)

            for stage, (hdram, hsc) in enumerate(((h1t, sc_h1), (h2t, sc_h2))):
                for o in range(32):
                    bk_ = 4 + o % 2
                    wsk = banks.start(bk_)
                    first = [a_rdy] if o == 0 else []
                    for quarter_ in range(4):
                        si, slot, sg = wpool.next()
                        p.wait("syn", *sg)
                        rs = stream_strip(hdram, o * 128, quarter_ * 8, 8, slot)
                        for c in range(8):
                            cc = quarter_ * 8 + c
                            hc = slot[:, c * 128:(c + 1) * 128]
                            p.op("pe", "matmul", pb[bk_][:, 0:FT], hc,
                                 Ah[:, cc * FT:(cc + 1) * FT],
                                 start=(cc == 0), stop=False,
                                 waits=([rs] if c == 0 else []) + first +
                                       (wsk if cc == 0 else []))
                            first = []
                            p.op("pe", "matmul", pb[bk_][:, 0:FT], hc,
                                 Al[:, cc * FT:(cc + 1) * FT],
                                 start=False, stop=(cc == 31),
                                 inc=s_pe)
                        wpool.used(si, s_pe)
                    mmd = w(s_pe)
                    p.op("act", "activation", Br[:, o * FT:(o + 1) * FT],
                         pb[bk_][:, 0:FT], AF.Gelu, scale=scol(hsc, o),
                         waits=[mmd, w(s_dve)], inc=s_act)
                    banks.read_done(bk_, s_act)
                br_rdy = w(s_act)
                for o in range(32):
                    p.op("act", "copy", Ah[:, o * FT:(o + 1) * FT],
                         Br[:, o * FT:(o + 1) * FT],
                         waits=([br_rdy, w(s_pe)] if o == 0 else []),
                         inc=s_act)
                    p.op("dve", "tensor_sub", Al[:, o * FT:(o + 1) * FT],
                         Br[:, o * FT:(o + 1) * FT],
                         Ah[:, o * FT:(o + 1) * FT], waits=[w(s_act)],
                         inc=s_dve)
                a_rdy = w(s_dve)

            for o in range(NCH):
                bk_ = 6 + o % 2
                wsk = banks.start(bk_)
                first = [a_rdy] if o == 0 else []
                for quarter_ in range(4):
                    si, slot, sg = wpool.next()
                    p.wait("syn", *sg)
                    rs = stream_strip(w3t, o * 128, quarter_ * 8, 8, slot)
                    for c in range(8):
                        cc = quarter_ * 8 + c
                        hc = slot[:, c * 128:(c + 1) * 128]
                        p.op("pe", "matmul", pb[bk_][:, 0:FT], hc,
                             Ah[:, cc * FT:(cc + 1) * FT],
                             start=(cc == 0), stop=False,
                             waits=([rs] if c == 0 else []) + first +
                                   (wsk if cc == 0 else []))
                        first = []
                        p.op("pe", "matmul", pb[bk_][:, 0:FT], hc,
                             Al[:, cc * FT:(cc + 1) * FT],
                             start=False, stop=(cc == 31),
                             inc=s_pe)
                    wpool.used(si, s_pe)
                mmd = w(s_pe)
                p.op("dve", "scalar_tensor_tensor", xch(o, t0, FT),
                     pb[bk_][:, 0:FT], scol(sc_w3, o), xch(o, t0, FT),
                     OP.mult, OP.add, waits=[mmd], inc=s_dve)
                banks.read_done(bk_, s_dve)

        phase_barrier()
        p.default_stack = None
        pstk.close()

    def l1_attention():
        pstk = contextlib.ExitStack()
        p.default_stack = pstk
        ln_src_dram = xs_hbm
        ln_stats()
        stats_rdy = w(s_dve)
        xbv2 = xb[:, :].rearrange("q (c t) -> q c t", c=NCH)
        xsv = xs_hbm.rearrange("(c q) t -> q c t", q=128)
        p.op("gps", "dma_start", out=xsv, in_=xbv2, waits=[w(s_dve), w(s_pe)],
             inc=s_dgps, inc16=True)
        spill_rdy = w(s_dgps)

        rows = sb("rows", [1, 2304], F32)
        R_VB, R_QKB, R_K2, R_RINV = 0, 256, 512, 1536

        def row(a, n):
            return rows[0:1, a:a + n]
        lnsrc = sb("lnsrc", [128, 2 * 512], F32)
        lspool = Pool([lnsrc[:, 0:512], lnsrc[:, 512:1024]])
        sql = sb("sql", [128, 2 * 512], F32)
        sqlpool = Pool([sql[:, 0:512], sql[:, 512:1024]])
        wldA = sb("wldA", [128, 1024], F32)
        wqkh = sb("wqkh", [128, 2 * 1024], F32R)
        wqkl = sb("wqkl", [128, 2 * 1024], F32R)
        wvh = sb("wvh", [128, 8 * 256], F32R)
        wvl = sb("wvl", [128, 8 * 256], F32R)
        lnhp = [sb(f"l1lnh_{i}", [128, 512], F32R) for i in range(2)]
        lnlp = [sb(f"l1lnl_{i}", [128, 512], F32R) for i in range(2)]
        lnpool = Pool(list(zip(lnhp, lnlp)))
        vq = sb("vq", [128, 16 * 130], F32)
        qhi = sb("qhi", [128, T], F32R)
        qlo = sb("qlo", [128, T], F32R)
        khi = sb("khi", [128, T], F32R)
        klo = sb("klo", [128, T], F32R)
        ptile = sb("ptile", [128, 2 * 512], F32)
        q2s = sb("q2s", [128, 512], F32)
        otAh = sb("otAh", [64, 512], F32R)
        otAl = sb("otAl", [64, 512], F32R)
        otBh = sb("otBh", [64, 512], F32R)
        otBl = sb("otBl", [64, 512], F32R)
        wps = sb("wps", [128, 1024], F32R)
        wpld = sb("wpld", [64, 256], F32)
        bprj = sb("bprj", [128, 8], F32)
        mcols = sb("mcols", [128, 32], F32)
        k2cols = sb("k2cols", [128, 32], F32)

        p.op("syn", "dma_start", out=bprj[:, 0:8], in_=bo_c,
             inc=s_dsyn, inc16=True)
        bprj_rdy = w(s_dsyn)

        def ln512(c, j, extra=()):
            si, (lh, ll), g = lnpool.next()
            t0 = j * 512
            ls_i, lsrc, lg = lspool.next()
            p.op("syn", "dma_start", out=lsrc,
                 in_=ln_src_dram[c * 128:(c + 1) * 128, t0:t0 + 512],
                 waits=lg + [spill_rdy], inc=s_dsyn, inc16=True)
            sq_i, scr, sg = sqlpool.next()
            p.op("dve", "tensor_sub", scr, lsrc, meanb[:, t0:t0 + 512],
                 waits=list(extra) + g + sg + [w(s_dsyn)])
            p.op("dve", "tensor_mul", scr, scr, rstdb[:, t0:t0 + 512],
                 inc=s_dve)
            lspool.used(ls_i, s_dve)
            p.op("act", "copy", lh[:, :], scr, waits=[w(s_dve)], inc=s_act)
            p.op("dve", "tensor_sub", ll[:, :], scr, lh[:, :], waits=[w(s_act)],
                 inc=s_dve)
            sqlpool.used(sq_i, s_dve)
            return si, lh, ll, w(s_dve)

        for quarter in range(4):
            vcol0 = 2 * D + quarter * 256
            for c in range(NCH):
                slot = c % 2
                p.op("syn", "dma_start", out=wldA[:, slot * 256:slot * 256 + 256],
                     in_=wqkvo[c * 128:(c + 1) * 128, vcol0:vcol0 + 256],
                     waits=[w(s_dve)], inc=s_dsyn, inc16=True)
                p.op("dve", "tensor_copy", wvh[:, c * 256:(c + 1) * 256],
                     wldA[:, slot * 256:slot * 256 + 256],
                     waits=[w(s_dsyn), w(s_pe)])
                p.op("dve", "tensor_sub", wvl[:, c * 256:(c + 1) * 256],
                     wldA[:, slot * 256:slot * 256 + 256],
                     wvh[:, c * 256:(c + 1) * 256], inc=s_dve)
            wv_rdy = w(s_dve)
            p.op("syn", "dma_start", out=row(R_VB, 256),
                 in_=bqkv[:, vcol0:vcol0 + 256], waits=[w(s_pe)],
                 inc=s_dsyn, inc16=True)
            brow_rdy = w(s_dsyn)

            for hp in (2 * quarter, 2 * quarter + 1):
                hloc = hp % 2
                qc0 = hp * 128
                wldAv = wldA[:, :].rearrange("q (c o) -> q c o", c=NCH)
                p.op("syn", "dma_start", out=wldAv,
                     in_=wqkvo[:, qc0:qc0 + 128].rearrange(
                         "(c q) o -> q c o", q=128),
                     waits=[w(s_dve)], inc=s_dsyn, inc16=True)
                split_into(wldA[:, :], wqkh[:, 0:1024], wqkl[:, 0:1024],
                           waits=[w(s_dsyn), w(s_pe)])
                p.op("syn", "dma_start", out=wldAv,
                     in_=wqkvo[:, D + qc0:D + qc0 + 128].rearrange(
                         "(c q) o -> q c o", q=128),
                     waits=[w(s_dve)], inc=s_dsyn, inc16=True)
                wqk_rdy = split_into(wldA[:, :], wqkh[:, 1024:2048],
                                     wqkl[:, 1024:2048],
                                     waits=[w(s_dsyn), w(s_pe)])
                p.op("syn", "dma_start", out=row(R_QKB, 128),
                     in_=bqkv[:, qc0:qc0 + 128],
                     waits=[w(s_pe)], inc=s_dsyn, inc16=True)
                p.op("syn", "dma_start", out=row(R_QKB + 128, 128),
                     in_=bqkv[:, D + qc0:D + qc0 + 128], inc=s_dsyn,
                     inc16=True)
                p.wait("pe", w(s_dsyn), w(s_act))
                colize(0, row(R_QKB, 128))
                bias_rdy0 = colize(1, row(R_QKB + 128, 128))
                p.op("dve", "tensor_scalar", bcol[:, 0:1], bcol[:, 0:1],
                     -2.0, None, OP.mult, inc=s_dve)
                bias_rdy = w(s_dve)
                zv = vq[:, :].rearrange(
                    "q (i h c) -> q i h c", i=16, h=2)[:, :, :, 64:65]

                for j in range(4):
                    t0 = j * 512
                    wsv = [banks.start(b) for b in range(4)]
                    wsq = banks.start(4)
                    wsk = banks.start(5)
                    for c in range(NCH):
                        extra = [stats_rdy] if (j == 0 and c == 0) else []
                        si, lh, ll, lt = ln512(c, j, extra=extra)
                        for sub in range(4):
                            st_ = lh[:, sub * 128:(sub + 1) * 128]
                            sl_ = ll[:, sub * 128:(sub + 1) * 128]
                            p.op("pe", "matmul", pb[sub][0:128, 0:256], st_,
                                 wvh[:, c * 256:(c + 1) * 256], start=(c == 0),
                                 stop=False,
                                 waits=[lt] + ([wv_rdy] if c == 0 else []) +
                                       (wsv[sub] if c == 0 else []))
                            p.op("pe", "matmul", pb[sub][0:128, 0:256], sl_,
                                 wvh[:, c * 256:(c + 1) * 256], start=False,
                                 stop=False)
                            p.op("pe", "matmul", pb[sub][0:128, 0:256], st_,
                                 wvl[:, c * 256:(c + 1) * 256], start=False,
                                 stop=False, inc=(s_pe if sub == 3 else None))
                        qh_ = wqkh[:, c * 128:(c + 1) * 128]
                        ql_ = wqkl[:, c * 128:(c + 1) * 128]
                        kh_ = wqkh[:, 1024 + c * 128:1024 + (c + 1) * 128]
                        kl_ = wqkl[:, 1024 + c * 128:1024 + (c + 1) * 128]
                        p.op("pe", "matmul", pb[4][:, :], qh_, lh[:, :],
                             start=(c == 0), stop=False,
                             waits=([wqk_rdy] if (j == 0 and c == 0) else []) +
                                   (wsq if c == 0 else []))
                        p.op("pe", "matmul", pb[4][:, :], qh_, ll[:, :],
                             start=False, stop=False)
                        p.op("pe", "matmul", pb[4][:, :], ql_, lh[:, :],
                             start=False, stop=(c == NCH - 1))
                        p.op("pe", "matmul", pb[5][:, :], kh_, lh[:, :],
                             start=(c == 0), stop=False,
                             waits=(wsk if c == 0 else []))
                        p.op("pe", "matmul", pb[5][:, :], kh_, ll[:, :],
                             start=False, stop=False)
                        p.op("pe", "matmul", pb[5][:, :], kl_, lh[:, :],
                             start=False, stop=(c == NCH - 1),
                             inc=s_pe)
                        lnpool.used(si, s_pe)
                    for sub in range(4):
                        i = 4 * j + sub
                        p.op("pe", "matmul", pb[sub][0:128, 0:256], onescol[:, :],
                             row(R_VB, 256), start=False, stop=True,
                             waits=([brow_rdy] if i == 0 else []), inc=s_pe)
                        vdone = w(s_pe)
                        for hh in range(2):
                            p.op("act", "copy",
                                 vq[:, i * 130 + hh * 65:i * 130 + hh * 65 + 64],
                                 pb[sub][0:128, hloc * 128 + hh * 64:
                                         hloc * 128 + (hh + 1) * 64],
                                 waits=([vdone] if hh == 0 else []),
                                 inc=(s_act if hh == 1 else None))
                        banks.read_done(sub, s_act)
                    for (bank, bci, dsth, dstl, scl) in (
                            (4, 0, qhi[:, t0:t0 + 512], qlo[:, t0:t0 + 512],
                             -2.0),
                            (5, 1, khi[:, t0:t0 + 512], klo[:, t0:t0 + 512],
                             1.0)):
                        qk_done = w(s_pe)
                        sq_i, scr, sg = sqlpool.next()
                        if scl == -2.0:
                            p.op("act", "activation", scr, pb[bank][:, :],
                                 AF.Identity, bias=bcol[:, bci:bci + 1],
                                 scale=cpool[:, C_NEG2:C_NEG2 + 1],
                                 waits=[qk_done, bias_rdy, w(s_dve)] + sg,
                                 inc=s_act)
                        else:
                            p.op("act", "activation", scr, pb[bank][:, :],
                                 AF.Identity, bias=bcol[:, bci:bci + 1],
                                 waits=[qk_done, bias_rdy, w(s_dve)] + sg,
                                 inc=s_act)
                        banks.read_done(bank, s_act)
                        p.op("act", "copy", dsth, scr, inc=s_act)
                        p.op("dve", "tensor_sub", dstl, scr, dsth,
                             waits=[w(s_act)], inc=s_dve)
                        sqlpool.used(sq_i, s_dve)
                    ksq_i, ksq, ksg = sqlpool.next()
                    p.op("dve", "tensor_add", ksq, khi[:, t0:t0 + 512],
                         klo[:, t0:t0 + 512], waits=[w(s_dve)] + ksg)
                    p.op("dve", "tensor_mul", ksq, ksq, ksq, inc=s_dve)
                    for hh, msk in ((0, onesAc), (1, onesBc)):
                        wb6 = banks.start(6)
                        p.op("pe", "matmul", pb[6][0:1, :], msk[:, :], ksq,
                             start=True, stop=True, waits=[w(s_dve)] + wb6,
                             inc=s_pe)
                        p.op("dve", "tensor_copy", row(R_K2, 512),
                             pb[6][0:1, :], waits=[w(s_pe)], inc=s_dve)
                        banks.read_done(6, s_dve)
                        p.op("act", "activation", row(R_K2 + 512, 512),
                             row(R_K2, 512), AF.Sqrt, waits=[w(s_dve)],
                             inc=s_act)
                        p.wait("pe", w(s_act))
                        for sub in range(4):
                            i = 4 * j + sub
                            mc = 2 * i + hh
                            wb7 = banks.start(7)
                            p.op("pe", "matmul", pb[7][:, 0:1],
                                 row(R_K2 + sub * 128, 128),
                                 cpool[0:1, C_ONE:C_ONE + 1],
                                 start=True, stop=False, waits=wb7)
                            p.op("pe", "matmul", pb[7][:, 1:2],
                                 row(R_K2 + 512 + sub * 128, 128),
                                 cpool[0:1, C_ONE:C_ONE + 1],
                                 start=False, stop=True, inc=s_pe)
                            p.op("dve", "tensor_copy",
                                 k2cols[:, mc:mc + 1], pb[7][:, 0:1],
                                 waits=[w(s_pe)])
                            p.op("dve", "tensor_copy",
                                 mcols[:, mc:mc + 1], pb[7][:, 1:2],
                                 inc=s_dve)
                            banks.read_done(7, s_dve)
                    sqlpool.used(ksq_i, s_pe)
                    for sub in range(4):
                        i = 4 * j + sub
                        for hh in range(2):
                            mc = mcols[:, 2 * i + hh:2 * i + hh + 1]
                            vsl = vq[:, i * 130 + hh * 65:i * 130 + hh * 65 + 64]
                            p.op("dve", "tensor_scalar", vsl, vsl, mc, None,
                                 OP.mult, waits=[w(s_act), w(s_dve)])
                            p.op("dve", "tensor_copy",
                                 vq[:, i * 130 + hh * 65 + 64:
                                    i * 130 + hh * 65 + 65], mc, inc=s_dve)
                vq_rdy = w(s_dve)

                for j in range(4):
                    t0 = j * 512
                    qsq_i, qsq, qsg = sqlpool.next()
                    p.op("dve", "tensor_add", qsq, qhi[:, t0:t0 + 512],
                         qlo[:, t0:t0 + 512], waits=[vq_rdy] + qsg)
                    p.op("dve", "tensor_mul", qsq, qsq, qsq, inc=s_dve)
                    for half in range(2):
                        hrow0 = 64 * half
                        obank = 2 + half
                        msk = onesAc if half == 0 else onesBc
                        wb6 = banks.start(6)
                        p.op("pe", "matmul", pb[6][0:1, :], msk[:, :], qsq,
                             start=True, stop=True, waits=[w(s_dve)] + wb6,
                             inc=s_pe)
                        p.op("dve", "tensor_scalar", row(R_RINV, 512),
                             pb[6][0:1, :], 0.25, None, OP.mult,
                             waits=[w(s_pe)], inc=s_dve)
                        banks.read_done(6, s_dve)
                        wb6 = banks.start(6)
                        p.op("pe", "matmul", pb[6][:, :], onescol[:, :],
                             row(R_RINV, 512), start=True, stop=True,
                             waits=[w(s_dve)] + wb6, inc=s_pe)
                        p.op("dve", "tensor_copy", q2s[:, :], pb[6][:, :],
                             waits=[w(s_pe)], inc=s_dve)
                        banks.read_done(6, s_dve)
                        q2_rdy = w(s_dve)
                        wso = banks.start(obank)
                        for i in range(16):
                            k0 = i * 128
                            sbk = i % 2
                            wss = banks.start(sbk)
                            p.op("pe", "matmul", pb[sbk][:, :],
                                 khi[hrow0:hrow0 + 64, k0:k0 + 128],
                                 qhi[hrow0:hrow0 + 64, t0:t0 + 512],
                                 start=True, stop=False, waits=wss)
                            p.op("pe", "matmul", pb[sbk][:, :],
                                 khi[hrow0:hrow0 + 64, k0:k0 + 128],
                                 qlo[hrow0:hrow0 + 64, t0:t0 + 512],
                                 start=False, stop=False)
                            p.op("pe", "matmul", pb[sbk][:, :],
                                 klo[hrow0:hrow0 + 64, k0:k0 + 128],
                                 qhi[hrow0:hrow0 + 64, t0:t0 + 512],
                                 start=False, stop=True, inc=s_pe)
                            s_done = w(s_pe)
                            pt = ptile[:, (i % 2) * 512:(i % 2) * 512 + 512]
                            k2c_i = 2 * i + half
                            p.op("dve", "scalar_tensor_tensor", pt,
                                 pb[sbk][:, :],
                                 k2cols[:, k2c_i:k2c_i + 1], q2s[:, :],
                                 OP.add, OP.add,
                                 waits=[s_done, q2_rdy], inc=s_dve)
                            banks.read_done(sbk, s_dve)
                            p.op("act", "activation", pt, pt, AF.Relu,
                                 waits=[w(s_dve)])
                            p.op("act", "activation", pt, pt, AF.Sqrt)
                            p.op("act", "activation", pt, pt, AF.Square,
                                 bias=cpool[:, C_EPS2:C_EPS2 + 1], inc=s_act)
                            p.op("dve", "reciprocal", pt, pt,
                                 waits=[w(s_act)], inc=s_dve)
                            vstat = vq[:, i * 130 + half * 65:
                                       i * 130 + half * 65 + 65]
                            p.op("pe", "matmul", pb[obank][0:65, :], vstat, pt,
                                 start=(i == 0), stop=(i == 15),
                                 waits=[w(s_dve)] + (wso if i == 0 else []),
                                 inc=(s_pe if i == 15 else None))
                    pv_done = w(s_pe)
                    otA = ptile[0:64, 0:512]
                    otB = ptile[0:64, 512:1024]
                    for half in range(2):
                        obank = 2 + half
                        ot = otA if half == 0 else otB
                        p.op("dve", "reciprocal", row(R_RINV, 512),
                             pb[obank][64:65, :], waits=[pv_done], inc=s_dve)
                        wb6 = banks.start(6)
                        p.op("pe", "matmul", pb[6][0:64, :], onescol[0:1, 0:64],
                             row(R_RINV, 512), start=True, stop=True,
                             waits=[w(s_dve)] + wb6, inc=s_pe)
                        si_r, scr_r, sg_r = sqlpool.next()
                        p.op("dve", "tensor_copy", scr_r[0:64, :],
                             pb[6][0:64, :], waits=[w(s_pe)] + sg_r, inc=s_dve)
                        banks.read_done(6, s_dve)
                        p.op("dve", "tensor_mul", ot, pb[obank][0:64, :],
                             scr_r[0:64, :], inc=s_dve)
                        sqlpool.used(si_r, s_dve)
                        banks.read_done(obank, s_dve)
                    o_rdy = w(s_dve)
                    if DEBUG_O:
                        p.op("gps", "dma_start",
                             out=dbg_o[hp * 128:hp * 128 + 64, t0:t0 + 512],
                             in_=otA, waits=[o_rdy], inc=s_dgps, inc16=True)
                        p.op("gps", "dma_start",
                             out=dbg_o[hp * 128 + 64:hp * 128 + 128,
                                       t0:t0 + 512],
                             in_=otB, inc=s_dgps, inc16=True)
                    p.op("act", "copy", otAh[:, :], otA, waits=[o_rdy, w(s_pe)])
                    p.op("act", "copy", otBh[:, :], otB, inc=s_act)
                    p.op("dve", "tensor_sub", otAl[:, :], otA, otAh[:, :],
                         waits=[w(s_act)])
                    p.op("dve", "tensor_sub", otBl[:, :], otB, otBh[:, :],
                         inc=s_dve)
                    osp = w(s_dve)
                    for oc in range(NCH):
                        obk = 4 + (oc % 2)
                        slot = oc % 2
                        sA = wps[0:64, slot * 512:slot * 512 + 512]
                        p.op("syn", "dma_start",
                             out=wpld[:, :].rearrange("q (h o) -> q h o", h=2),
                             in_=wqkvo[hp * 128:hp * 128 + 128,
                                       3 * D + oc * 128:3 * D + (oc + 1) * 128
                                       ].rearrange("(h q) o -> q h o", h=2),
                             waits=[w(s_dve)], inc=s_dsyn, inc16=True)
                        p.op("dve", "tensor_copy", sA[:, 0:256],
                             wpld[:, :], waits=[w(s_dsyn), w(s_pe)])
                        p.op("dve", "tensor_sub", sA[:, 256:512],
                             wpld[:, :], sA[:, 0:256],
                             inc=s_dve)
                        wp_rdy = w(s_dve)
                        wpAh_ = sA[:, 0:128]
                        wpBh_ = sA[:, 128:256]
                        wpAl_ = sA[:, 256:384]
                        wpBl_ = sA[:, 384:512]
                        wso2 = banks.start(obk)
                        ws0 = [osp] if oc == 0 else []
                        p.op("pe", "matmul", pb[obk][:, :], wpAh_, otAh[:, :],
                             start=True, stop=False, waits=ws0 + wso2 + [wp_rdy])
                        p.op("pe", "matmul", pb[obk][:, :], wpAh_, otAl[:, :],
                             start=False, stop=False)
                        p.op("pe", "matmul", pb[obk][:, :], wpAl_, otAh[:, :],
                             start=False, stop=False)
                        p.op("pe", "matmul", pb[obk][:, :], wpBh_, otBh[:, :],
                             start=False, stop=False)
                        p.op("pe", "matmul", pb[obk][:, :], wpBh_, otBl[:, :],
                             start=False, stop=False)
                        p.op("pe", "matmul", pb[obk][:, :], wpBl_, otBh[:, :],
                             start=False, stop=True, inc=s_pe)
                        prj_done = w(s_pe)
                        if hp == 0:
                            p.op("dve", "scalar_tensor_tensor",
                                 xch(oc, t0, 512), pb[obk][:, :],
                                 bprj[:, oc:oc + 1], xch(oc, t0, 512),
                                 OP.add, OP.add,
                                 waits=[prj_done] + ([bprj_rdy] if (j == 0 and oc == 0) else []),
                                 inc=s_dve)
                        else:
                            p.op("dve", "tensor_add", xch(oc, t0, 512),
                                 xch(oc, t0, 512), pb[obk][:, :],
                                 waits=[prj_done], inc=s_dve)
                        banks.read_done(obk, s_dve)

        phase_barrier()
        p.default_stack = None
        pstk.close()

    def l1_ffn():
        pstk = contextlib.ExitStack()
        p.default_stack = pstk
        sident = sb("sident", [128, 128], F32)
        p.op("syn", "dma_start", out=sident[:, :], in_=ident, inc=s_dsyn,
             inc16=True)
        ident_rdy = w(s_dsyn)
        colb = sb("colb", [128, 4], F32)
        rowsf = sb("rowsf", [1, 512], F32)

        def part_max_bcast(src_t, width):
            nch_ = (width + 127) // 128
            for ch in range(nch_):
                w_ = min(128, width - ch * 128)
                wb7 = banks.start(7)
                p.op("pe", "transpose", pb[7][0:w_, 0:128],
                     src_t[:, ch * 128:ch * 128 + w_], sident[:, :],
                     waits=[w(s_dve), ident_rdy] + wb7, inc=s_pe)
                p.op("dve", "reduce_max", colb[0:w_, ch:ch + 1],
                     pb[7][0:w_, 0:128], AX, waits=[w(s_pe)], inc=s_dve)
                banks.read_done(7, s_dve)
            off = 0
            for ch in range(nch_):
                w_ = min(128, width - ch * 128)
                wb7 = banks.start(7)
                p.op("pe", "transpose", pb[7][0:1, 0:128],
                     colb[:, ch:ch + 1], sident[:, :],
                     waits=[w(s_dve)] + wb7, inc=s_pe)
                p.op("dve", "tensor_copy", rowsf[0:1, off:off + w_],
                     pb[7][0:1, 0:w_], waits=[w(s_pe)], inc=s_dve)
                banks.read_done(7, s_dve)
                off += w_
            wb6 = banks.start(6)
            p.op("pe", "matmul", pb[6][:, 0:width], onescol[:, :],
                 rowsf[0:1, 0:width], start=True, stop=True,
                 waits=[w(s_dve)] + wb6, inc=s_pe)
            return pb[6], w(s_pe)
        FT = 256
        FW = FT + 2
        NT = T // FT
        ln_stats()
        stats_rdy = w(s_dve)
        xbv3 = xb[:, :].rearrange("q (c t) -> q c t", c=NCH)
        xsv3 = xs_hbm.rearrange("(c q) t -> q c t", q=128)
        p.op("gps", "dma_start", out=xsv3, in_=xbv3, waits=[w(s_dve), w(s_pe)],
             inc=s_dgps, inc16=True)
        spill_rdy = w(s_dgps)

        qstk = contextlib.ExitStack()
        p.default_stack = qstk
        wqld = sb("wqld", [128, 2 * 512], F32)
        wqpool = Pool([wqld[:, 0:512], wqld[:, 512:1024]])
        qscr = sb("qscr", [128, 2 * 512], F32)
        tqb = sb("tqb", [128, 2 * 512], BF16)
        tqpool = Pool([tqb[:, 0:512], tqb[:, 512:1024]])
        acc = sb("acc1", [128, 2], F32)
        p.default_stack = pstk

        def quantize_weights(src_dram, dst_dram, nrow, ncol, tau_col_i):
            ntr, ntc = nrow // 128, ncol // 512
            p.op("dve", "memset", acc[:, 0:1], 0.0)
            for r_ in range(ntr):
                for c_ in range(ntc):
                    si, slot, sg = wqpool.next()
                    p.op("syn", "dma_start", out=slot,
                         in_=src_dram[r_ * 128:(r_ + 1) * 128,
                                      c_ * 512:(c_ + 1) * 512],
                         waits=sg, inc=s_dsyn, inc16=True)
                    p.op("dve", "tensor_reduce", acc[:, 1:2], slot, AX, OP.add,
                         apply_absolute_value=True, waits=[w(s_dsyn)])
                    p.op("dve", "tensor_add", acc[:, 0:1], acc[:, 0:1],
                         acc[:, 1:2], inc=s_dve)
                    wqpool.used(si, s_dve)
            wb7 = banks.start(7)
            p.op("pe", "matmul", pb[7][0:1, 0:1], cpool[:, C_ONE:C_ONE + 1],
                 acc[:, 0:1],
                 start=True, stop=True, waits=[w(s_dve)] + wb7, inc=s_pe)
            p.op("dve", "tensor_scalar", cell(0), pb[7][0:1, 0:1],
                 1.0 / (nrow * ncol), 1e-5, OP.mult, OP.max,
                 waits=[w(s_pe)], inc=s_dve)
            banks.read_done(7, s_dve)
            p.op("dve", "tensor_scalar", cell(1), cell(0), 0.5, None, OP.mult)
            p.op("dve", "tensor_scalar", cell(2), cell(1), -1.0, None, OP.mult,
                 inc=s_dve)
            p.wait("pe", w(s_dve))
            colize(tau_col_i, cell(1))
            colize(tau_col_i + 1, cell(2))
            mrdy = colize(tau_col_i + 2, cell(0))
            for r_ in range(ntr):
                for c_ in range(ntc):
                    si, slot, sg = wqpool.next()
                    p.op("syn", "dma_start", out=slot,
                         in_=src_dram[r_ * 128:(r_ + 1) * 128,
                                      c_ * 512:(c_ + 1) * 512],
                         waits=sg, inc=s_dsyn, inc16=True)
                    p.op("dve", "tensor_scalar", qscr[:, 0:512], slot,
                         bcol[:, tau_col_i:tau_col_i + 1], None, OP.is_ge,
                         waits=[w(s_dsyn), mrdy])
                    p.op("dve", "tensor_scalar", qscr[:, 512:1024], slot,
                         bcol[:, tau_col_i + 1:tau_col_i + 2], None,
                         OP.is_le, inc=s_dve)
                    wqpool.used(si, s_dve)
                    ti, tslot, tg = tqpool.next()
                    p.op("dve", "tensor_sub", tslot, qscr[:, 0:512],
                         qscr[:, 512:1024], waits=tg, inc=s_dve)
                    p.op("gps", "dma_start",
                         out=dst_dram[r_ * 128:(r_ + 1) * 128,
                                      c_ * 512:(c_ + 1) * 512],
                         in_=tslot, waits=[w(s_dve)], inc=s_dgps, inc16=True)
                    tqpool.used(ti, s_dgps)

        quantize_weights(f1, t1_hbm, D, HID, 4)
        m1rdy = w(s_dve)
        quantize_weights(f2, t2_hbm, HID, D, 0)
        tq_done = w(s_dgps)
        phase_barrier()
        qstk.close()

        lnb = sb("lnb", [128, NCH * FW], F32)
        qx = sb("qx", [128, NCH * FW], BF16)
        y1h = sb("y1h", [128, 32 * FW], F32)
        h2b = sb("h2b", [128, 32 * FT], F32)
        qh = sb("qh", [128, 32 * FT], BF16)
        wtb = sb("wtb", [128, 2 * 2048], BF16)
        wtpool = Pool([wtb[:, 0:2048], wtb[:, 2048:4096]])
        sclb = sb("sclb", [128, 4 * FW], F32)
        s_b = sclb[:, 0:FW]
        osc_b = sclb[:, FW:2 * FW]
        s2_b = sclb[:, 2 * FW:2 * FW + FT]
        osc2_b = sclb[:, 3 * FW:3 * FW + FT]
        amx = sb("amx", [128, FW], F32)
        sconvw = sb("sconvw", [128, 96], F32)
        sconvb = sb("sconvb", [128, 32], F32)
        p.op("syn", "dma_start", out=sconvw[:, :], in_=convw, inc=s_dsyn,
             inc16=True)
        p.op("syn", "dma_start", out=sconvb[:, :], in_=convb, inc=s_dsyn,
             inc16=True)
        conv_rdy = w(s_dsyn)
        lnsrc = sb("lnsrc", [128, 2 * 512], F32)
        lspool = Pool([lnsrc[:, 0:512], lnsrc[:, 512:1024]])

        def cast_round_clip(dst_bf16, srcf, sb_scale, width, eng="dve"):
            i32 = cr_i32[:, 0:width]
            pr = cr_p[:, 0:width]
            rr = cr_r[:, 0:width]
            p.op(eng, "tensor_mul", pr, srcf, sb_scale)
            p.op(eng, "tensor_copy", i32, pr)
            p.op(eng, "tensor_copy", rr, i32)
            p.op(eng, "tensor_sub", pr, pr, rr)
            p.op(eng, "scalar_tensor_tensor", rr, pr, 0.5, rr, OP.is_ge,
                 OP.add)
            p.op(eng, "tensor_scalar", pr, pr, -0.5, -1.0, OP.is_le, OP.mult)
            p.op(eng, "tensor_add", rr, rr, pr)
            p.op(eng, "tensor_scalar", rr, rr, -128.0, 127.0, OP.max, OP.min)
            p.op(eng, "tensor_copy", dst_bf16, rr, inc=s_dve)

        cr_i32 = sb("cr_i32", [128, FW], I32)
        cr_p = sb("cr_p", [128, FW], F32)
        cr_r = sb("cr_r", [128, FW], F32)

        for t in range(NT):
            t0 = t * FT
            lo_pad = 1 if t == 0 else 0
            hi_pad = 1 if t == NT - 1 else 0
            src0 = t0 - 1 + lo_pad
            wsrc = FW - lo_pad - hi_pad
            for c in range(NCH):
                ls_i, lsrc, lg = lspool.next()
                p.op("syn", "dma_start", out=lsrc[:, 0:wsrc],
                     in_=xs_hbm[c * 128:(c + 1) * 128, src0:src0 + wsrc],
                     waits=lg + [spill_rdy], inc=s_dsyn, inc16=True)
                dst = lnb[:, c * FW:(c + 1) * FW]
                if lo_pad:
                    p.op("dve", "memset", dst[:, 0:1], 0.0)
                if hi_pad:
                    p.op("dve", "memset", dst[:, FW - 1:FW], 0.0)
                p.op("dve", "tensor_sub", dst[:, lo_pad:lo_pad + wsrc],
                     lsrc[:, 0:wsrc],
                     meanb[:, src0:src0 + wsrc],
                     waits=[w(s_dsyn)] +
                           ([stats_rdy] if (t == 0 and c == 0) else []) +
                           ([w(s_pe)] if c == 0 else []))
                p.op("dve", "tensor_mul", dst[:, lo_pad:lo_pad + wsrc],
                     dst[:, lo_pad:lo_pad + wsrc],
                     rstdb[:, src0:src0 + wsrc], inc=s_dve)
                lspool.used(ls_i, s_dve)
            if lo_pad or hi_pad:
                pass
            p.op("act", "activation", amx[:, :], lnb[:, 0:FW], AF.Abs,
                 waits=[w(s_dve)], inc=s_act)
            for c in range(1, NCH):
                scr_ab = cr_p[:, 0:FW] if c % 2 else cr_r[:, 0:FW]
                p.op("act", "activation", scr_ab,
                     lnb[:, c * FW:(c + 1) * FW], AF.Abs, inc=s_act)
                p.op("dve", "tensor_max", amx[:, :], amx[:, :], scr_ab,
                     waits=[w(s_act)], inc=s_dve)
            pmx, pmx_rdy = part_max_bcast(amx, FW)
            p.op("dve", "tensor_single_scalar", s_b, pmx[:, 0:FW], 1e-5,
                 OP.max, waits=[pmx_rdy])
            p.op("dve", "tensor_scalar", osc_b, s_b,
                 bcol[:, 6:7], None, OP.mult, waits=[m1rdy])
            p.op("dve", "tensor_scalar", osc_b, osc_b, 1.0 / 127.0, None,
                 OP.mult)
            p.op("dve", "reciprocal", s_b, s_b)
            p.op("dve", "tensor_scalar", s_b, s_b, 127.0, None, OP.mult,
                 inc=s_dve)
            banks.read_done(6, s_dve)
            for c in range(NCH):
                cast_round_clip(qx[:, c * FW:(c + 1) * FW],
                                lnb[:, c * FW:(c + 1) * FW], s_b, FW)
            qx_rdy = w(s_dve)

            for h in range(32):
                si, slot, sg = wtpool.next()
                p.wait("syn", *sg)
                p.op("syn", "dma_start",
                     out=slot[:, 0:1024].rearrange("q (c o) -> q c o", c=NCH),
                     in_=t1_hbm[:, h * 128:(h + 1) * 128].rearrange(
                         "(c q) o -> q c o", q=128),
                     waits=([tq_done] if (t == 0 and h == 0) else []),
                     inc=s_dsyn, inc16=True)
                srdy = w(s_dsyn)
                bk_ = h % 2
                wsb_ = banks.start(bk_)
                for c in range(NCH):
                    p.op("pe", "matmul", pb[bk_][:, 0:FW],
                         slot[:, c * 128:(c + 1) * 128],
                         qx[:, c * FW:(c + 1) * FW],
                         start=(c == 0), stop=(c == NCH - 1),
                         waits=([srdy, qx_rdy] if c == 0 else []) +
                               (wsb_ if c == 0 else []),
                         inc=(s_pe if c == NCH - 1 else None))
                wtpool.used(si, s_pe)
                mmd = w(s_pe)
                p.op("dve", "tensor_mul", y1h[:, h * FW:(h + 1) * FW],
                     pb[bk_][:, 0:FW], osc_b, waits=[mmd], inc=s_dve)
                banks.read_done(bk_, s_dve)
            for h in range(32):
                yh = y1h[:, h * FW:(h + 1) * FW]
                sq_i, scr, sg = sqpool.next()
                p.op("dve", "tensor_scalar", scr, yh[:, 0:FT],
                     sconvw[:, h * 3:h * 3 + 1], None, OP.mult,
                     waits=sg + ([conv_rdy] if (t == 0 and h == 0) else []))
                p.op("dve", "scalar_tensor_tensor", scr, yh[:, 1:1 + FT],
                     sconvw[:, h * 3 + 1:h * 3 + 2], scr, OP.mult, OP.add)
                p.op("dve", "scalar_tensor_tensor", scr, yh[:, 2:2 + FT],
                     sconvw[:, h * 3 + 2:h * 3 + 3], scr, OP.mult, OP.add,
                     inc=s_dve)
                p.op("act", "activation", h2b[:, h * FT:(h + 1) * FT], scr,
                     AF.Silu, bias=sconvb[:, h:h + 1], waits=[w(s_dve)],
                     inc=s_act)
                sqpool.used(sq_i, s_act)
            h2_rdy = w(s_act)
            p.op("act", "activation", amx[:, 0:FT], h2b[:, 0:FT], AF.Abs,
                 waits=[h2_rdy], inc=s_act)
            for h in range(1, 32):
                scr_ab = cr_p[:, 0:FT] if h % 2 else cr_r[:, 0:FT]
                p.op("act", "activation", scr_ab,
                     h2b[:, h * FT:(h + 1) * FT], AF.Abs, inc=s_act)
                p.op("dve", "tensor_max", amx[:, 0:FT], amx[:, 0:FT], scr_ab,
                     waits=[w(s_act)], inc=s_dve)
            pmx2, pmx2_rdy = part_max_bcast(amx[:, 0:FT], FT)
            p.op("dve", "tensor_single_scalar", s2_b, pmx2[:, 0:FT], 1e-5,
                 OP.max, waits=[pmx2_rdy])
            p.op("dve", "tensor_scalar", osc2_b, s2_b, bcol[:, 2:3], None,
                 OP.mult)
            p.op("dve", "tensor_scalar", osc2_b, osc2_b, 1.0 / 127.0, None,
                 OP.mult)
            p.op("dve", "reciprocal", s2_b, s2_b)
            p.op("dve", "tensor_scalar", s2_b, s2_b, 127.0, None, OP.mult,
                 inc=s_dve)
            banks.read_done(6, s_dve)
            for h in range(32):
                cast_round_clip(qh[:, h * FT:(h + 1) * FT],
                                h2b[:, h * FT:(h + 1) * FT], s2_b, FT)
            qh_rdy = w(s_dve)
            if DEBUG_O and t == 7:
                p.op("gps", "dma_start", out=dbg_c[:, 0:32 * FT],
                     in_=qh[:, :], waits=[qh_rdy], inc=s_dgps, inc16=True)
            for o in range(NCH):
                bk_ = 2 + o % 2
                wsb_ = banks.start(bk_)
                first = [qh_rdy] if o == 0 else []
                for half in range(2):
                    si, slot, sg = wtpool.next()
                    p.wait("syn", *sg)
                    p.op("syn", "dma_start",
                         out=slot[:, 0:2048].rearrange(
                             "q (c o) -> q c o", c=16),
                         in_=t2_hbm[half * 2048:(half + 1) * 2048,
                                    o * 128:(o + 1) * 128].rearrange(
                             "(c q) o -> q c o", q=128),
                         inc=s_dsyn, inc16=True)
                    srdy = w(s_dsyn)
                    if DEBUG_O and t == 7 and o == 0:
                        p.op("gps", "dma_start",
                             out=dbg_c[:, 8192 + half * 2048:
                                       8192 + half * 2048 + 2048],
                             in_=slot[:, 0:2048], waits=[srdy],
                             inc=s_dgps, inc16=True)
                        p.wait("pe", w(s_dgps))
                    for c in range(16):
                        cc = half * 16 + c
                        p.op("pe", "matmul", pb[bk_][:, 0:FT],
                             slot[:, c * 128:(c + 1) * 128],
                             qh[:, cc * FT:(cc + 1) * FT],
                             start=(cc == 0), stop=(cc == 31),
                             waits=([srdy] if c == 0 else []) + first +
                                   (wsb_ if cc == 0 else []),
                             inc=(s_pe if c == 15 else None))
                        first = []
                    wtpool.used(si, s_pe)
                mmd = w(s_pe)
                sq_i, scr, sg = sqpool.next()
                p.op("dve", "tensor_mul", scr, pb[bk_][:, 0:FT], osc2_b,
                     waits=[mmd] + sg, inc=s_dve)
                banks.read_done(bk_, s_dve)
                if DEBUG_O:
                    p.op("gps", "dma_start",
                         out=dbg_o[o * 128:(o + 1) * 128, t0:t0 + FT],
                         in_=scr, waits=[w(s_dve)], inc=s_dgps, inc16=True)
                p.op("dve", "tensor_add", xch(o, t0, FT), xch(o, t0, FT),
                     scr, inc=s_dve)
                sqpool.used(sq_i, s_dve)

        phase_barrier()
        p.default_stack = None
        pstk.close()

    if "l0a" in phases:
        l0_attention()
    if "l0f" in phases:
        l0_ffn()
    if "l1a" in phases:
        l1_attention()
    if "l1f" in phases:
        l1_ffn()

    yv = y_out.rearrange("(c q) t -> q c t", q=128)
    p.wait("gps", w(s_dve), w(s_pe), w(s_act))
    p.op("gps", "dma_start", out=yv, in_=xbv, inc=s_dgps, inc16=True)
    p.wait("gps", w(s_dgps))

    with contextlib.ExitStack() as stack:
        p.emit(stack)
    return nc


def _host_prep(inputs):
    f32 = np.float32
    x = np.ascontiguousarray(inputs["L0_attn_x"])
    com = {}
    com["w_attn"] = np.ascontiguousarray(inputs["L0_attn_W_attn"])
    com["b_attn"] = np.ascontiguousarray(inputs["L0_attn_b_attn"].reshape(1, -1))
    com["w_proj"] = np.ascontiguousarray(inputs["L0_attn_W_proj"])
    com["b_proj"] = np.ascontiguousarray(inputs["L0_attn_b_proj"].reshape(8, 128).T)
    com["w1t"] = np.ascontiguousarray(inputs["L0_ffn_w1_w"].T)
    com["w2t"] = np.ascontiguousarray(inputs["L0_ffn_w2_w"].T)
    com["h1t"] = np.ascontiguousarray(inputs["L0_ffn_h1_w"].T)
    com["h2t"] = np.ascontiguousarray(inputs["L0_ffn_h2_w"].T)
    com["w3t"] = np.ascontiguousarray(inputs["L0_ffn_w3_w"].T)
    com["s1c"] = np.ascontiguousarray(inputs["L0_ffn_w1_s"].reshape(32, 128).T)
    com["s2c"] = np.ascontiguousarray(inputs["L0_ffn_w2_s"].reshape(32, 128).T)
    com["h1sc"] = np.ascontiguousarray(inputs["L0_ffn_h1_s"].reshape(32, 128).T)
    com["h2sc"] = np.ascontiguousarray(inputs["L0_ffn_h2_s"].reshape(32, 128).T)
    com["w3sc"] = np.ascontiguousarray(inputs["L0_ffn_w3_s"].reshape(8, 128).T)
    com["wqkvo"] = np.ascontiguousarray(np.concatenate(
        [inputs["L1_attn_Wq"].T, inputs["L1_attn_Wk"].T,
         inputs["L1_attn_Wv"].T, inputs["L1_attn_Wo"].T], axis=1))
    com["bqkv"] = np.ascontiguousarray(np.concatenate(
        [inputs["L1_attn_bq"], inputs["L1_attn_bk"],
         inputs["L1_attn_bv"]]).reshape(1, -1))
    com["bo_c"] = np.ascontiguousarray(inputs["L1_attn_bo"].reshape(8, 128).T)
    com["f1"] = np.ascontiguousarray(inputs["L1_ffn_w1"].T)
    com["f2"] = np.ascontiguousarray(inputs["L1_ffn_w2"].T)
    com["convw"] = np.ascontiguousarray(
        inputs["L1_ffn_conv_w"][:, 0, :].reshape(32, 128, 3).transpose(1, 0, 2)
        .reshape(128, 96))
    com["convb"] = np.ascontiguousarray(inputs["L1_ffn_conv_b"].reshape(32, 128).T)
    com["ident"] = np.eye(128, dtype=f32)
    permm = np.zeros((128, 128), f32)
    for i in range(0, 128, 2):
        permm[i, i + 1] = 1.0
        permm[i + 1, i] = 1.0
    com["perm"] = permm
    with jax.default_device(jax.devices("cpu")[0]):
        _th = 1.0 / (10000.0 ** (jnp.arange(0, 64, 2, dtype=jnp.float32) / 64))
        _it = jnp.arange(16, dtype=jnp.float32)[:, None] * _th[None, :]
        cos16 = np.asarray(jnp.cos(_it))
        sin16 = np.asarray(jnp.sin(_it))
    pidx = np.arange(128)
    freq = (pidx % 64) // 2
    sgn = np.where(pidx % 2 == 0, -1.0, 1.0).astype(f32)
    rt = np.zeros((128, 32), f32)
    rt[:, 0:16] = cos16.T[freq]
    rt[:, 16:32] = sin16.T[freq] * sgn[:, None]
    com["ropet"] = rt

    in_maps = []
    for c in range(8):
        m = dict(com)
        m["x0"] = np.ascontiguousarray(x[c].T)
        in_maps.append(m)
    return in_maps


def kernel(**inputs):
    in_maps = _host_prep(inputs)
    nc = build_nc()
    res = run_bass_kernel_spmd(nc, in_maps, core_ids=list(range(8)))
    out = np.empty((8, T, D), np.float32)
    for c in range(8):
        out[c] = res.results[c]["y"].T
    return out


# version 04 (final)
# speedup vs baseline: 1.1593x; 1.1542x over previous
import math
import contextlib
import numpy as np
import jax
import jax.numpy as jnp
import concourse.bass as bass
import concourse.mybir as mybir
from concourse.bass_utils import run_bass_kernel_spmd

F32 = mybir.dt.float32
F32R = mybir.dt.float32r
BF16 = mybir.dt.bfloat16
I32 = mybir.dt.int32
OP = mybir.AluOpType
AF = mybir.ActivationFunctionType
AX = mybir.AxisListType.X

T = 2048
D = 1024
HID = 4096
NHEAD = 16
DH = 64
NCH = D // 128
LN_EPS = 1e-5
ALL_PHASES = ("l0a", "l0f", "l1a", "l1f")
DEBUG_O = False


class Sem:
    __slots__ = ("name", "h", "v")

    def __init__(self, name):
        self.name = name
        self.h = None
        self.v = 0


def w(sem):
    return (sem, sem.v)


class Prog:
    ENGINES = ("pe", "act", "dve", "gps", "syn")

    def __init__(self, nc):
        self.nc = nc
        self.items = {e: [] for e in self.ENGINES}
        self.sems = []
        self.sbuf_bytes = 0
        self.hz = {e: {"cycle": 0, "writes": []} for e in self.ENGINES}
        self.default_stack = None
        self.sb_names = set()

    def sem(self, name):
        s = Sem(name)
        self.sems.append(s)
        return s

    def sb(self, name, shape, dt):
        if name in self.sb_names:
            i = 2
            while f"{name}_{i}" in self.sb_names:
                i += 1
            name = f"{name}_{i}"
        self.sb_names.add(name)
        self.sbuf_bytes += shape[-1] * mybir.dt.size(dt)
        side = "right" if dt == F32R else "left"
        if self.default_stack is not None:
            return self.default_stack.enter_context(
                self.nc.sbuf_tensor(name, shape, dt, side=side))
        return self.nc.alloc_sbuf_tensor(name, shape, dt, side=side)

    HZ_SLACK = {"dve": 96, "act": 640, "gps": 256}
    HZ_DEPTH = {"dve": 24, "act": 420, "gps": 64}

    @staticmethod
    def _ap_info(x):
        try:
            name = x.tensor.name
            span = 1
            for step, count in list(x.ap)[1:]:
                span += abs(int(step)) * (int(count) - 1)
            off = int(x.offset)
            return (name, off, off + span, int(x.free_size()))
        except Exception:
            return None

    def _hazard(self, eng, meth, a, kw):
        if eng not in self.HZ_SLACK:
            return False
        st = self.hz[eng]
        if meth == "dma_start":
            return False
        aps = [x for x in a if self._ap_info(x) is not None]
        for k in ("bias", "scale", "accum_out"):
            v = kw.get(k)
            if v is not None and self._ap_info(v) is not None:
                aps.append(v)
        infos = [self._ap_info(x) for x in aps]
        outs, ins = infos[:1], infos[1:]
        if kw.get("accum_out") is not None and infos:
            outs = [infos[0], infos[-1]]
            ins = infos[1:-1]
        slack = self.HZ_SLACK[eng]
        depth = self.HZ_DEPTH[eng]
        now = st["cycle"]
        st["writes"] = [wr for wr in st["writes"] if now - wr[3] < slack]
        need = False
        for (n, lo, hi, sz) in ins:
            for (wn, wlo, whi, wend, wsz) in st["writes"]:
                if wn != n or lo >= whi or wlo >= hi:
                    continue
                if now - wend >= depth:
                    continue
                if wsz >= depth and sz >= wsz and lo == wlo and hi == whi:
                    continue
                need = True
                break
            if need:
                break
        if need:
            st["writes"] = []
        dur = max((i[3] for i in infos), default=1)
        st["cycle"] = now + dur
        for o in outs:
            if o is not None:
                st["writes"].append((o[0], o[1], o[2], st["cycle"], o[3]))
        return need

    def op(self, eng, meth, *a, waits=(), inc=None, inc16=False, **kw):
        waits = tuple((s, v) for (s, v) in waits if v > 0)
        inc_n = 16 if inc16 else 1
        need_drain = self._hazard(eng, meth, a, kw)

        def f(e, _m=meth, _a=a, _kw=kw, _w=waits, _i=inc, _n=inc_n,
              _d=need_drain):
            if _d:
                e.drain()
            for (s, v) in _w:
                e.wait_ge(s.h, v)
            ins = getattr(e, _m)(*_a, **_kw)
            if _i is not None:
                ins.then_inc(_i.h, _n)

        self.items[eng].append(f)
        if inc is not None:
            inc.v += inc_n

    def wait(self, eng, *waits):
        waits = tuple((s, v) for (s, v) in waits if v > 0)

        def f(e, _w=waits):
            for (s, v) in _w:
                e.wait_ge(s.h, v)
        self.items[eng].append(f)

    def emit(self, stack):
        nc = self.nc
        for s in self.sems:
            s.h = stack.enter_context(nc.semaphore(s.name))
        block = stack.enter_context(nc.Block())
        eng_map = {"pe": block.tensor, "act": block.scalar, "dve": block.vector,
                   "gps": block.gpsimd, "syn": block.sync}
        for name in self.ENGINES:
            fns = self.items[name]
            if not fns:
                continue

            def body(eng, _fns=fns):
                for f in _fns:
                    f(eng)
            eng_map[name](body)


class Banks:
    def __init__(self, pbs):
        self.pb = pbs
        self.gate = [None] * len(pbs)

    def start(self, i):
        g = self.gate[i]
        return [g] if g else []

    def read_done(self, i, sem):
        self.gate[i] = (sem, sem.v)


class Pool:
    def __init__(self, aps):
        self.aps = aps
        self.gate = [None] * len(aps)
        self.i = 0

    def next(self):
        i = self.i
        self.i = (i + 1) % len(self.aps)
        g = self.gate[i]
        return i, self.aps[i], ([g] if g else [])

    def used(self, i, sem):
        self.gate[i] = (sem, sem.v)


def hi_lo_const(v):
    hi = np.float32(v)
    lo = np.float32(v - float(hi))
    return float(hi), float(lo)


def build_nc(phases=ALL_PHASES):
    nc = bass.Bass("TRN2", target_bir_lowering=False,
                   detect_race_conditions=False)

    def din(name, shape, dt=F32):
        return nc.dram_tensor(name, shape, dt, kind="ExternalInput").ap()

    x0 = din("x0", [D, T])
    w_attn = din("w_attn", [D, 3 * D])
    b_attn = din("b_attn", [1, 3 * D])
    w_proj = din("w_proj", [D, D])
    b_proj = din("b_proj", [128, NCH])
    w1t = din("w1t", [D, HID], F32R)
    w2t = din("w2t", [D, HID], F32R)
    h1t = din("h1t", [HID, HID], F32R)
    h2t = din("h2t", [HID, HID], F32R)
    w3t = din("w3t", [HID, D], F32R)
    s1c = din("s1c", [128, 32])
    s2c = din("s2c", [128, 32])
    h1sc = din("h1sc", [128, 32])
    h2sc = din("h2sc", [128, 32])
    w3sc = din("w3sc", [128, 8])
    wqkvo = din("wqkvo", [D, 4 * D])
    bqkv = din("bqkv", [1, 3 * D])
    bo_c = din("bo_c", [128, NCH])
    f1 = din("f1", [D, HID])
    f2 = din("f2", [HID, D])
    convw = din("convw", [128, 96])
    convb = din("convb", [128, 32])
    perm = din("perm", [128, 128])
    ident = din("ident", [128, 128])
    ropet = din("ropet", [128, 32])

    y_out = nc.dram_tensor("y", [D, T], F32, kind="ExternalOutput").ap()
    t1_hbm = nc.dram_tensor("t1_hbm", [D, HID], BF16).ap()
    xs_hbm = nc.dram_tensor("xs_hbm", [D, T], F32).ap()
    dbg_o = nc.dram_tensor("dbg_o", [D, T], F32, kind="ExternalOutput").ap() if DEBUG_O else None
    dbg_b = nc.dram_tensor("dbg_b", [128, 8192], F32, kind="ExternalOutput").ap() if DEBUG_O else None
    dbg_c = nc.dram_tensor("dbg_c", [128, 16384], BF16, kind="ExternalOutput").ap() if DEBUG_O else None
    t2_hbm = nc.dram_tensor("t2_hbm", [HID, D], BF16).ap()

    p = Prog(nc)
    sb = p.sb

    s_pe = p.sem("s_pe")
    s_act = p.sem("s_act")
    s_dve = p.sem("s_dve")
    s_gps = p.sem("s_gps")
    s_dsyn = p.sem("s_dsyn")
    s_dgps = p.sem("s_dgps")

    xb = sb("xb", [128, NCH * T], F32)
    meanb = sb("meanb", [128, T], F32)
    rstdb = sb("rstdb", [128, T], F32)
    ones14 = sb("ones14", [128, 128], F32)
    onescol = sb("onescol", [1, 128], F32)
    onesAc = sb("onesAc", [128, 1], F32)
    onesBc = sb("onesBc", [128, 1], F32)
    cpool = sb("cpool", [128, 8], F32)
    C_EIGHTH, C_EPS, C_ONE, C_NEG2, C_EPS2 = 0, 1, 2, 3, 4
    cells = sb("cells", [1, 32], F32)
    bcol = sb("bcol", [128, 8], F32)
    sqs = sb("sqs", [128, 2 * 256], F32)

    pb = [nc.alloc_psum_tensor(f"pb{i}", [128, 512], F32) for i in range(8)]
    banks = Banks(pb)


    def row(a, n):
        return rows[0:1, a:a + n]

    def cell(i, n=1):
        return cells[0:1, i:i + n]

    p.op("dve", "memset", ones14[:, :], 1.0 / 1024.0)
    p.op("dve", "memset", onescol[:, :], 1.0)
    p.op("dve", "memset", onesAc[0:64, :], 1.0)
    p.op("dve", "memset", onesAc[64:128, :], 0.0)
    p.op("dve", "memset", onesBc[0:64, :], 0.0)
    p.op("dve", "memset", onesBc[64:128, :], 1.0)
    p.op("dve", "memset", cpool[:, C_EIGHTH:C_EIGHTH + 1], 0.125)
    p.op("dve", "memset", cpool[:, C_EPS:C_EPS + 1], LN_EPS)
    p.op("dve", "memset", bcol[:, :], 0.0)
    p.op("dve", "memset", cpool[:, C_ONE:C_ONE + 1], 1.0)
    p.op("dve", "memset", cpool[:, C_NEG2:C_NEG2 + 1], -2.0)
    p.op("dve", "memset", cpool[:, C_EPS2:C_EPS2 + 1], 1e-8, inc=s_dve)

    x0v = x0.rearrange("(c q) t -> q c t", q=128)
    xbv = xb[:, :].rearrange("q (c t) -> q c t", c=NCH)
    p.op("syn", "dma_start", out=xbv, in_=x0v, inc=s_dsyn, inc16=True)
    init_dma = w(s_dsyn)


    def xch(c, t0=0, tw=T):
        return xb[:, c * T + t0:c * T + t0 + tw]

    def phase_barrier():
        snaps = [w(s_pe), w(s_act), w(s_dve), w(s_gps), w(s_dsyn), w(s_dgps)]
        for e in ("pe", "act", "dve", "gps", "syn"):
            p.wait(e, *snaps)

    sqpool = Pool([sqs[:, 0:256], sqs[:, 256:512]])

    def ln_stats(input_rdy=()):
        for j in range(8):
            t0 = j * 256
            wm = banks.start(4)
            ws = banks.start(5)
            for c in range(NCH):
                si, sl, g = sqpool.next()
                p.op("dve", "tensor_mul", sl, xch(c, t0, 256), xch(c, t0, 256),
                     waits=g + list(input_rdy), inc=s_dve)
                p.op("pe", "matmul", pb[4][:, 0:256], ones14[:, :],
                     xch(c, t0, 256),
                     start=(c == 0), stop=(c == NCH - 1),
                     waits=(wm if c == 0 else []) + list(input_rdy))
                p.op("pe", "matmul", pb[5][:, 0:256], ones14[:, :], sl,
                     start=(c == 0), stop=(c == NCH - 1),
                     waits=[w(s_dve)] + (ws if c == 0 else []),
                     inc=s_pe)
                sqpool.used(si, s_pe)
            mmd = w(s_pe)
            p.op("dve", "tensor_copy", meanb[:, t0:t0 + 256], pb[4][:, 0:256],
                 waits=[mmd], inc=s_dve)
            banks.read_done(4, s_dve)
            si1, sl1, g1 = sqpool.next()
            p.op("dve", "tensor_mul", sl1, meanb[:, t0:t0 + 256],
                 meanb[:, t0:t0 + 256], waits=g1)
            p.op("dve", "tensor_sub", sl1, pb[5][:, 0:256], sl1, inc=s_dve)
            banks.read_done(5, s_dve)
            si2, sl2, g2 = sqpool.next()
            p.op("act", "activation", sl2, sl1, AF.Sqrt,
                 bias=cpool[:, C_EPS:C_EPS + 1], waits=[w(s_dve)] + g2,
                 inc=s_act)
            sqpool.used(si1, s_act)
            p.op("dve", "reciprocal", rstdb[:, t0:t0 + 256], sl2,
                 waits=[w(s_act)], inc=s_dve)
            sqpool.used(si2, s_dve)

    def ln_tile_into(dsth, dstl, scratch, c, t0, tw, extra_waits=()):
        p.op("dve", "tensor_sub", scratch, xch(c, t0, tw), meanb[:, t0:t0 + tw],
             waits=list(extra_waits))
        p.op("dve", "tensor_mul", scratch, scratch, rstdb[:, t0:t0 + tw])
        p.op("dve", "tensor_copy", dsth, scratch)
        p.op("dve", "tensor_sub", dstl, scratch, dsth, inc=s_dve)
        return w(s_dve)

    def split_into(src, dsth, dstl, waits=()):
        p.op("dve", "tensor_copy", dsth, src, waits=list(waits))
        p.op("dve", "tensor_sub", dstl, src, dsth, inc=s_dve)
        return w(s_dve)

    def colize(dst_col_idx, src_row_ap, bank=7):
        wb = banks.start(bank)
        if src_row_ap.shape[-1] == 1:
            lhsT, rhs = onescol[:, :], src_row_ap
        else:
            lhsT, rhs = src_row_ap, cpool[0:1, C_ONE:C_ONE + 1]
        p.op("pe", "matmul", pb[bank][:, dst_col_idx:dst_col_idx + 1],
             lhsT, rhs, start=True, stop=True,
             waits=wb, inc=s_pe)
        p.op("dve", "tensor_copy", bcol[:, dst_col_idx:dst_col_idx + 1],
             pb[bank][:, dst_col_idx:dst_col_idx + 1], waits=[w(s_pe)],
             inc=s_dve)
        banks.read_done(bank, s_dve)
        return w(s_dve)

    def l0_attention():
        pstk = contextlib.ExitStack()
        p.default_stack = pstk
        ln_src_dram = x0
        rows = sb("rows", [1, 1280], F32)
        R_BIAS, R_QKB, R_RINV = 0, 256, 512

        def row(a, n):
            return rows[0:1, a:a + n]
        lnsrc = sb("lnsrc", [128, 2 * 512], F32)
        lspool = Pool([lnsrc[:, 0:512], lnsrc[:, 512:1024]])
        sql = sb("sql", [128, 2 * 512], F32)
        sqlpool = Pool([sql[:, 0:512], sql[:, 512:1024]])
        lcos = sb("lcos", [128, 16], F32)
        lsin = sb("lsin", [128, 16], F32)
        sperm = sb("sperm", [128, 128], F32)
        p.op("syn", "dma_start", out=sperm[:, :], in_=perm, inc=s_dsyn,
             inc16=True)
        p.op("syn", "dma_start", out=lcos[:, :], in_=ropet[:, 0:16],
             inc=s_dsyn, inc16=True)
        p.op("syn", "dma_start", out=lsin[:, :], in_=ropet[:, 16:32],
             inc=s_dsyn, inc16=True)
        ropecs = sb("ropecs", [128, 2], F32)

        ln_stats(input_rdy=[init_dma])
        stats_rdy = w(s_dve)

        wldA = sb("wldA", [128, 1024], F32)
        wqkh = sb("wqkh", [128, 2 * 1024], F32R)
        wqkl = sb("wqkl", [128, 2 * 1024], F32R)
        wvh = sb("wvh", [128, 8 * 256], F32R)
        wvl = sb("wvl", [128, 8 * 256], F32R)
        lnhp = [sb(f"lnh4_{i}", [128, 512], F32R) for i in range(2)]
        lnlp = [sb(f"lnl4_{i}", [128, 512], F32R) for i in range(2)]
        lnpool = Pool(list(zip(lnhp, lnlp)))
        vq = sb("vq", [128, 16 * 130], F32)
        qth = sb("qth", [128, 512], F32R)
        qtl = sb("qtl", [128, 512], F32R)
        khi = sb("khi", [128, T], F32R)
        klo = sb("klo", [128, T], F32R)
        ptile = sb("ptile", [128, 2 * 512], F32)
        otAh = sb("otAh", [64, 512], F32R)
        otAl = sb("otAl", [64, 512], F32R)
        otBh = sb("otBh", [64, 512], F32R)
        otBl = sb("otBl", [64, 512], F32R)
        wps = sb("wps", [128, 1024], F32R)
        wpld = sb("wpld", [64, 256], F32)
        bprj = sb("bprj", [128, 8], F32)

        p.op("syn", "dma_start", out=bprj[:, 0:8], in_=b_proj,
             inc=s_dsyn, inc16=True)
        bprj_rdy = w(s_dsyn)
        rope_tab_rdy = w(s_dsyn)

        def ln512(c, j, extra=()):
            si, (lh, ll), g = lnpool.next()
            t0 = j * 512
            ls_i, lsrc, lg = lspool.next()
            p.op("syn", "dma_start", out=lsrc,
                 in_=ln_src_dram[c * 128:(c + 1) * 128, t0:t0 + 512],
                 waits=lg, inc=s_dsyn, inc16=True)
            sq_i, scr, sg = sqlpool.next()
            p.op("dve", "tensor_sub", scr, lsrc, meanb[:, t0:t0 + 512],
                 waits=list(extra) + g + sg + [w(s_dsyn)])
            p.op("dve", "tensor_mul", scr, scr, rstdb[:, t0:t0 + 512],
                 inc=s_dve)
            lspool.used(ls_i, s_dve)
            p.op("act", "copy", lh[:, :], scr, waits=[w(s_dve)], inc=s_act)
            p.op("dve", "tensor_sub", ll[:, :], scr, lh[:, :], waits=[w(s_act)],
                 inc=s_dve)
            sqlpool.used(sq_i, s_dve)
            return si, lh, ll, w(s_dve)

        first_ln = [True]

        for quarter in range(4):
            vcol0 = 2 * D + quarter * 256
            for c in range(NCH):
                slot = c % 2
                p.op("syn", "dma_start", out=wldA[:, slot * 256:slot * 256 + 256],
                     in_=w_attn[c * 128:(c + 1) * 128, vcol0:vcol0 + 256],
                     waits=[w(s_dve)], inc=s_dsyn, inc16=True)
                p.op("dve", "tensor_copy", wvh[:, c * 256:(c + 1) * 256],
                     wldA[:, slot * 256:slot * 256 + 256],
                     waits=[w(s_dsyn), w(s_pe)])
                p.op("dve", "tensor_sub", wvl[:, c * 256:(c + 1) * 256],
                     wldA[:, slot * 256:slot * 256 + 256],
                     wvh[:, c * 256:(c + 1) * 256], inc=s_dve)
            wv_rdy = w(s_dve)
            p.op("syn", "dma_start", out=row(R_BIAS, 256),
                 in_=b_attn[:, vcol0:vcol0 + 256], waits=[w(s_pe)],
                 inc=s_dsyn, inc16=True)
            brow_rdy = w(s_dsyn)

            for hp in (2 * quarter, 2 * quarter + 1):
                hloc = hp % 2
                qc0 = hp * 128
                wldAv = wldA[:, :].rearrange("q (c o) -> q c o", c=NCH)
                p.op("syn", "dma_start", out=wldAv,
                     in_=w_attn[:, qc0:qc0 + 128].rearrange(
                         "(c q) o -> q c o", q=128),
                     waits=[w(s_dve)], inc=s_dsyn, inc16=True)
                split_into(wldA[:, :], wqkh[:, 0:1024], wqkl[:, 0:1024],
                           waits=[w(s_dsyn), w(s_pe)])
                p.op("syn", "dma_start", out=wldAv,
                     in_=w_attn[:, D + qc0:D + qc0 + 128].rearrange(
                         "(c q) o -> q c o", q=128),
                     waits=[w(s_dve)], inc=s_dsyn, inc16=True)
                wqk_rdy = split_into(wldA[:, :], wqkh[:, 1024:2048],
                                     wqkl[:, 1024:2048],
                                     waits=[w(s_dsyn), w(s_pe)])
                p.op("syn", "dma_start", out=row(R_QKB, 128),
                     in_=b_attn[:, qc0:qc0 + 128],
                     waits=[w(s_pe)], inc=s_dsyn, inc16=True)
                p.op("syn", "dma_start", out=row(R_QKB + 128, 128),
                     in_=b_attn[:, D + qc0:D + qc0 + 128], inc=s_dsyn,
                     inc16=True)
                p.wait("pe", w(s_dsyn), w(s_act))
                colize(0, row(R_QKB, 128))
                bias_rdy = colize(1, row(R_QKB + 128, 128))
                zcols = vq[:, :].rearrange(
                    "q (i h c) -> q i h c", i=16, h=2)[:, :, :, 64:65]
                p.op("dve", "memset", zcols, 1.0, waits=[w(s_pe)], inc=s_dve)
                z_rdy = w(s_dve)
                p.op("dve", "memset", cells[0:1, 0:4], 0.0, inc=s_dve)
                hA, hB = 2 * hp, 2 * hp + 1
                p.op("dve", "tensor_copy", ropecs[0:64, 0:1],
                     lcos[0:64, hA:hA + 1], waits=[rope_tab_rdy, w(s_act)])
                p.op("dve", "tensor_copy", ropecs[64:128, 0:1],
                     lcos[64:128, hB:hB + 1])
                p.op("dve", "tensor_copy", ropecs[0:64, 1:2],
                     lsin[0:64, hA:hA + 1])
                p.op("dve", "tensor_copy", ropecs[64:128, 1:2],
                     lsin[64:128, hB:hB + 1], inc=s_dve)
                ropec_rdy = w(s_dve)

                for j in range(4):
                    t0 = j * 512
                    wsv = [banks.start(b) for b in range(4)]
                    wsq = banks.start(4)
                    wsk = banks.start(5)
                    for c in range(NCH):
                        extra = [stats_rdy] if first_ln[0] else []
                        first_ln[0] = False
                        si, lh, ll, lt = ln512(c, j, extra=extra)
                        for sub in range(4):
                            st_ = lh[:, sub * 128:(sub + 1) * 128]
                            sl_ = ll[:, sub * 128:(sub + 1) * 128]
                            p.op("pe", "matmul", pb[sub][0:128, 0:256], st_,
                                 wvh[:, c * 256:(c + 1) * 256], start=(c == 0),
                                 stop=False,
                                 waits=[lt] + ([wv_rdy] if c == 0 else []) +
                                       (wsv[sub] if c == 0 else []))
                            p.op("pe", "matmul", pb[sub][0:128, 0:256], sl_,
                                 wvh[:, c * 256:(c + 1) * 256], start=False,
                                 stop=False)
                            p.op("pe", "matmul", pb[sub][0:128, 0:256], st_,
                                 wvl[:, c * 256:(c + 1) * 256], start=False,
                                 stop=False)
                        qh_ = wqkh[:, c * 128:(c + 1) * 128]
                        ql_ = wqkl[:, c * 128:(c + 1) * 128]
                        kh_ = wqkh[:, 1024 + c * 128:1024 + (c + 1) * 128]
                        kl_ = wqkl[:, 1024 + c * 128:1024 + (c + 1) * 128]
                        p.op("pe", "matmul", pb[4][:, :], qh_, lh[:, :],
                             start=(c == 0), stop=False,
                             waits=([wqk_rdy] if (j == 0 and c == 0) else []) +
                                   (wsq if c == 0 else []))
                        p.op("pe", "matmul", pb[4][:, :], qh_, ll[:, :],
                             start=False, stop=False)
                        p.op("pe", "matmul", pb[4][:, :], ql_, lh[:, :],
                             start=False, stop=(c == NCH - 1))
                        p.op("pe", "matmul", pb[5][:, :], kh_, lh[:, :],
                             start=(c == 0), stop=False,
                             waits=(wsk if c == 0 else []))
                        p.op("pe", "matmul", pb[5][:, :], kh_, ll[:, :],
                             start=False, stop=False)
                        p.op("pe", "matmul", pb[5][:, :], kl_, lh[:, :],
                             start=False, stop=(c == NCH - 1),
                             inc=s_pe)
                        lnpool.used(si, s_pe)
                    for sub in range(4):
                        i = 4 * j + sub
                        p.op("pe", "matmul", pb[sub][0:128, 0:256], onescol[:, :],
                             row(R_BIAS, 256), start=False, stop=True,
                             waits=([brow_rdy] if i == 0 else []), inc=s_pe)
                        vdone = w(s_pe)
                        for hh in range(2):
                            p.op("act", "copy",
                                 vq[:, i * 130 + hh * 65:i * 130 + hh * 65 + 64],
                                 pb[sub][0:128, hloc * 128 + hh * 64:
                                         hloc * 128 + (hh + 1) * 64],
                                 waits=([vdone] if hh == 0 else []),
                                 inc=(s_act if hh == 1 else None))
                        banks.read_done(sub, s_act)
                    for (bank, bci, dsth, dstl) in (
                            (4, 0, qth[:, :], qtl[:, :]),
                            (5, 1, khi[:, t0:t0 + 512], klo[:, t0:t0 + 512])):
                        qk_done = w(s_pe)
                        sq_i, scr, sg = sqlpool.next()
                        p.op("act", "activation", scr, pb[bank][:, :],
                             AF.Identity, bias=bcol[:, bci:bci + 1],
                             waits=[qk_done, bias_rdy, w(s_dve)] + sg,
                             inc=s_act)
                        banks.read_done(bank, s_act)
                        wb7 = banks.start(7)
                        p.op("pe", "matmul", pb[7][:, :], sperm[:, :], scr,
                             start=True, stop=True, waits=[w(s_act)] + wb7,
                             inc=s_pe)
                        swp = w(s_pe)
                        sq_i2, scr2, sg2 = sqlpool.next()
                        p.op("dve", "tensor_scalar", scr2, pb[7][:, :],
                             ropecs[:, 1:2], None, OP.mult,
                             waits=[swp, ropec_rdy] + sg2, inc=s_dve)
                        banks.read_done(7, s_dve)
                        p.op("dve", "tensor_scalar", scr, scr,
                             ropecs[:, 0:1], None, OP.mult)
                        p.op("dve", "tensor_add", scr, scr, scr2, inc=s_dve)
                        sqlpool.used(sq_i2, s_dve)
                        p.op("act", "copy", dsth, scr,
                             waits=[w(s_dve), w(s_pe)], inc=s_act)
                        p.op("dve", "tensor_sub", dstl, scr, dsth,
                             waits=[w(s_act)], inc=s_dve)
                        sqlpool.used(sq_i, s_dve)
                    qk_split_rdy = w(s_dve)
                    if DEBUG_O and hp == 0:
                        p.op("gps", "dma_start", out=dbg_b[:, 0:2048],
                             in_=meanb[:, :], waits=[qk_split_rdy],
                             inc=s_dgps, inc16=True)
                        p.op("gps", "dma_start", out=dbg_b[:, 2048:4096],
                             in_=rstdb[:, :], inc=s_dgps, inc16=True)
                        p.op("gps", "dma_start",
                             out=dbg_b[:, 4096 + t0:4096 + t0 + 512],
                             in_=khi[:, t0:t0 + 512], inc=s_dgps, inc16=True)
                        p.op("gps", "dma_start",
                             out=dbg_b[:, 6144 + t0 // 4:6144 + t0 // 4 + 128],
                             in_=qth[:, 0:128], inc=s_dgps, inc16=True)
                        p.op("gps", "dma_start", out=dbg_b[:, 6656:6658],
                             in_=ropecs[:, :], inc=s_dgps, inc16=True)
                        p.op("gps", "dma_start", out=dbg_b[:, 6658:6674],
                             in_=lcos[:, :], inc=s_dgps, inc16=True)
                        p.op("gps", "dma_start", out=dbg_b[:, 6674:6690],
                             in_=lsin[:, :], inc=s_dgps, inc16=True)
                        p.op("gps", "dma_start", out=dbg_b[:, 6690:6698],
                             in_=bcol[:, :], inc=s_dgps, inc16=True)

                    for r, (bhi, blo, msk, acc) in enumerate(
                            ((qth[:, :], qtl[:, :], onesAc, 0),
                             (qth[:, :], qtl[:, :], onesBc, 1),
                             (khi[:, t0:t0 + 512], klo[:, t0:t0 + 512],
                              onesAc, 2),
                             (khi[:, t0:t0 + 512], klo[:, t0:t0 + 512],
                              onesBc, 3))):
                        msk = msk[:, :]
                        si, sl, g = sqlpool.next()
                        p.op("dve", "tensor_add", sl, bhi, blo,
                             waits=[qk_split_rdy] + g)
                        p.op("dve", "tensor_mul", sl, sl, sl, inc=s_dve)
                        wb6 = banks.start(6)
                        p.op("pe", "matmul", pb[6][0:1, :], msk, sl,
                             start=True, stop=True, waits=[w(s_dve)] + wb6,
                             inc=s_pe)
                        sqlpool.used(si, s_pe)
                        p.op("dve", "reduce_max", cell(16), pb[6][0:1, :], AX,
                             waits=[w(s_pe)], inc=s_dve)
                        banks.read_done(6, s_dve)
                        if r < 2:
                            p.op("dve", "tensor_copy", cell(acc), cell(16))
                        else:
                            p.op("dve", "tensor_max", cell(acc), cell(acc),
                                 cell(16))
                    for hh in range(2):
                        p.op("dve", "tensor_mul", cell(4 + hh), cell(hh),
                             cell(2 + hh))
                    p.op("dve", "tensor_copy", cell(8, 2), cell(4, 2),
                         inc=s_dve)
                    p.op("act", "activation", cell(6, 2), cell(8, 2), AF.Sqrt,
                         waits=[w(s_dve)], inc=s_act)
                    p.op("dve", "tensor_scalar", cell(10, 2), cell(6, 2),
                         -0.125, None, OP.mult, waits=[w(s_act)], inc=s_dve)
                    p.wait("pe", w(s_dve), w(s_act))
                    colize(2, cell(10))
                    smax_rdy = colize(3, cell(11))

                    ntk = 4 * j + 4
                    for half in range(2):
                        hrow0 = 64 * half
                        obank = 2 + half
                        wso = banks.start(obank)
                        for i in range(ntk):
                            k0 = i * 128
                            sbk = i % 2
                            wss = banks.start(sbk)
                            p.op("pe", "matmul", pb[sbk][:, :],
                                 khi[hrow0:hrow0 + 64, k0:k0 + 128],
                                 qth[hrow0:hrow0 + 64, :],
                                 start=True, stop=False,
                                 waits=[qk_split_rdy] + wss)
                            p.op("pe", "matmul", pb[sbk][:, :],
                                 khi[hrow0:hrow0 + 64, k0:k0 + 128],
                                 qtl[hrow0:hrow0 + 64, :],
                                 start=False, stop=False)
                            p.op("pe", "matmul", pb[sbk][:, :],
                                 klo[hrow0:hrow0 + 64, k0:k0 + 128],
                                 qth[hrow0:hrow0 + 64, :],
                                 start=False, stop=True, inc=s_pe)
                            s_done = w(s_pe)
                            pt = ptile[:, (i % 2) * 512:(i % 2) * 512 + 512]
                            p.op("act", "activation", pt, pb[sbk][:, :], AF.Exp,
                                 bias=bcol[:, 2 + half:3 + half],
                                 scale=cpool[:, C_EIGHTH:C_EIGHTH + 1],
                                 waits=[s_done, smax_rdy], inc=s_act)
                            banks.read_done(sbk, s_act)
                            pw = [w(s_act)]
                            if k0 >= t0:
                                p.op("gps", "affine_select", pt, pt,
                                     [[1, 512]], OP.is_ge, 0.0,
                                     base=t0 - k0, channel_multiplier=-1,
                                     waits=[w(s_act)], inc=s_gps)
                                pw = [w(s_gps)]
                            vstat = vq[:, i * 130 + half * 65:
                                       i * 130 + half * 65 + 65]
                            p.op("pe", "matmul", pb[obank][0:65, :], vstat, pt,
                                 start=(i == 0), stop=(i == ntk - 1),
                                 waits=pw + ([z_rdy] if i == 0 else []) +
                                       (wso if i == 0 else []),
                                 inc=(s_pe if i == ntk - 1 else None))
                    pv_done = w(s_pe)
                    otA = ptile[0:64, 0:512]
                    otB = ptile[0:64, 512:1024]
                    for half in range(2):
                        obank = 2 + half
                        ot = otA if half == 0 else otB
                        p.op("dve", "reciprocal", row(R_RINV, 512),
                             pb[obank][64:65, :],
                             waits=[pv_done] + ([w(s_gps)] if half == 0 else []),
                             inc=s_dve)
                        wb6 = banks.start(6)
                        p.op("pe", "matmul", pb[6][0:64, :], onescol[0:1, 0:64],
                             row(R_RINV, 512), start=True, stop=True,
                             waits=[w(s_dve)] + wb6, inc=s_pe)
                        si_r, scr_r, sg_r = sqlpool.next()
                        p.op("dve", "tensor_copy", scr_r[0:64, :],
                             pb[6][0:64, :], waits=[w(s_pe)] + sg_r, inc=s_dve)
                        banks.read_done(6, s_dve)
                        p.op("dve", "tensor_mul", ot, pb[obank][0:64, :],
                             scr_r[0:64, :], inc=s_dve)
                        sqlpool.used(si_r, s_dve)
                        banks.read_done(obank, s_dve)
                    o_rdy = w(s_dve)
                    if DEBUG_O:
                        p.op("gps", "dma_start",
                             out=dbg_o[hp * 128:hp * 128 + 64, t0:t0 + 512],
                             in_=otA, waits=[o_rdy], inc=s_dgps, inc16=True)
                        p.op("gps", "dma_start",
                             out=dbg_o[hp * 128 + 64:hp * 128 + 128, t0:t0 + 512],
                             in_=otB, inc=s_dgps, inc16=True)
                    p.op("act", "copy", otAh[:, :], otA, waits=[o_rdy, w(s_pe)])
                    p.op("act", "copy", otBh[:, :], otB, inc=s_act)
                    p.op("dve", "tensor_sub", otAl[:, :], otA, otAh[:, :],
                         waits=[w(s_act)])
                    p.op("dve", "tensor_sub", otBl[:, :], otB, otBh[:, :],
                         inc=s_dve)
                    osp = w(s_dve)
                    for oc in range(NCH):
                        obk = 4 + (oc % 2)
                        slot = oc % 2
                        sA = wps[0:64, slot * 512:slot * 512 + 512]
                        p.op("syn", "dma_start",
                             out=wpld[:, :].rearrange("q (h o) -> q h o", h=2),
                             in_=w_proj[hp * 128:hp * 128 + 128,
                                        oc * 128:(oc + 1) * 128].rearrange(
                                 "(h q) o -> q h o", h=2),
                             waits=[w(s_dve)], inc=s_dsyn, inc16=True)
                        p.op("dve", "tensor_copy", sA[:, 0:256],
                             wpld[:, :], waits=[w(s_dsyn), w(s_pe)])
                        p.op("dve", "tensor_sub", sA[:, 256:512],
                             wpld[:, :], sA[:, 0:256],
                             inc=s_dve)
                        wp_rdy = w(s_dve)
                        wpAh_ = sA[:, 0:128]
                        wpBh_ = sA[:, 128:256]
                        wpAl_ = sA[:, 256:384]
                        wpBl_ = sA[:, 384:512]
                        wso2 = banks.start(obk)
                        ws0 = [osp] if oc == 0 else []
                        p.op("pe", "matmul", pb[obk][:, :], wpAh_, otAh[:, :],
                             start=True, stop=False, waits=ws0 + wso2 + [wp_rdy])
                        p.op("pe", "matmul", pb[obk][:, :], wpAh_, otAl[:, :],
                             start=False, stop=False)
                        p.op("pe", "matmul", pb[obk][:, :], wpAl_, otAh[:, :],
                             start=False, stop=False)
                        p.op("pe", "matmul", pb[obk][:, :], wpBh_, otBh[:, :],
                             start=False, stop=False)
                        p.op("pe", "matmul", pb[obk][:, :], wpBh_, otBl[:, :],
                             start=False, stop=False)
                        p.op("pe", "matmul", pb[obk][:, :], wpBl_, otBh[:, :],
                             start=False, stop=True, inc=s_pe)
                        prj_done = w(s_pe)
                        if hp == 0:
                            p.op("dve", "scalar_tensor_tensor",
                                 xch(oc, t0, 512), pb[obk][:, :],
                                 bprj[:, oc:oc + 1], xch(oc, t0, 512),
                                 OP.add, OP.add,
                                 waits=[prj_done] + ([bprj_rdy] if (j == 0 and oc == 0) else []),
                                 inc=s_dve)
                        else:
                            p.op("dve", "tensor_add", xch(oc, t0, 512),
                                 xch(oc, t0, 512), pb[obk][:, :],
                                 waits=[prj_done], inc=s_dve)
                        banks.read_done(obk, s_dve)

        phase_barrier()
        p.default_stack = None
        pstk.close()

    def l0_ffn():
        pstk = contextlib.ExitStack()
        p.default_stack = pstk
        FT = 256
        NT = T // FT
        ln_stats()
        stats_rdy = w(s_dve)

        Ah = sb("Ah", [128, 32 * FT], F32R)
        Al = sb("Al", [128, 32 * FT], F32R)
        Br = sb("Br", [128, 32 * FT], F32)
        ghi_t = sb("ghi", [128, NCH * FT], F32R)
        glo_t = sb("glo", [128, NCH * FT], F32R)
        ghi = ghi_t[:, :]
        glo = glo_t[:, :]
        wstrip = sb("wstrip", [128, 3 * 1024], F32R)
        wpool = Pool([wstrip[:, 0:1024], wstrip[:, 1024:2048],
                      wstrip[:, 2048:3072]])
        scales = sb("scales", [128, 4 * 32 + 8], F32)
        sc_s1, sc_s2, sc_h1, sc_h2, sc_w3 = 0, 32, 64, 96, 128

        p.op("syn", "dma_start", out=scales[:, 0:32], in_=s1c, inc=s_dsyn,
             inc16=True)
        p.op("syn", "dma_start", out=scales[:, 32:64], in_=s2c, inc=s_dsyn,
             inc16=True)
        p.op("syn", "dma_start", out=scales[:, 64:96], in_=h1sc, inc=s_dsyn,
             inc16=True)
        p.op("syn", "dma_start", out=scales[:, 96:128], in_=h2sc, inc=s_dsyn,
             inc16=True)
        p.op("syn", "dma_start", out=scales[:, 128:136], in_=w3sc, inc=s_dsyn,
             inc16=True)
        scales_rdy = w(s_dsyn)

        def scol(base, o):
            return scales[:, base + o:base + o + 1]

        def stream_strip(dram, ocols, c0, ncw, slot_ap):
            p.op("syn", "dma_start",
                 out=slot_ap.rearrange("q (c o) -> q c o", c=ncw),
                 in_=dram[c0 * 128:(c0 + ncw) * 128,
                          ocols:ocols + 128].rearrange(
                     "(c q) o -> q c o", q=128),
                 inc=s_dsyn, inc16=True)
            return w(s_dsyn)

        for t in range(NT):
            t0 = t * FT
            gw = [w(s_pe), stats_rdy] if True else []
            for c in range(NCH):
                sq_i, scr, sg = sqpool.next()
                scr = scr[:, 0:FT]
                p.op("dve", "tensor_sub", scr, xch(c, t0, FT),
                     meanb[:, t0:t0 + FT], waits=gw + sg)
                gw = []
                p.op("dve", "tensor_mul", scr, scr, rstdb[:, t0:t0 + FT],
                     inc=s_dve)
                p.op("act", "copy", ghi[:, c * FT:(c + 1) * FT], scr,
                     waits=[w(s_dve)], inc=s_act)
                p.op("dve", "tensor_sub", glo[:, c * FT:(c + 1) * FT], scr,
                     ghi[:, c * FT:(c + 1) * FT], waits=[w(s_act)], inc=s_dve)
                sqpool.used(sq_i, s_dve)
            g_rdy = w(s_dve)

            for o in range(32):
                s1i, w1s, sg1 = wpool.next()
                p.wait("syn", *sg1)
                r1 = stream_strip(w1t, o * 128, 0, 8, w1s)
                s2i, w2s, sg2 = wpool.next()
                p.wait("syn", *sg2)
                r2 = stream_strip(w2t, o * 128, 0, 8, w2s)
                ba = o % 2
                bb = 2 + o % 2
                wsa = banks.start(ba)
                wsb = banks.start(bb)
                for c in range(NCH):
                    gh = ghi[:, c * FT:(c + 1) * FT]
                    gl = glo[:, c * FT:(c + 1) * FT]
                    w1c = w1s[:, c * 128:(c + 1) * 128]
                    w2c = w2s[:, c * 128:(c + 1) * 128]
                    p.op("pe", "matmul", pb[ba][:, 0:FT], w1c, gh,
                         start=(c == 0), stop=False,
                         waits=([g_rdy, r2] if c == 0 else []) +
                               (wsa if c == 0 else []))
                    p.op("pe", "matmul", pb[ba][:, 0:FT], w1c, gl,
                         start=False, stop=(c == NCH - 1))
                    p.op("pe", "matmul", pb[bb][:, 0:FT], w2c, gh,
                         start=(c == 0), stop=False,
                         waits=(wsb if c == 0 else []))
                    p.op("pe", "matmul", pb[bb][:, 0:FT], w2c, gl,
                         start=False, stop=(c == NCH - 1),
                         inc=s_pe)
                wpool.used(s1i, s_pe)
                wpool.used(s2i, s_pe)
                mmd = w(s_pe)
                sq_i, scr, sg = sqpool.next()
                scr = scr[:, 0:FT]
                p.op("act", "activation", scr, pb[ba][:, 0:FT], AF.Silu,
                     scale=scol(sc_s1, o),
                     waits=[mmd, scales_rdy] + sg, inc=s_act)
                banks.read_done(ba, s_act)
                sq_i2, scr2, sg2 = sqpool.next()
                scr2 = scr2[:, 0:FT]
                p.op("dve", "scalar_tensor_tensor", scr2, pb[bb][:, 0:FT],
                     scol(sc_s2, o), scr, OP.mult, OP.mult,
                     waits=[w(s_act)] + sg2, inc=s_dve)
                banks.read_done(bb, s_dve)
                p.op("act", "copy", Ah[:, o * FT:(o + 1) * FT], scr2,
                     waits=[w(s_dve), w(s_pe)], inc=s_act)
                p.op("dve", "tensor_sub", Al[:, o * FT:(o + 1) * FT], scr2,
                     Ah[:, o * FT:(o + 1) * FT], waits=[w(s_act)], inc=s_dve)
                sqpool.used(sq_i, s_act)
                sqpool.used(sq_i2, s_dve)
            a_rdy = w(s_dve)

            for stage, (hdram, hsc) in enumerate(((h1t, sc_h1), (h2t, sc_h2))):
                for o in range(32):
                    bk_ = 4 + o % 2
                    wsk = banks.start(bk_)
                    first = [a_rdy] if o == 0 else []
                    for quarter_ in range(4):
                        si, slot, sg = wpool.next()
                        p.wait("syn", *sg)
                        rs = stream_strip(hdram, o * 128, quarter_ * 8, 8, slot)
                        for c in range(8):
                            cc = quarter_ * 8 + c
                            hc = slot[:, c * 128:(c + 1) * 128]
                            p.op("pe", "matmul", pb[bk_][:, 0:FT], hc,
                                 Ah[:, cc * FT:(cc + 1) * FT],
                                 start=(cc == 0), stop=False,
                                 waits=([rs] if c == 0 else []) + first +
                                       (wsk if cc == 0 else []))
                            first = []
                            p.op("pe", "matmul", pb[bk_][:, 0:FT], hc,
                                 Al[:, cc * FT:(cc + 1) * FT],
                                 start=False, stop=(cc == 31),
                                 inc=s_pe)
                        wpool.used(si, s_pe)
                    mmd = w(s_pe)
                    p.op("act", "activation", Br[:, o * FT:(o + 1) * FT],
                         pb[bk_][:, 0:FT], AF.Gelu, scale=scol(hsc, o),
                         waits=[mmd, w(s_dve)], inc=s_act)
                    banks.read_done(bk_, s_act)
                br_rdy = w(s_act)
                for o in range(32):
                    p.op("act", "copy", Ah[:, o * FT:(o + 1) * FT],
                         Br[:, o * FT:(o + 1) * FT],
                         waits=([br_rdy, w(s_pe)] if o == 0 else []),
                         inc=s_act)
                    p.op("dve", "tensor_sub", Al[:, o * FT:(o + 1) * FT],
                         Br[:, o * FT:(o + 1) * FT],
                         Ah[:, o * FT:(o + 1) * FT], waits=[w(s_act)],
                         inc=s_dve)
                a_rdy = w(s_dve)

            for o in range(NCH):
                bk_ = 6 + o % 2
                wsk = banks.start(bk_)
                first = [a_rdy] if o == 0 else []
                for quarter_ in range(4):
                    si, slot, sg = wpool.next()
                    p.wait("syn", *sg)
                    rs = stream_strip(w3t, o * 128, quarter_ * 8, 8, slot)
                    for c in range(8):
                        cc = quarter_ * 8 + c
                        hc = slot[:, c * 128:(c + 1) * 128]
                        p.op("pe", "matmul", pb[bk_][:, 0:FT], hc,
                             Ah[:, cc * FT:(cc + 1) * FT],
                             start=(cc == 0), stop=False,
                             waits=([rs] if c == 0 else []) + first +
                                   (wsk if cc == 0 else []))
                        first = []
                        p.op("pe", "matmul", pb[bk_][:, 0:FT], hc,
                             Al[:, cc * FT:(cc + 1) * FT],
                             start=False, stop=(cc == 31),
                             inc=s_pe)
                    wpool.used(si, s_pe)
                mmd = w(s_pe)
                p.op("dve", "scalar_tensor_tensor", xch(o, t0, FT),
                     pb[bk_][:, 0:FT], scol(sc_w3, o), xch(o, t0, FT),
                     OP.mult, OP.add, waits=[mmd], inc=s_dve)
                banks.read_done(bk_, s_dve)

        phase_barrier()
        p.default_stack = None
        pstk.close()

    def l1_attention():
        pstk = contextlib.ExitStack()
        p.default_stack = pstk
        ln_src_dram = xs_hbm
        ln_stats()
        stats_rdy = w(s_dve)
        xbv2 = xb[:, :].rearrange("q (c t) -> q c t", c=NCH)
        xsv = xs_hbm.rearrange("(c q) t -> q c t", q=128)
        p.op("gps", "dma_start", out=xsv, in_=xbv2, waits=[w(s_dve), w(s_pe)],
             inc=s_dgps, inc16=True)
        spill_rdy = w(s_dgps)

        rows = sb("rows", [1, 2048], F32)
        R_VB, R_QKB, R_K2, R_RINV, R_Q2 = 0, 256, 512, 512, 1024

        def row(a, n):
            return rows[0:1, a:a + n]
        lnsrc = sb("lnsrc", [128, 2 * 512], F32)
        lspool = Pool([lnsrc[:, 0:512], lnsrc[:, 512:1024]])
        sql = sb("sql", [128, 2 * 512], F32)
        sqlpool = Pool([sql[:, 0:512], sql[:, 512:1024]])
        wldA = sb("wldA", [128, 1024], F32)
        wqkh = sb("wqkh", [128, 2 * 1024], F32R)
        wqkl = sb("wqkl", [128, 2 * 1024], F32R)
        wvh = sb("wvh", [128, 8 * 256], F32R)
        wvl = sb("wvl", [128, 8 * 256], F32R)
        lnhp = [sb(f"l1lnh_{i}", [128, 512], F32R) for i in range(2)]
        lnlp = [sb(f"l1lnl_{i}", [128, 512], F32R) for i in range(2)]
        lnpool = Pool(list(zip(lnhp, lnlp)))
        vq = sb("vq", [128, 16 * 130], F32)
        qhi = sb("qhi", [128, T], F32R)
        qlo = sb("qlo", [128, T], F32R)
        khi = sb("khi", [128, T], F32R)
        klo = sb("klo", [128, T], F32R)
        ptile = sb("ptile", [128, 4 * 512], F32)
        q2s = sb("q2s", [128, 512], F32)
        otAh = sb("otAh", [64, 512], F32R)
        otAl = sb("otAl", [64, 512], F32R)
        otBh = sb("otBh", [64, 512], F32R)
        otBl = sb("otBl", [64, 512], F32R)
        wps = sb("wps", [128, 1024], F32R)
        wpld = sb("wpld", [64, 256], F32)
        bprj = sb("bprj", [128, 8], F32)
        mcols = sb("mcols", [128, 32], F32)
        k2cols = sb("k2cols", [128, 32], F32)

        p.op("syn", "dma_start", out=bprj[:, 0:8], in_=bo_c,
             inc=s_dsyn, inc16=True)
        bprj_rdy = w(s_dsyn)

        def ln512(c, j, extra=()):
            si, (lh, ll), g = lnpool.next()
            t0 = j * 512
            ls_i, lsrc, lg = lspool.next()
            p.op("syn", "dma_start", out=lsrc,
                 in_=ln_src_dram[c * 128:(c + 1) * 128, t0:t0 + 512],
                 waits=lg + [spill_rdy], inc=s_dsyn, inc16=True)
            sq_i, scr, sg = sqlpool.next()
            p.op("dve", "tensor_sub", scr, lsrc, meanb[:, t0:t0 + 512],
                 waits=list(extra) + g + sg + [w(s_dsyn)])
            p.op("dve", "tensor_mul", scr, scr, rstdb[:, t0:t0 + 512],
                 inc=s_dve)
            lspool.used(ls_i, s_dve)
            p.op("act", "copy", lh[:, :], scr, waits=[w(s_dve)], inc=s_act)
            p.op("dve", "tensor_sub", ll[:, :], scr, lh[:, :], waits=[w(s_act)],
                 inc=s_dve)
            sqlpool.used(sq_i, s_dve)
            return si, lh, ll, w(s_dve)

        for quarter in range(4):
            vcol0 = 2 * D + quarter * 256
            for c in range(NCH):
                slot = c % 2
                p.op("syn", "dma_start", out=wldA[:, slot * 256:slot * 256 + 256],
                     in_=wqkvo[c * 128:(c + 1) * 128, vcol0:vcol0 + 256],
                     waits=[w(s_dve)], inc=s_dsyn, inc16=True)
                p.op("dve", "tensor_copy", wvh[:, c * 256:(c + 1) * 256],
                     wldA[:, slot * 256:slot * 256 + 256],
                     waits=[w(s_dsyn), w(s_pe)])
                p.op("dve", "tensor_sub", wvl[:, c * 256:(c + 1) * 256],
                     wldA[:, slot * 256:slot * 256 + 256],
                     wvh[:, c * 256:(c + 1) * 256], inc=s_dve)
            wv_rdy = w(s_dve)
            p.op("syn", "dma_start", out=row(R_VB, 256),
                 in_=bqkv[:, vcol0:vcol0 + 256], waits=[w(s_pe)],
                 inc=s_dsyn, inc16=True)
            brow_rdy = w(s_dsyn)

            for hp in (2 * quarter, 2 * quarter + 1):
                hloc = hp % 2
                qc0 = hp * 128
                wldAv = wldA[:, :].rearrange("q (c o) -> q c o", c=NCH)
                p.op("syn", "dma_start", out=wldAv,
                     in_=wqkvo[:, qc0:qc0 + 128].rearrange(
                         "(c q) o -> q c o", q=128),
                     waits=[w(s_dve)], inc=s_dsyn, inc16=True)
                split_into(wldA[:, :], wqkh[:, 0:1024], wqkl[:, 0:1024],
                           waits=[w(s_dsyn), w(s_pe)])
                p.op("syn", "dma_start", out=wldAv,
                     in_=wqkvo[:, D + qc0:D + qc0 + 128].rearrange(
                         "(c q) o -> q c o", q=128),
                     waits=[w(s_dve)], inc=s_dsyn, inc16=True)
                wqk_rdy = split_into(wldA[:, :], wqkh[:, 1024:2048],
                                     wqkl[:, 1024:2048],
                                     waits=[w(s_dsyn), w(s_pe)])
                p.op("syn", "dma_start", out=row(R_QKB, 128),
                     in_=bqkv[:, qc0:qc0 + 128],
                     waits=[w(s_pe)], inc=s_dsyn, inc16=True)
                p.op("syn", "dma_start", out=row(R_QKB + 128, 128),
                     in_=bqkv[:, D + qc0:D + qc0 + 128], inc=s_dsyn,
                     inc16=True)
                p.wait("pe", w(s_dsyn), w(s_act))
                colize(0, row(R_QKB, 128))
                bias_rdy0 = colize(1, row(R_QKB + 128, 128))
                p.op("dve", "tensor_scalar", bcol[:, 0:1], bcol[:, 0:1],
                     -2.0, None, OP.mult, inc=s_dve)
                bias_rdy = w(s_dve)
                zv = vq[:, :].rearrange(
                    "q (i h c) -> q i h c", i=16, h=2)[:, :, :, 64:65]

                for j in range(4):
                    t0 = j * 512
                    wsv = [banks.start(b) for b in range(4)]
                    wsq = banks.start(4)
                    wsk = banks.start(5)
                    for c in range(NCH):
                        extra = [stats_rdy] if (j == 0 and c == 0) else []
                        si, lh, ll, lt = ln512(c, j, extra=extra)
                        for sub in range(4):
                            st_ = lh[:, sub * 128:(sub + 1) * 128]
                            sl_ = ll[:, sub * 128:(sub + 1) * 128]
                            p.op("pe", "matmul", pb[sub][0:128, 0:256], st_,
                                 wvh[:, c * 256:(c + 1) * 256], start=(c == 0),
                                 stop=False,
                                 waits=[lt] + ([wv_rdy] if c == 0 else []) +
                                       (wsv[sub] if c == 0 else []))
                            p.op("pe", "matmul", pb[sub][0:128, 0:256], sl_,
                                 wvh[:, c * 256:(c + 1) * 256], start=False,
                                 stop=False)
                            p.op("pe", "matmul", pb[sub][0:128, 0:256], st_,
                                 wvl[:, c * 256:(c + 1) * 256], start=False,
                                 stop=False, inc=(s_pe if sub == 3 else None))
                        qh_ = wqkh[:, c * 128:(c + 1) * 128]
                        ql_ = wqkl[:, c * 128:(c + 1) * 128]
                        kh_ = wqkh[:, 1024 + c * 128:1024 + (c + 1) * 128]
                        kl_ = wqkl[:, 1024 + c * 128:1024 + (c + 1) * 128]
                        p.op("pe", "matmul", pb[4][:, :], qh_, lh[:, :],
                             start=(c == 0), stop=False,
                             waits=([wqk_rdy] if (j == 0 and c == 0) else []) +
                                   (wsq if c == 0 else []))
                        p.op("pe", "matmul", pb[4][:, :], qh_, ll[:, :],
                             start=False, stop=False)
                        p.op("pe", "matmul", pb[4][:, :], ql_, lh[:, :],
                             start=False, stop=(c == NCH - 1))
                        p.op("pe", "matmul", pb[5][:, :], kh_, lh[:, :],
                             start=(c == 0), stop=False,
                             waits=(wsk if c == 0 else []))
                        p.op("pe", "matmul", pb[5][:, :], kh_, ll[:, :],
                             start=False, stop=False)
                        p.op("pe", "matmul", pb[5][:, :], kl_, lh[:, :],
                             start=False, stop=(c == NCH - 1),
                             inc=s_pe)
                        lnpool.used(si, s_pe)
                    for sub in range(4):
                        i = 4 * j + sub
                        p.op("pe", "matmul", pb[sub][0:128, 0:256], onescol[:, :],
                             row(R_VB, 256), start=False, stop=True,
                             waits=([brow_rdy] if i == 0 else []), inc=s_pe)
                        vdone = w(s_pe)
                        for hh in range(2):
                            p.op("act", "copy",
                                 vq[:, i * 130 + hh * 65:i * 130 + hh * 65 + 64],
                                 pb[sub][0:128, hloc * 128 + hh * 64:
                                         hloc * 128 + (hh + 1) * 64],
                                 waits=([vdone] if hh == 0 else []),
                                 inc=(s_act if hh == 1 else None))
                        banks.read_done(sub, s_act)
                    for (bank, bci, dsth, dstl, scl) in (
                            (4, 0, qhi[:, t0:t0 + 512], qlo[:, t0:t0 + 512],
                             -2.0),
                            (5, 1, khi[:, t0:t0 + 512], klo[:, t0:t0 + 512],
                             1.0)):
                        qk_done = w(s_pe)
                        sq_i, scr, sg = sqlpool.next()
                        if scl == -2.0:
                            p.op("act", "activation", scr, pb[bank][:, :],
                                 AF.Identity, bias=bcol[:, bci:bci + 1],
                                 scale=cpool[:, C_NEG2:C_NEG2 + 1],
                                 waits=[qk_done, bias_rdy, w(s_dve)] + sg,
                                 inc=s_act)
                        else:
                            p.op("act", "activation", scr, pb[bank][:, :],
                                 AF.Identity, bias=bcol[:, bci:bci + 1],
                                 waits=[qk_done, bias_rdy, w(s_dve)] + sg,
                                 inc=s_act)
                        banks.read_done(bank, s_act)
                        p.op("act", "copy", dsth, scr, inc=s_act)
                        p.op("dve", "tensor_sub", dstl, scr, dsth,
                             waits=[w(s_act)], inc=s_dve)
                        sqlpool.used(sq_i, s_dve)
                    ksq_i, ksq, ksg = sqlpool.next()
                    p.op("dve", "tensor_add", ksq, khi[:, t0:t0 + 512],
                         klo[:, t0:t0 + 512], waits=[w(s_dve)] + ksg)
                    p.op("dve", "tensor_mul", ksq, ksq, ksq, inc=s_dve)
                    for hh, msk in ((0, onesAc), (1, onesBc)):
                        wb6 = banks.start(6)
                        p.op("pe", "matmul", pb[6][0:1, :], msk[:, :], ksq,
                             start=True, stop=True, waits=[w(s_dve)] + wb6,
                             inc=s_pe)
                        p.op("dve", "tensor_copy", row(R_K2, 512),
                             pb[6][0:1, :], waits=[w(s_pe)], inc=s_dve)
                        banks.read_done(6, s_dve)
                        p.op("act", "activation", row(R_K2 + 512, 512),
                             row(R_K2, 512), AF.Sqrt, waits=[w(s_dve)],
                             inc=s_act)
                        p.wait("pe", w(s_act))
                        for sub in range(4):
                            i = 4 * j + sub
                            mc = 2 * i + hh
                            wb7 = banks.start(7)
                            p.op("pe", "matmul", pb[7][:, 0:1],
                                 row(R_K2 + sub * 128, 128),
                                 cpool[0:1, C_ONE:C_ONE + 1],
                                 start=True, stop=False, waits=wb7)
                            p.op("pe", "matmul", pb[7][:, 1:2],
                                 row(R_K2 + 512 + sub * 128, 128),
                                 cpool[0:1, C_ONE:C_ONE + 1],
                                 start=False, stop=True, inc=s_pe)
                            p.op("dve", "tensor_copy",
                                 k2cols[:, mc:mc + 1], pb[7][:, 0:1],
                                 waits=[w(s_pe)])
                            p.op("dve", "tensor_copy",
                                 mcols[:, mc:mc + 1], pb[7][:, 1:2],
                                 inc=s_dve)
                            banks.read_done(7, s_dve)
                    sqlpool.used(ksq_i, s_pe)
                    for sub in range(4):
                        i = 4 * j + sub
                        for hh in range(2):
                            mc = mcols[:, 2 * i + hh:2 * i + hh + 1]
                            vsl = vq[:, i * 130 + hh * 65:i * 130 + hh * 65 + 64]
                            p.op("dve", "tensor_scalar", vsl, vsl, mc, None,
                                 OP.mult, waits=[w(s_act), w(s_dve)])
                            p.op("dve", "tensor_copy",
                                 vq[:, i * 130 + hh * 65 + 64:
                                    i * 130 + hh * 65 + 65], mc, inc=s_dve)
                vq_rdy = w(s_dve)

                for j in range(4):
                    t0 = j * 512
                    qsq_i, qsq, qsg = sqlpool.next()
                    p.op("dve", "tensor_add", qsq, qhi[:, t0:t0 + 512],
                         qlo[:, t0:t0 + 512], waits=[vq_rdy] + qsg)
                    p.op("dve", "tensor_mul", qsq, qsq, qsq, inc=s_dve)
                    for half in range(2):
                        hrow0 = 64 * half
                        obank = 2 + half
                        msk = onesAc if half == 0 else onesBc
                        wb6 = banks.start(6)
                        p.op("pe", "matmul", pb[6][0:1, :], msk[:, :], qsq,
                             start=True, stop=True, waits=[w(s_dve)] + wb6,
                             inc=s_pe)
                        p.op("dve", "tensor_scalar", row(R_Q2, 512),
                             pb[6][0:1, :], 0.25, None, OP.mult,
                             waits=[w(s_pe)], inc=s_dve)
                        banks.read_done(6, s_dve)
                        wb6 = banks.start(6)
                        p.op("pe", "matmul", pb[6][:, :], onescol[:, :],
                             row(R_Q2, 512), start=True, stop=True,
                             waits=[w(s_dve)] + wb6, inc=s_pe)
                        p.op("dve", "tensor_copy", q2s[:, :], pb[6][:, :],
                             waits=[w(s_pe)], inc=s_dve)
                        banks.read_done(6, s_dve)
                        q2_rdy = w(s_dve)
                        wso = banks.start(obank)
                        BT = 4
                        add_done = [None] * 16
                        exp_done = [None] * 16
                        pv_done_i = [None] * 16

                        def pts(i):
                            return ptile[:, (i % BT) * 512:(i % BT) * 512 + 512]

                        def s_mms(i):
                            k0 = i * 128
                            sbk = i % 2
                            wss = banks.start(sbk)
                            p.op("pe", "matmul", pb[sbk][:, :],
                                 khi[hrow0:hrow0 + 64, k0:k0 + 128],
                                 qhi[hrow0:hrow0 + 64, t0:t0 + 512],
                                 start=True, stop=False, waits=wss)
                            p.op("pe", "matmul", pb[sbk][:, :],
                                 khi[hrow0:hrow0 + 64, k0:k0 + 128],
                                 qlo[hrow0:hrow0 + 64, t0:t0 + 512],
                                 start=False, stop=False)
                            p.op("pe", "matmul", pb[sbk][:, :],
                                 klo[hrow0:hrow0 + 64, k0:k0 + 128],
                                 qhi[hrow0:hrow0 + 64, t0:t0 + 512],
                                 start=False, stop=True, inc=s_pe)
                            return w(s_pe)

                        def s_add(i, s_done):
                            sbk = i % 2
                            p.op("dve", "tensor_add", pts(i), pb[sbk][:, :],
                                 q2s[:, :],
                                 waits=[s_done] +
                                       ([q2_rdy] if i == 0 else []) +
                                       ([pv_done_i[i - BT]] if i >= BT else []),
                                 inc=s_dve)
                            banks.read_done(sbk, s_dve)
                            add_done[i] = w(s_dve)

                        def chain_batch(b0):
                            for i in range(b0, b0 + BT):
                                k2c_i = 2 * i + half
                                p.op("act", "activation", pts(i), pts(i),
                                     AF.Relu, bias=k2cols[:, k2c_i:k2c_i + 1],
                                     waits=[add_done[i]])
                            for i in range(b0, b0 + BT):
                                p.op("act", "activation", pts(i), pts(i),
                                     AF.Sqrt)
                            for i in range(b0, b0 + BT):
                                p.op("act", "activation", pts(i), pts(i),
                                     AF.Ln, bias=cpool[:, C_EPS2:C_EPS2 + 1])
                            for i in range(b0, b0 + BT):
                                p.op("act", "activation", pts(i), pts(i),
                                     AF.Exp, scale=cpool[:, C_NEG2:C_NEG2 + 1],
                                     inc=s_act)
                                exp_done[i] = w(s_act)

                        def pv_tile(i):
                            vstat = vq[:, i * 130 + half * 65:
                                       i * 130 + half * 65 + 65]
                            p.op("pe", "matmul", pb[obank][0:65, :], vstat,
                                 pts(i), start=(i == 0), stop=(i == 15),
                                 waits=[exp_done[i]] + (wso if i == 0 else []),
                                 inc=s_pe)
                            pv_done_i[i] = w(s_pe)

                        for i in range(BT):
                            s_add(i, s_mms(i))
                        chain_batch(0)
                        for b in range(BT, 16, BT):
                            for k in range(BT):
                                sd = s_mms(b + k)
                                pv_tile(b - BT + k)
                                s_add(b + k, sd)
                            chain_batch(b)
                        for i in range(16 - BT, 16):
                            pv_tile(i)
                    pv_done = w(s_pe)
                    otA = ptile[0:64, 0:512]
                    otB = ptile[0:64, 512:1024]
                    for half in range(2):
                        obank = 2 + half
                        ot = otA if half == 0 else otB
                        p.op("dve", "reciprocal", row(R_RINV, 512),
                             pb[obank][64:65, :], waits=[pv_done], inc=s_dve)
                        wb6 = banks.start(6)
                        p.op("pe", "matmul", pb[6][0:64, :], onescol[0:1, 0:64],
                             row(R_RINV, 512), start=True, stop=True,
                             waits=[w(s_dve)] + wb6, inc=s_pe)
                        si_r, scr_r, sg_r = sqlpool.next()
                        p.op("dve", "tensor_copy", scr_r[0:64, :],
                             pb[6][0:64, :], waits=[w(s_pe)] + sg_r, inc=s_dve)
                        banks.read_done(6, s_dve)
                        p.op("dve", "tensor_mul", ot, pb[obank][0:64, :],
                             scr_r[0:64, :], inc=s_dve)
                        sqlpool.used(si_r, s_dve)
                        banks.read_done(obank, s_dve)
                    o_rdy = w(s_dve)
                    if DEBUG_O:
                        p.op("gps", "dma_start",
                             out=dbg_o[hp * 128:hp * 128 + 64, t0:t0 + 512],
                             in_=otA, waits=[o_rdy], inc=s_dgps, inc16=True)
                        p.op("gps", "dma_start",
                             out=dbg_o[hp * 128 + 64:hp * 128 + 128,
                                       t0:t0 + 512],
                             in_=otB, inc=s_dgps, inc16=True)
                    p.op("act", "copy", otAh[:, :], otA, waits=[o_rdy, w(s_pe)])
                    p.op("act", "copy", otBh[:, :], otB, inc=s_act)
                    p.op("dve", "tensor_sub", otAl[:, :], otA, otAh[:, :],
                         waits=[w(s_act)])
                    p.op("dve", "tensor_sub", otBl[:, :], otB, otBh[:, :],
                         inc=s_dve)
                    osp = w(s_dve)
                    for oc in range(NCH):
                        obk = 4 + (oc % 2)
                        slot = oc % 2
                        sA = wps[0:64, slot * 512:slot * 512 + 512]
                        p.op("syn", "dma_start",
                             out=wpld[:, :].rearrange("q (h o) -> q h o", h=2),
                             in_=wqkvo[hp * 128:hp * 128 + 128,
                                       3 * D + oc * 128:3 * D + (oc + 1) * 128
                                       ].rearrange("(h q) o -> q h o", h=2),
                             waits=[w(s_dve)], inc=s_dsyn, inc16=True)
                        p.op("dve", "tensor_copy", sA[:, 0:256],
                             wpld[:, :], waits=[w(s_dsyn), w(s_pe)])
                        p.op("dve", "tensor_sub", sA[:, 256:512],
                             wpld[:, :], sA[:, 0:256],
                             inc=s_dve)
                        wp_rdy = w(s_dve)
                        wpAh_ = sA[:, 0:128]
                        wpBh_ = sA[:, 128:256]
                        wpAl_ = sA[:, 256:384]
                        wpBl_ = sA[:, 384:512]
                        wso2 = banks.start(obk)
                        ws0 = [osp] if oc == 0 else []
                        p.op("pe", "matmul", pb[obk][:, :], wpAh_, otAh[:, :],
                             start=True, stop=False, waits=ws0 + wso2 + [wp_rdy])
                        p.op("pe", "matmul", pb[obk][:, :], wpAh_, otAl[:, :],
                             start=False, stop=False)
                        p.op("pe", "matmul", pb[obk][:, :], wpAl_, otAh[:, :],
                             start=False, stop=False)
                        p.op("pe", "matmul", pb[obk][:, :], wpBh_, otBh[:, :],
                             start=False, stop=False)
                        p.op("pe", "matmul", pb[obk][:, :], wpBh_, otBl[:, :],
                             start=False, stop=False)
                        p.op("pe", "matmul", pb[obk][:, :], wpBl_, otBh[:, :],
                             start=False, stop=True, inc=s_pe)
                        prj_done = w(s_pe)
                        if hp == 0:
                            p.op("dve", "scalar_tensor_tensor",
                                 xch(oc, t0, 512), pb[obk][:, :],
                                 bprj[:, oc:oc + 1], xch(oc, t0, 512),
                                 OP.add, OP.add,
                                 waits=[prj_done] + ([bprj_rdy] if (j == 0 and oc == 0) else []),
                                 inc=s_dve)
                        else:
                            p.op("dve", "tensor_add", xch(oc, t0, 512),
                                 xch(oc, t0, 512), pb[obk][:, :],
                                 waits=[prj_done], inc=s_dve)
                        banks.read_done(obk, s_dve)

        phase_barrier()
        p.default_stack = None
        pstk.close()

    def l1_ffn():
        pstk = contextlib.ExitStack()
        p.default_stack = pstk
        sident = sb("sident", [128, 128], F32)
        p.op("syn", "dma_start", out=sident[:, :], in_=ident, inc=s_dsyn,
             inc16=True)
        ident_rdy = w(s_dsyn)
        colb = sb("colb", [128, 4], F32)
        rowsf = sb("rowsf", [1, 512], F32)

        def part_max_bcast(src_t, width):
            nch_ = (width + 127) // 128
            for ch in range(nch_):
                w_ = min(128, width - ch * 128)
                wb7 = banks.start(7)
                p.op("pe", "transpose", pb[7][0:w_, 0:128],
                     src_t[:, ch * 128:ch * 128 + w_], sident[:, :],
                     waits=[w(s_dve), ident_rdy] + wb7, inc=s_pe)
                p.op("dve", "reduce_max", colb[0:w_, ch:ch + 1],
                     pb[7][0:w_, 0:128], AX, waits=[w(s_pe)], inc=s_dve)
                banks.read_done(7, s_dve)
            off = 0
            for ch in range(nch_):
                w_ = min(128, width - ch * 128)
                wb7 = banks.start(7)
                p.op("pe", "transpose", pb[7][0:1, 0:128],
                     colb[:, ch:ch + 1], sident[:, :],
                     waits=[w(s_dve)] + wb7, inc=s_pe)
                p.op("dve", "tensor_copy", rowsf[0:1, off:off + w_],
                     pb[7][0:1, 0:w_], waits=[w(s_pe)], inc=s_dve)
                banks.read_done(7, s_dve)
                off += w_
            wb6 = banks.start(6)
            p.op("pe", "matmul", pb[6][:, 0:width], onescol[:, :],
                 rowsf[0:1, 0:width], start=True, stop=True,
                 waits=[w(s_dve)] + wb6, inc=s_pe)
            return pb[6], w(s_pe)
        FT = 256
        FW = FT + 2
        NT = T // FT
        ln_stats()
        stats_rdy = w(s_dve)
        xbv3 = xb[:, :].rearrange("q (c t) -> q c t", c=NCH)
        xsv3 = xs_hbm.rearrange("(c q) t -> q c t", q=128)
        p.op("gps", "dma_start", out=xsv3, in_=xbv3, waits=[w(s_dve), w(s_pe)],
             inc=s_dgps, inc16=True)
        spill_rdy = w(s_dgps)

        qstk = contextlib.ExitStack()
        p.default_stack = qstk
        wqld = sb("wqld", [128, 2 * 512], F32)
        wqpool = Pool([wqld[:, 0:512], wqld[:, 512:1024]])
        qscr = sb("qscr", [128, 2 * 512], F32)
        tqb = sb("tqb", [128, 2 * 512], BF16)
        tqpool = Pool([tqb[:, 0:512], tqb[:, 512:1024]])
        acc = sb("acc1", [128, 2], F32)
        p.default_stack = pstk

        def quantize_weights(src_dram, dst_dram, nrow, ncol, tau_col_i):
            ntr, ntc = nrow // 128, ncol // 512
            p.op("dve", "memset", acc[:, 0:1], 0.0)
            for r_ in range(ntr):
                for c_ in range(ntc):
                    si, slot, sg = wqpool.next()
                    p.op("syn", "dma_start", out=slot,
                         in_=src_dram[r_ * 128:(r_ + 1) * 128,
                                      c_ * 512:(c_ + 1) * 512],
                         waits=sg, inc=s_dsyn, inc16=True)
                    p.op("dve", "tensor_reduce", acc[:, 1:2], slot, AX, OP.add,
                         apply_absolute_value=True, waits=[w(s_dsyn)])
                    p.op("dve", "tensor_add", acc[:, 0:1], acc[:, 0:1],
                         acc[:, 1:2], inc=s_dve)
                    wqpool.used(si, s_dve)
            wb7 = banks.start(7)
            p.op("pe", "matmul", pb[7][0:1, 0:1], cpool[:, C_ONE:C_ONE + 1],
                 acc[:, 0:1],
                 start=True, stop=True, waits=[w(s_dve)] + wb7, inc=s_pe)
            p.op("dve", "tensor_scalar", cell(0), pb[7][0:1, 0:1],
                 1.0 / (nrow * ncol), 1e-5, OP.mult, OP.max,
                 waits=[w(s_pe)], inc=s_dve)
            banks.read_done(7, s_dve)
            p.op("dve", "tensor_scalar", cell(1), cell(0), 0.5, None, OP.mult)
            p.op("dve", "tensor_scalar", cell(2), cell(1), -1.0, None, OP.mult,
                 inc=s_dve)
            p.wait("pe", w(s_dve))
            colize(tau_col_i, cell(1))
            colize(tau_col_i + 1, cell(2))
            mrdy = colize(tau_col_i + 2, cell(0))
            for r_ in range(ntr):
                for c_ in range(ntc):
                    si, slot, sg = wqpool.next()
                    p.op("syn", "dma_start", out=slot,
                         in_=src_dram[r_ * 128:(r_ + 1) * 128,
                                      c_ * 512:(c_ + 1) * 512],
                         waits=sg, inc=s_dsyn, inc16=True)
                    p.op("dve", "tensor_scalar", qscr[:, 0:512], slot,
                         bcol[:, tau_col_i:tau_col_i + 1], None, OP.is_ge,
                         waits=[w(s_dsyn), mrdy])
                    p.op("dve", "tensor_scalar", qscr[:, 512:1024], slot,
                         bcol[:, tau_col_i + 1:tau_col_i + 2], None,
                         OP.is_le, inc=s_dve)
                    wqpool.used(si, s_dve)
                    ti, tslot, tg = tqpool.next()
                    p.op("dve", "tensor_sub", tslot, qscr[:, 0:512],
                         qscr[:, 512:1024], waits=tg, inc=s_dve)
                    p.op("gps", "dma_start",
                         out=dst_dram[r_ * 128:(r_ + 1) * 128,
                                      c_ * 512:(c_ + 1) * 512],
                         in_=tslot, waits=[w(s_dve)], inc=s_dgps, inc16=True)
                    tqpool.used(ti, s_dgps)

        quantize_weights(f1, t1_hbm, D, HID, 4)
        m1rdy = w(s_dve)
        quantize_weights(f2, t2_hbm, HID, D, 0)
        tq_done = w(s_dgps)
        phase_barrier()
        qstk.close()

        lnb = sb("lnb", [128, NCH * FW], F32)
        qx = sb("qx", [128, NCH * FW], BF16)
        y1h = sb("y1h", [128, 32 * FW], F32)
        h2b = sb("h2b", [128, 32 * FT], F32)
        qh = sb("qh", [128, 32 * FT], BF16)
        wtb = sb("wtb", [128, 2 * 2048], BF16)
        wtpool = Pool([wtb[:, 0:2048], wtb[:, 2048:4096]])
        sclb = sb("sclb", [128, 4 * FW], F32)
        s_b = sclb[:, 0:FW]
        osc_b = sclb[:, FW:2 * FW]
        s2_b = sclb[:, 2 * FW:2 * FW + FT]
        osc2_b = sclb[:, 3 * FW:3 * FW + FT]
        amx = sb("amx", [128, FW], F32)
        sconvw = sb("sconvw", [128, 96], F32)
        sconvb = sb("sconvb", [128, 32], F32)
        p.op("syn", "dma_start", out=sconvw[:, :], in_=convw, inc=s_dsyn,
             inc16=True)
        p.op("syn", "dma_start", out=sconvb[:, :], in_=convb, inc=s_dsyn,
             inc16=True)
        conv_rdy = w(s_dsyn)
        lnsrc = sb("lnsrc", [128, 2 * 512], F32)
        lspool = Pool([lnsrc[:, 0:512], lnsrc[:, 512:1024]])

        def cast_round_clip(dst_bf16, srcf, sb_scale, width, eng="dve"):
            i32 = cr_i32[:, 0:width]
            pr = cr_p[:, 0:width]
            rr = cr_r[:, 0:width]
            p.op(eng, "tensor_mul", pr, srcf, sb_scale)
            p.op(eng, "tensor_copy", i32, pr)
            p.op(eng, "tensor_copy", rr, i32)
            p.op(eng, "tensor_sub", pr, pr, rr)
            p.op(eng, "scalar_tensor_tensor", rr, pr, 0.5, rr, OP.is_ge,
                 OP.add)
            p.op(eng, "tensor_scalar", pr, pr, -0.5, -1.0, OP.is_le, OP.mult)
            p.op(eng, "tensor_add", rr, rr, pr)
            p.op(eng, "tensor_scalar", rr, rr, -128.0, 127.0, OP.max, OP.min)
            p.op(eng, "tensor_copy", dst_bf16, rr, inc=s_dve)

        cr_i32 = sb("cr_i32", [128, FW], I32)
        cr_p = sb("cr_p", [128, FW], F32)
        cr_r = sb("cr_r", [128, FW], F32)

        for t in range(NT):
            t0 = t * FT
            lo_pad = 1 if t == 0 else 0
            hi_pad = 1 if t == NT - 1 else 0
            src0 = t0 - 1 + lo_pad
            wsrc = FW - lo_pad - hi_pad
            for c in range(NCH):
                ls_i, lsrc, lg = lspool.next()
                p.op("syn", "dma_start", out=lsrc[:, 0:wsrc],
                     in_=xs_hbm[c * 128:(c + 1) * 128, src0:src0 + wsrc],
                     waits=lg + [spill_rdy], inc=s_dsyn, inc16=True)
                dst = lnb[:, c * FW:(c + 1) * FW]
                if lo_pad:
                    p.op("dve", "memset", dst[:, 0:1], 0.0)
                if hi_pad:
                    p.op("dve", "memset", dst[:, FW - 1:FW], 0.0)
                p.op("dve", "tensor_sub", dst[:, lo_pad:lo_pad + wsrc],
                     lsrc[:, 0:wsrc],
                     meanb[:, src0:src0 + wsrc],
                     waits=[w(s_dsyn)] +
                           ([stats_rdy] if (t == 0 and c == 0) else []) +
                           ([w(s_pe)] if c == 0 else []))
                p.op("dve", "tensor_mul", dst[:, lo_pad:lo_pad + wsrc],
                     dst[:, lo_pad:lo_pad + wsrc],
                     rstdb[:, src0:src0 + wsrc], inc=s_dve)
                lspool.used(ls_i, s_dve)
            if lo_pad or hi_pad:
                pass
            p.op("act", "activation", amx[:, :], lnb[:, 0:FW], AF.Abs,
                 waits=[w(s_dve)], inc=s_act)
            for c in range(1, NCH):
                scr_ab = cr_p[:, 0:FW] if c % 2 else cr_r[:, 0:FW]
                p.op("act", "activation", scr_ab,
                     lnb[:, c * FW:(c + 1) * FW], AF.Abs, inc=s_act)
                p.op("dve", "tensor_max", amx[:, :], amx[:, :], scr_ab,
                     waits=[w(s_act)], inc=s_dve)
            pmx, pmx_rdy = part_max_bcast(amx, FW)
            p.op("dve", "tensor_single_scalar", s_b, pmx[:, 0:FW], 1e-5,
                 OP.max, waits=[pmx_rdy])
            p.op("dve", "tensor_scalar", osc_b, s_b,
                 bcol[:, 6:7], None, OP.mult, waits=[m1rdy])
            p.op("dve", "tensor_scalar", osc_b, osc_b, 1.0 / 127.0, None,
                 OP.mult)
            p.op("dve", "reciprocal", s_b, s_b)
            p.op("dve", "tensor_scalar", s_b, s_b, 127.0, None, OP.mult,
                 inc=s_dve)
            banks.read_done(6, s_dve)
            for c in range(NCH):
                cast_round_clip(qx[:, c * FW:(c + 1) * FW],
                                lnb[:, c * FW:(c + 1) * FW], s_b, FW)
            qx_rdy = w(s_dve)

            for h in range(32):
                si, slot, sg = wtpool.next()
                p.wait("syn", *sg)
                p.op("syn", "dma_start",
                     out=slot[:, 0:1024].rearrange("q (c o) -> q c o", c=NCH),
                     in_=t1_hbm[:, h * 128:(h + 1) * 128].rearrange(
                         "(c q) o -> q c o", q=128),
                     waits=([tq_done] if (t == 0 and h == 0) else []),
                     inc=s_dsyn, inc16=True)
                srdy = w(s_dsyn)
                bk_ = h % 2
                wsb_ = banks.start(bk_)
                for c in range(NCH):
                    p.op("pe", "matmul", pb[bk_][:, 0:FW],
                         slot[:, c * 128:(c + 1) * 128],
                         qx[:, c * FW:(c + 1) * FW],
                         start=(c == 0), stop=(c == NCH - 1),
                         waits=([srdy, qx_rdy] if c == 0 else []) +
                               (wsb_ if c == 0 else []),
                         inc=(s_pe if c == NCH - 1 else None))
                wtpool.used(si, s_pe)
                mmd = w(s_pe)
                p.op("dve", "tensor_mul", y1h[:, h * FW:(h + 1) * FW],
                     pb[bk_][:, 0:FW], osc_b, waits=[mmd], inc=s_dve)
                banks.read_done(bk_, s_dve)
            for h in range(32):
                yh = y1h[:, h * FW:(h + 1) * FW]
                sq_i, scr, sg = sqpool.next()
                p.op("dve", "tensor_scalar", scr, yh[:, 0:FT],
                     sconvw[:, h * 3:h * 3 + 1], None, OP.mult,
                     waits=sg + ([conv_rdy] if (t == 0 and h == 0) else []))
                p.op("dve", "scalar_tensor_tensor", scr, yh[:, 1:1 + FT],
                     sconvw[:, h * 3 + 1:h * 3 + 2], scr, OP.mult, OP.add)
                p.op("dve", "scalar_tensor_tensor", scr, yh[:, 2:2 + FT],
                     sconvw[:, h * 3 + 2:h * 3 + 3], scr, OP.mult, OP.add,
                     inc=s_dve)
                p.op("act", "activation", h2b[:, h * FT:(h + 1) * FT], scr,
                     AF.Silu, bias=sconvb[:, h:h + 1], waits=[w(s_dve)],
                     inc=s_act)
                sqpool.used(sq_i, s_act)
            h2_rdy = w(s_act)
            p.op("act", "activation", amx[:, 0:FT], h2b[:, 0:FT], AF.Abs,
                 waits=[h2_rdy], inc=s_act)
            for h in range(1, 32):
                scr_ab = cr_p[:, 0:FT] if h % 2 else cr_r[:, 0:FT]
                p.op("act", "activation", scr_ab,
                     h2b[:, h * FT:(h + 1) * FT], AF.Abs, inc=s_act)
                p.op("dve", "tensor_max", amx[:, 0:FT], amx[:, 0:FT], scr_ab,
                     waits=[w(s_act)], inc=s_dve)
            pmx2, pmx2_rdy = part_max_bcast(amx[:, 0:FT], FT)
            p.op("dve", "tensor_single_scalar", s2_b, pmx2[:, 0:FT], 1e-5,
                 OP.max, waits=[pmx2_rdy])
            p.op("dve", "tensor_scalar", osc2_b, s2_b, bcol[:, 2:3], None,
                 OP.mult)
            p.op("dve", "tensor_scalar", osc2_b, osc2_b, 1.0 / 127.0, None,
                 OP.mult)
            p.op("dve", "reciprocal", s2_b, s2_b)
            p.op("dve", "tensor_scalar", s2_b, s2_b, 127.0, None, OP.mult,
                 inc=s_dve)
            banks.read_done(6, s_dve)
            for h in range(32):
                cast_round_clip(qh[:, h * FT:(h + 1) * FT],
                                h2b[:, h * FT:(h + 1) * FT], s2_b, FT)
            qh_rdy = w(s_dve)
            if DEBUG_O and t == 7:
                p.op("gps", "dma_start", out=dbg_c[:, 0:32 * FT],
                     in_=qh[:, :], waits=[qh_rdy], inc=s_dgps, inc16=True)
            for o in range(NCH):
                bk_ = 2 + o % 2
                wsb_ = banks.start(bk_)
                first = [qh_rdy] if o == 0 else []
                for half in range(2):
                    si, slot, sg = wtpool.next()
                    p.wait("syn", *sg)
                    p.op("syn", "dma_start",
                         out=slot[:, 0:2048].rearrange(
                             "q (c o) -> q c o", c=16),
                         in_=t2_hbm[half * 2048:(half + 1) * 2048,
                                    o * 128:(o + 1) * 128].rearrange(
                             "(c q) o -> q c o", q=128),
                         inc=s_dsyn, inc16=True)
                    srdy = w(s_dsyn)
                    if DEBUG_O and t == 7 and o == 0:
                        p.op("gps", "dma_start",
                             out=dbg_c[:, 8192 + half * 2048:
                                       8192 + half * 2048 + 2048],
                             in_=slot[:, 0:2048], waits=[srdy],
                             inc=s_dgps, inc16=True)
                        p.wait("pe", w(s_dgps))
                    for c in range(16):
                        cc = half * 16 + c
                        p.op("pe", "matmul", pb[bk_][:, 0:FT],
                             slot[:, c * 128:(c + 1) * 128],
                             qh[:, cc * FT:(cc + 1) * FT],
                             start=(cc == 0), stop=(cc == 31),
                             waits=([srdy] if c == 0 else []) + first +
                                   (wsb_ if cc == 0 else []),
                             inc=(s_pe if c == 15 else None))
                        first = []
                    wtpool.used(si, s_pe)
                mmd = w(s_pe)
                sq_i, scr, sg = sqpool.next()
                p.op("dve", "tensor_mul", scr, pb[bk_][:, 0:FT], osc2_b,
                     waits=[mmd] + sg, inc=s_dve)
                banks.read_done(bk_, s_dve)
                if DEBUG_O:
                    p.op("gps", "dma_start",
                         out=dbg_o[o * 128:(o + 1) * 128, t0:t0 + FT],
                         in_=scr, waits=[w(s_dve)], inc=s_dgps, inc16=True)
                p.op("dve", "tensor_add", xch(o, t0, FT), xch(o, t0, FT),
                     scr, inc=s_dve)
                sqpool.used(sq_i, s_dve)

        phase_barrier()
        p.default_stack = None
        pstk.close()

    if "l0a" in phases:
        l0_attention()
    if "l0f" in phases:
        l0_ffn()
    if "l1a" in phases:
        l1_attention()
    if "l1f" in phases:
        l1_ffn()

    yv = y_out.rearrange("(c q) t -> q c t", q=128)
    p.wait("gps", w(s_dve), w(s_pe), w(s_act))
    p.op("gps", "dma_start", out=yv, in_=xbv, inc=s_dgps, inc16=True)
    p.wait("gps", w(s_dgps))

    with contextlib.ExitStack() as stack:
        p.emit(stack)
    return nc


def _host_prep(inputs):
    f32 = np.float32
    x = np.ascontiguousarray(inputs["L0_attn_x"])
    com = {}
    com["w_attn"] = np.ascontiguousarray(inputs["L0_attn_W_attn"])
    com["b_attn"] = np.ascontiguousarray(inputs["L0_attn_b_attn"].reshape(1, -1))
    com["w_proj"] = np.ascontiguousarray(inputs["L0_attn_W_proj"])
    com["b_proj"] = np.ascontiguousarray(inputs["L0_attn_b_proj"].reshape(8, 128).T)
    com["w1t"] = np.ascontiguousarray(inputs["L0_ffn_w1_w"].T)
    com["w2t"] = np.ascontiguousarray(inputs["L0_ffn_w2_w"].T)
    com["h1t"] = np.ascontiguousarray(inputs["L0_ffn_h1_w"].T)
    com["h2t"] = np.ascontiguousarray(inputs["L0_ffn_h2_w"].T)
    com["w3t"] = np.ascontiguousarray(inputs["L0_ffn_w3_w"].T)
    com["s1c"] = np.ascontiguousarray(inputs["L0_ffn_w1_s"].reshape(32, 128).T)
    com["s2c"] = np.ascontiguousarray(inputs["L0_ffn_w2_s"].reshape(32, 128).T)
    com["h1sc"] = np.ascontiguousarray(inputs["L0_ffn_h1_s"].reshape(32, 128).T)
    com["h2sc"] = np.ascontiguousarray(inputs["L0_ffn_h2_s"].reshape(32, 128).T)
    com["w3sc"] = np.ascontiguousarray(inputs["L0_ffn_w3_s"].reshape(8, 128).T)
    com["wqkvo"] = np.ascontiguousarray(np.concatenate(
        [inputs["L1_attn_Wq"].T, inputs["L1_attn_Wk"].T,
         inputs["L1_attn_Wv"].T, inputs["L1_attn_Wo"].T], axis=1))
    com["bqkv"] = np.ascontiguousarray(np.concatenate(
        [inputs["L1_attn_bq"], inputs["L1_attn_bk"],
         inputs["L1_attn_bv"]]).reshape(1, -1))
    com["bo_c"] = np.ascontiguousarray(inputs["L1_attn_bo"].reshape(8, 128).T)
    com["f1"] = np.ascontiguousarray(inputs["L1_ffn_w1"].T)
    com["f2"] = np.ascontiguousarray(inputs["L1_ffn_w2"].T)
    com["convw"] = np.ascontiguousarray(
        inputs["L1_ffn_conv_w"][:, 0, :].reshape(32, 128, 3).transpose(1, 0, 2)
        .reshape(128, 96))
    com["convb"] = np.ascontiguousarray(inputs["L1_ffn_conv_b"].reshape(32, 128).T)
    com["ident"] = np.eye(128, dtype=f32)
    permm = np.zeros((128, 128), f32)
    for i in range(0, 128, 2):
        permm[i, i + 1] = 1.0
        permm[i + 1, i] = 1.0
    com["perm"] = permm
    with jax.default_device(jax.devices("cpu")[0]):
        _th = 1.0 / (10000.0 ** (jnp.arange(0, 64, 2, dtype=jnp.float32) / 64))
        _it = jnp.arange(16, dtype=jnp.float32)[:, None] * _th[None, :]
        cos16 = np.asarray(jnp.cos(_it))
        sin16 = np.asarray(jnp.sin(_it))
    pidx = np.arange(128)
    freq = (pidx % 64) // 2
    sgn = np.where(pidx % 2 == 0, -1.0, 1.0).astype(f32)
    rt = np.zeros((128, 32), f32)
    rt[:, 0:16] = cos16.T[freq]
    rt[:, 16:32] = sin16.T[freq] * sgn[:, None]
    com["ropet"] = rt

    in_maps = []
    for c in range(8):
        m = dict(com)
        m["x0"] = np.ascontiguousarray(x[c].T)
        in_maps.append(m)
    return in_maps


def kernel(**inputs):
    in_maps = _host_prep(inputs)
    nc = build_nc()
    res = run_bass_kernel_spmd(nc, in_maps, core_ids=list(range(8)))
    out = np.empty((8, T, D), np.float32)
    for c in range(8):
        out[c] = res.results[c]["y"].T
    return out
